# version 07
# baseline (speedup 1.0000x reference)
import numpy as np
import ml_dtypes
from contextlib import ExitStack

import concourse.bass as bass
import concourse.mybir as mybir
from concourse.bass_utils import run_bass_kernel_spmd

F32 = mybir.dt.float32
BF16 = mybir.dt.bfloat16
AF = mybir.ActivationFunctionType
ALU = mybir.AluOpType
AX = mybir.AxisListType

D = 2048
S = 4096
NQ = 2048
NH = 16
HD = 128
G = 2
CC = D // 128
KC = S // 128
QTW = 512
NQT = NQ // QTW
STRIP = 512
NSTRIP = S // STRIP
NE = 8
SCALE = float(HD) ** -0.5
EPS = 1e-5
NTT = NQ // 128

ENGINES = ("tensor", "vector", "scalar", "sync", "gpsimd")


class Sem:
    def __init__(self, ctx, nc, name):
        self.h = ctx.enter_context(nc.semaphore(name))
        self.n = 0
        self.name = name


class Sched:
    def __init__(self):
        self.q = {e: [] for e in ENGINES}

    def op(self, eng, fn, waits=(), incs=()):
        ev = None
        inc_list = []
        for sem, amt in incs:
            sem.n += amt
            inc_list.append((sem, amt))
            if ev is None:
                ev = (sem, sem.n)
        self.q[eng].append((tuple(w for w in waits if w is not None), fn, tuple(inc_list)))
        return ev

    def emit(self, blk):
        for eng in ENGINES:
            items = self.q[eng]
            if not items:
                continue

            def body(engine, items=items):
                for waits, fn, incs in items:
                    for sem, val in waits:
                        engine.wait_ge(sem.h, val)
                    inst = fn(engine)
                    for sem, amt in incs:
                        inst.then_inc(sem.h, amt)

            getattr(blk, eng)(body)


def build_program(npass=8, do_bc=True, dbg_attn=False, dbg_x2=False, dbg_proj=False,
                  dbg_att1=False):
    nc = bass.Bass("TRN2", target_bir_lowering=False)

    xT = nc.dram_tensor("xT", [D, S], F32, kind="ExternalInput").ap()
    xq = nc.dram_tensor("xq", [NQ, D], F32, kind="ExternalInput").ap()
    wqT = nc.dram_tensor("wqT", [D, D], F32, kind="ExternalInput").ap()
    wkT = nc.dram_tensor("wkT", [D, D], F32, kind="ExternalInput").ap()
    wvT = nc.dram_tensor("wvT", [D, D], F32, kind="ExternalInput").ap()
    woT = nc.dram_tensor("woT", [D, D], F32, kind="ExternalInput").ap()
    gwt = nc.dram_tensor("gwt", [128, CC * NE], F32, kind="ExternalInput").ap()
    gb = nc.dram_tensor("gb", [1, NE], F32, kind="ExternalInput").ap()
    ew = nc.dram_tensor("ew", [NE, D, D], F32, kind="ExternalInput").ap()
    eb = nc.dram_tensor("eb", [1, NE * D], F32, kind="ExternalInput").ap()
    ones_bf = nc.dram_tensor("ones_bf", [1, 128], BF16, kind="ExternalInput").ap()
    onescol_bf = nc.dram_tensor("onescol_bf", [128, 1], BF16, kind="ExternalInput").ap()
    onesmat_bf = nc.dram_tensor("onesmat_bf", [128, 128], BF16, kind="ExternalInput").ap()
    ones_f = nc.dram_tensor("ones_f", [1, 128], F32, kind="ExternalInput").ap()
    epscol = nc.dram_tensor("epscol", [128, 1], F32, kind="ExternalInput").ap()

    y = nc.dram_tensor("y", [NQ, D], F32, kind="ExternalOutput").ap()
    dbgA = nc.dram_tensor("dbgA", [128, NH * NQ], F32, kind="ExternalOutput").ap() if dbg_attn else None
    dbgX = nc.dram_tensor("dbgX", [NQ, D], F32, kind="ExternalOutput").ap() if dbg_x2 else None
    if dbg_proj:
        dbgK = nc.dram_tensor("dbgK", [128, G * S], F32, kind="ExternalOutput").ap()
        dbgQ = nc.dram_tensor("dbgQ", [128, G * NQ], F32, kind="ExternalOutput").ap()
        dbgV = nc.dram_tensor("dbgV", [128, KC * G * HD], F32, kind="ExternalOutput").ap()
        dbgN = nc.dram_tensor("dbgN", [128, 16], F32, kind="ExternalOutput").ap()

    if dbg_att1:
        dbgD = nc.dram_tensor("dbgD", [1, 512], F32, kind="ExternalOutput").ap()
        dbgR = nc.dram_tensor("dbgR", [128, 512], F32, kind="ExternalOutput").ap()
        dbgE = nc.dram_tensor("dbgE", [128, 512], F32, kind="ExternalOutput").ap()
        dbgE0 = nc.dram_tensor("dbgE0", [128, 512], F32, kind="ExternalOutput").ap()
        dbgE1 = nc.dram_tensor("dbgE1", [128, 512], F32, kind="ExternalOutput").ap()
        dbgU = nc.dram_tensor("dbgU", [128, 512], F32, kind="ExternalOutput").ap()

    x2_scr = nc.dram_tensor("x2_scr", [NQ, D], F32, kind="Internal").ap()
    x2bf_scr = nc.dram_tensor("x2bf_scr", [NQ, D], BF16, kind="Internal").ap()

    ctx = ExitStack()

    onesbf_sb = ctx.enter_context(nc.sbuf_tensor("onesbf_sb", [1, 128], BF16, side="right"))
    onescol_sb = ctx.enter_context(nc.sbuf_tensor("onescol_sb", [128, 1], BF16, side="right"))
    onesmat_sb = ctx.enter_context(nc.sbuf_tensor("onesmat_sb", [128, 128], BF16, side="right"))
    onesf_sb = ctx.enter_context(nc.sbuf_tensor("onesf_sb", [1, 128], F32, side="right"))
    eps_sb = ctx.enter_context(nc.sbuf_tensor("eps_sb", [128, 1], F32, side="right"))

    psc = [ctx.enter_context(nc.psum_tensor(f"psc{i}", [128, 1024], F32)) for i in range(2)]
    pab = [ctx.enter_context(nc.psum_tensor(f"pa{i}", [128, 512], F32)) for i in range(2)]
    pdb = [ctx.enter_context(nc.psum_tensor(f"pd{i}", [128, 512], F32)) for i in range(2)]
    slots8 = [("a0", pab[0], 0), ("a1", pab[1], 0), ("d0", pdb[0], 0), ("d1", pdb[1], 0),
              ("s0L", psc[0], 0), ("s0R", psc[0], 512), ("s1L", psc[1], 0), ("s1R", psc[1], 512)]

    s_xt = Sem(ctx, nc, "s_xt")
    s_w = Sem(ctx, nc, "s_w")
    s_pe = Sem(ctx, nc, "s_pe")
    s_sc = Sem(ctx, nc, "s_sc")
    s_ve = Sem(ctx, nc, "s_ve")
    s_dq = Sem(ctx, nc, "s_dq")
    s_sy = Sem(ctx, nc, "s_sy")
    s_out = Sem(ctx, nc, "s_out")

    sch = Sched()
    blk = ctx.enter_context(nc.Block())

    bank_free = {}

    def wfree(bid):
        return bank_free.get(bid)

    with ctx:
        st_ab = ExitStack()
        attnT = st_ab.enter_context(nc.sbuf_tensor("attnT", [128, NH * NQ], BF16))
        st_a = ExitStack()

        def sba(name, shape, dt):
            return st_a.enter_context(nc.sbuf_tensor(name, shape, dt))

        wq_b = [sba(f"wqb{i}", [128, CC * G * HD], BF16) for i in range(2)]
        wk_b = [sba(f"wkb{i}", [128, CC * G * HD], BF16) for i in range(2)]
        wv_b = [sba(f"wvb{i}", [128, CC * G * HD], BF16) for i in range(2)]
        xt_b = [sba(f"xtb{i}", [128, CC * STRIP], BF16) for i in range(2)]
        KT = sba("KT", [128, G * S], BF16)
        QTm = sba("QTm", [128, G * NQ], BF16)
        Vm = sba("Vm", [128, KC * G * HD], BF16)
        nsc = sba("nsc", [128, 2 * 512], BF16)
        runmx = sba("runmx", [1, 8], F32)
        sc11 = sba("sc11", [1, 8], F32)
        negc = sba("negc", [1, G], F32)
        negc_col = sba("negc_col", [128, G], F32)
        expSp = [sba(f"expSp{i}", [128, 1024], BF16) for i in range(2)]
        dnln = [sba(f"dnln{i}", [128, 512], F32) for i in range(2)]
        recip_rep = [sba(f"recip_rep{i}", [128, 512], F32) for i in range(2)]

        for t_sb, t_dr in ((onesbf_sb, ones_bf), (onescol_sb, onescol_bf),
                           (onesmat_sb, onesmat_bf), (onesf_sb, ones_f), (eps_sb, epscol)):
            sch.op("gpsimd", lambda g, a=t_sb, b=t_dr: g.dma_start(a[:, :], b),
                   incs=[(s_w, 16)])

        def w_slice_dma(dst, srcT, p, waits=()):
            src = srcT.rearrange("(c p) d -> p c d", p=128)[:, :, p * G * HD:(p + 1) * G * HD]
            d3 = dst[:, :].rearrange("p (c n) -> p c n", c=CC)
            ev = None
            for half in range(2):
                cs = slice(half * 8, (half + 1) * 8)
                ev = sch.op("gpsimd",
                            lambda g, d=d3[:, cs, :], s2=src[:, cs, :]: g.dma_start(d, s2),
                            waits=waits, incs=[(s_w, 16)])
            return ev

        w_slice_dma(wq_b[0], wqT, 0)
        w_slice_dma(wk_b[0], wkT, 0)
        w_ready = [w_slice_dma(wv_b[0], wvT, 0)]

        xT_r = xT.rearrange("(c p) t -> p c t", p=128)

        strip_ev = {}
        strip_done = {}
        pass_proj_last = {}
        kq_evac_last = [None]
        attn_ev = []
        exp_free = {}
        expSp_free = {}
        gpair = [0]
        hqt_ctr = [0]
        dnln_free = {}
        rr_free = {}
        d_ev_last = [None]

        for p in range(npass):
            wq, wk, wv = wq_b[p % 2], wk_b[p % 2], wv_b[p % 2]
            grp_ctr = [0]
            norm_ctr = [0]
            nsc_free = {}

            mx_init_ev = sch.op("vector", lambda v: v.memset(runmx[:, 0:4], -1e30),
                                waits=[attn_ev[-1]] if attn_ev else (), incs=[(s_ve, 1)])
            norm_last = {0: mx_init_ev, 1: mx_init_ev, 2: mx_init_ev, 3: mx_init_ev}

            def proj_group(p, s, lhs_fn, rhs_fn, n_out, dst, doff, evac_eng):
                key, bank, boff = slots8[grp_ctr[0] % 8]
                grp_ctr[0] += 1
                waits = [strip_ev[(p, s)], wfree(key), w_ready[p]]

                def fn(t, bank=bank, boff=boff, lhs_fn=lhs_fn, rhs_fn=rhs_fn, n_out=n_out):
                    inst = None
                    for c in range(CC):
                        inst = t.matmul(bank[:, boff:boff + n_out], lhs_fn(c), rhs_fn(c),
                                        start=(c == 0), stop=(c == CC - 1))
                    return inst

                mm_ev = sch.op("tensor", fn, waits=waits, incs=[(s_pe, 1)])
                strip_done[(p, s)] = s_pe.n
                sem = s_ve if evac_eng == "vector" else s_sc

                def ev_fn(e, dst=dst, doff=doff, n_out=n_out, bank=bank, boff=boff):
                    if evac_eng == "vector":
                        return e.tensor_copy(dst[:, doff:doff + n_out], bank[:, boff:boff + n_out])
                    return e.copy(dst[:, doff:doff + n_out], bank[:, boff:boff + n_out])

                evac_ev = sch.op(evac_eng, ev_fn, waits=[mm_ev], incs=[(sem, 1)])
                bank_free[key] = evac_ev
                return evac_ev

            def sq_and_norm(p, src, soff, col, prior_ev):
                ping = norm_ctr[0] % 2
                norm_ctr[0] += 1
                waits = [prior_ev, nsc_free.get(ping)]
                sq_ev = sch.op("scalar",
                               lambda sc, src=src, soff=soff, ping=ping: sc.activation(
                                   nsc[:, ping * 512:(ping + 1) * 512],
                                   src[:, soff:soff + 512], AF.Square),
                               waits=waits, incs=[(s_sc, 1)])
                mm_ev = sch.op("tensor",
                               lambda t, ping=ping: t.matmul(
                                   pdb[0][0:1, :], onescol_sb[:, :],
                                   nsc[:, ping * 512:(ping + 1) * 512],
                                   start=True, stop=True),
                               waits=[sq_ev, wfree("d0")], incs=[(s_pe, 1)])
                nsc_free[ping] = mm_ev
                r_ev = sch.op("vector",
                              lambda v: v.reduce_max(sc11[:, 7:8], pdb[0][0:1, :], axis=AX.X),
                              waits=[mm_ev, norm_last[col]], incs=[(s_ve, 1)])
                bank_free["d0"] = r_ev
                m_ev = sch.op("vector",
                              lambda v, col=col: v.tensor_tensor(
                                  runmx[:, col:col + 1], runmx[:, col:col + 1],
                                  sc11[:, 7:8], ALU.max),
                              waits=[r_ev], incs=[(s_ve, 1)])
                norm_last[col] = m_ev

            for s in range(NSTRIP):
                waits = []
                if (p, s) != (0, 0) and (p, s) != (0, 1):
                    pp, ss = (p, s - 2) if s >= 2 else (p - 1, NSTRIP + s - 2)
                    waits = [(s_pe, strip_done[(pp, ss)])]
                dst = xt_b[s % 2][:, :].rearrange("p (c n) -> p c n", c=CC)
                src = xT_r[:, :, s * STRIP:(s + 1) * STRIP]
                evx = None
                for half in range(2):
                    cs = slice(half * 8, (half + 1) * 8)
                    evx = sch.op("gpsimd",
                                 lambda g, d=dst[:, cs, :], sr=src[:, cs, :]: g.dma_start(d, sr),
                                 waits=waits, incs=[(s_xt, 16)])
                strip_ev[(p, s)] = evx
                xb = xt_b[s % 2]

                for dc in range(G):
                    e1 = proj_group(
                        p, s,
                        lambda c, dc=dc, wk=wk: wk[:, c * G * HD + dc * HD: c * G * HD + (dc + 1) * HD],
                        lambda c, xb=xb: xb[:, c * STRIP:(c + 1) * STRIP],
                        STRIP, KT, dc * S + s * STRIP, "vector")
                    kq_evac_last[0] = e1
                    sq_and_norm(p, KT, dc * S + s * STRIP, 2 + dc, e1)
                if s < NQ // STRIP:
                    for dc in range(G):
                        e1 = proj_group(
                            p, s,
                            lambda c, dc=dc, wq=wq: wq[:, c * G * HD + dc * HD: c * G * HD + (dc + 1) * HD],
                            lambda c, xb=xb: xb[:, c * STRIP:(c + 1) * STRIP],
                            STRIP, QTm, dc * NQ + s * STRIP, "vector")
                        kq_evac_last[0] = e1
                        sq_and_norm(p, QTm, dc * NQ + s * STRIP, dc, e1)
                for tt in range(STRIP // 128):
                    ktile = s * (STRIP // 128) + tt
                    proj_group(
                        p, s,
                        lambda c, xb=xb, tt=tt: xb[:, c * STRIP + tt * 128: c * STRIP + (tt + 1) * 128],
                        lambda c, wv=wv: wv[:, c * G * HD:(c + 1) * G * HD],
                        G * HD, Vm, ktile * G * HD, "scalar")

            pass_proj_last[p] = s_pe.n

            if p + 1 < npass:
                nxt = (p + 1) % 2
                w0 = [(s_pe, pass_proj_last[p - 1])] if p >= 1 else ()
                w_slice_dma(wq_b[nxt], wqT, p + 1, waits=w0)
                w_slice_dma(wk_b[nxt], wkT, p + 1)
                w_ready.append(w_slice_dma(wv_b[nxt], wvT, p + 1))

            neg_evs = []
            for h in range(G):
                e1 = sch.op("scalar",
                            lambda sc, h=h: sc.activation(sc11[:, 0:1], runmx[:, h:h + 1], AF.Ln),
                            waits=[norm_last[h], norm_last[2 + h]], incs=[(s_sc, 1)])
                e2 = sch.op("scalar",
                            lambda sc, h=h: sc.activation(sc11[:, 1:2], runmx[:, 2 + h:3 + h], AF.Ln),
                            waits=[e1], incs=[(s_sc, 1)])
                e3 = sch.op("vector",
                            lambda v: v.tensor_tensor(sc11[:, 2:3], sc11[:, 0:1], sc11[:, 1:2], ALU.add),
                            waits=[e2], incs=[(s_ve, 1)])
                e4 = sch.op("scalar",
                            lambda sc: sc.activation(sc11[:, 3:4], sc11[:, 2:3], AF.Exp, scale=0.5),
                            waits=[e3], incs=[(s_sc, 1)])
                e5 = sch.op("vector",
                            lambda v, h=h: v.tensor_scalar(
                                negc[:, h:h + 1], sc11[:, 3:4], -1.01 * SCALE, None, ALU.mult),
                            waits=[e4], incs=[(s_ve, 1)])
                neg_evs.append(e5)
            rep_mm = sch.op("tensor",
                            lambda t: t.matmul(pab[0][:, 0:G], onesf_sb[:, :], negc[:, 0:G],
                                               start=True, stop=True),
                            waits=neg_evs + [wfree("a0")], incs=[(s_pe, 1)])
            rep_cp = sch.op("scalar",
                            lambda sc: sc.copy(negc_col[:, :], pab[0][:, 0:G]),
                            waits=[rep_mm], incs=[(s_sc, 1)])
            bank_free["a0"] = rep_cp

            psc_first = {0: True, 1: True}
            for h in range(G):
                hd = p * G + h
                for qt in range(NQT):
                    qoff = qt * QTW
                    par = hqt_ctr[0] % 2
                    hqt_ctr[0] += 1
                    at_key, at_bank, _ = slots8[par]
                    dn_key, dn_bank, _ = slots8[2 + par]
                    pend = []

                    def emit_attn_pair(pr2, e_ev, base_kc):
                        a_ev = d_ev = None
                        for half in range(2):
                            kc = base_kc + half
                            aw = [e_ev]
                            if kc == 0:
                                aw.append(wfree(at_key))

                            def mm_a(t, kc=kc, h=h, pr2=pr2, half=half, at_bank=at_bank):
                                return t.matmul(
                                    at_bank[:, :],
                                    Vm[:, kc * G * HD + h * HD: kc * G * HD + (h + 1) * HD],
                                    expSp[pr2][:, half * 512:(half + 1) * 512],
                                    start=(kc == 0), stop=(kc == KC - 1))

                            a_ev = sch.op("tensor", mm_a, waits=aw, incs=[(s_pe, 1)])
                            dw = [wfree(dn_key)] if kc == 0 else []

                            def mm_d(t, kc=kc, pr2=pr2, half=half, dn_bank=dn_bank):
                                return t.matmul(
                                    dn_bank[:, :], onesmat_sb[:, :],
                                    expSp[pr2][:, half * 512:(half + 1) * 512],
                                    start=(kc == 0), stop=(kc == KC - 1))

                            d_ev = sch.op("tensor", mm_d, waits=dw, incs=[(s_pe, 1)])
                        expSp_free[pr2] = d_ev
                        return a_ev, d_ev

                    a_ev = d_ev = None
                    for pr in range(KC // 2):
                        pr2 = gpair[0] % 2
                        gpair[0] += 1
                        scb = psc[0] if pr2 == 0 else psc[1]
                        s2_ev = None
                        for half in range(2):
                            kc = 2 * pr + half
                            sw = []
                            if half == 0:
                                sw.append(exp_free.get(pr2))
                                if psc_first[pr2]:
                                    sw += [wfree("s0L" if pr2 == 0 else "s1L"),
                                           wfree("s0R" if pr2 == 0 else "s1R")]
                                    psc_first[pr2] = False
                                if kc == 0 and qt == 0 and h == 0:
                                    sw.append(kq_evac_last[0])

                            def mm_s(t, h=h, kc=kc, qoff=qoff, half=half, scb=scb):
                                return t.matmul(
                                    scb[:, half * 512:(half + 1) * 512],
                                    KT[:, h * S + kc * 128:h * S + (kc + 1) * 128],
                                    QTm[:, h * NQ + qoff:h * NQ + qoff + QTW],
                                    start=True, stop=True)

                            s2_ev = sch.op("tensor", mm_s, waits=sw, incs=[(s_pe, 1)])

                        ew_ = [s2_ev, expSp_free.get(pr2)]
                        if pr == 0 and qt == 0 and h == 0:
                            ew_.append(rep_cp)
                        e_ev = sch.op("scalar",
                                      lambda sc, pr2=pr2, h=h, scb=scb: sc.activation(
                                          expSp[pr2][:, :], scb[:, :], AF.Exp,
                                          bias=negc_col[:, h:h + 1], scale=SCALE),
                                      waits=ew_, incs=[(s_sc, 1)])
                        exp_free[pr2] = e_ev
                        pend.append((pr, pr2, e_ev))
                        if len(pend) >= 2:
                            pr0, pr20, e0 = pend.pop(0)
                            a_ev, d_ev = emit_attn_pair(pr20, e0, 2 * pr0)
                    while pend:
                        pr0, pr20, e0 = pend.pop(0)
                        a_ev, d_ev = emit_attn_pair(pr20, e0, 2 * pr0)

                    r1 = sch.op("scalar",
                                lambda sc, dn_bank=dn_bank, par=par: sc.copy(
                                    dnln[par][:, :], dn_bank[:, :]),
                                waits=[d_ev, dnln_free.get(par)], incs=[(s_sc, 1)])
                    bank_free[dn_key] = r1
                    r2w = [r1, rr_free.get(par)]
                    r2 = sch.op("vector",
                                lambda v, par=par: v.reciprocal(recip_rep[par][:, :],
                                                                dnln[par][:, :]),
                                waits=r2w, incs=[(s_ve, 1)])
                    dnln_free[par] = r2
                    t_ev = sch.op("vector",
                                  lambda v, hd=hd, qoff=qoff, at_bank=at_bank, par=par:
                                  v.tensor_tensor(
                                      attnT[:, hd * NQ + qoff: hd * NQ + qoff + QTW],
                                      at_bank[:, :], recip_rep[par][:, :], ALU.mult),
                                  waits=[a_ev, r2], incs=[(s_ve, 1)])
                    attn_ev.append(t_ev)
                    bank_free[at_key] = t_ev
                    rr_free[par] = t_ev

            for i2 in range(2):
                if exp_free.get(i2) is not None:
                    bank_free["s0L" if i2 == 0 else "s1L"] = exp_free[i2]
                    bank_free["s0R" if i2 == 0 else "s1R"] = exp_free[i2]

        phA = [(s_pe, s_pe.n), (s_ve, s_ve.n), (s_sc, s_sc.n)]

        if dbg_proj:
            wv_ = [(s_pe, s_pe.n), (s_ve, s_ve.n), (s_sc, s_sc.n)]
            sch.op("gpsimd", lambda g: g.dma_start(dbgK, KT[:, :]), waits=wv_, incs=[(s_out, 16)])
            sch.op("gpsimd", lambda g: g.dma_start(dbgQ, QTm[:, :]), incs=[(s_out, 16)])
            sch.op("gpsimd", lambda g: g.dma_start(dbgV, Vm[:, :]), incs=[(s_out, 16)])
            sch.op("gpsimd", lambda g: g.dma_start(dbgN[:, 0:G], negc_col[:, :]), incs=[(s_out, 16)])
            sch.op("gpsimd", lambda g: g.dma_start(dbgN[0:1, 8:16], runmx[:, :]), incs=[(s_out, 16)])

        if dbg_attn:
            dmp = sch.op("gpsimd", lambda g: g.dma_start(dbgA, attnT[:, :]),
                         waits=[attn_ev[-1]], incs=[(s_out, 16)])
            sch.op("sync", lambda s2: s2.nop(), waits=[(s_out, s_out.n)])

        if not do_bc:
            sch.emit(blk)
            st_a.close()
            st_ab.close()
            return nc

        st_a.close()
        X2T0 = ctx.enter_context(nc.sbuf_tensor("X2T0", [128, CC * 1024], BF16, side="right"))
        x2bf = [ctx.enter_context(nc.sbuf_tensor(f"x2bf{i}", [128, D], BF16, side="right"))
                for i in range(2)]
        st_b = ExitStack()
        woT_b = st_b.enter_context(nc.sbuf_tensor("woT_b", [128, CC * D], BF16))
        xres = [st_b.enter_context(nc.sbuf_tensor(f"xres{i}", [128, D], F32)) for i in range(2)]
        cen = [st_b.enter_context(nc.sbuf_tensor(f"cen{i}", [128, D], F32)) for i in range(2)]
        nmean = [st_b.enter_context(nc.sbuf_tensor(f"nmean{i}", [128, 1], F32)) for i in range(2)]
        sumsq = [st_b.enter_context(nc.sbuf_tensor(f"sumsq{i}", [128, 1], F32)) for i in range(2)]
        rstd = [st_b.enter_context(nc.sbuf_tensor(f"rstd{i}", [128, 1], F32)) for i in range(2)]
        sqd = st_b.enter_context(nc.sbuf_tensor("sqd", [128, D], BF16))

        woT_ev = []
        for hdc in range(CC):
            wv_ = phA if hdc == 0 else ()
            woT_ev.append(sch.op("gpsimd",
                                 lambda g, hdc=hdc: g.dma_start(
                                     woT_b[:, hdc * D:(hdc + 1) * D],
                                     woT[hdc * 128:(hdc + 1) * 128, :]),
                                 waits=wv_, incs=[(s_w, 16)]))

        resid_ev = {}
        cen_free = {}
        x2w_ev = {}
        x2bf_free = {}
        tposed_ev = {}
        ln_free = {}

        for tt in range(NTT):
            pi = tt % 2
            xr_ev = sch.op("scalar",
                           lambda sc, tt=tt, pi=pi: sc.dma_start(
                               xres[pi][:, :], xq[tt * 128:(tt + 1) * 128, :]),
                           waits=[resid_ev.get(tt - 2)], incs=[(s_dq, 16)])
            g_evs = []
            slots_tt = [slots8[(tt * 4 + dmt) % 8] for dmt in range(4)]
            for dmt in range(4):
                key, bank, boff = slots_tt[dmt]
                waits = [wfree(key)]
                if tt == 0:
                    waits += phA

                def fn(t, dmt=dmt, tt=tt, bank=bank, boff=boff):
                    inst = None
                    for hdc in range(CC):
                        if tt == 0 and dmt == 0:
                            t.wait_ge(woT_ev[hdc][0].h, woT_ev[hdc][1])
                        inst = t.matmul(bank[:, boff:boff + 512],
                                        attnT[:, hdc * NQ + tt * 128:hdc * NQ + (tt + 1) * 128],
                                        woT_b[:, hdc * D + dmt * 512:hdc * D + (dmt + 1) * 512],
                                        start=(hdc == 0), stop=(hdc == CC - 1))
                    return inst

                g_evs.append(sch.op("tensor", fn, waits=waits, incs=[(s_pe, 1)]))
            r_evs = []
            for dmt in range(4):
                key, bank, boff = slots_tt[dmt]
                r_evs.append(sch.op("vector",
                                    lambda v, dmt=dmt, pi=pi, bank=bank, boff=boff: v.tensor_tensor(
                                        cen[pi][:, dmt * 512:(dmt + 1) * 512],
                                        bank[:, boff:boff + 512],
                                        xres[pi][:, dmt * 512:(dmt + 1) * 512], ALU.add),
                                    waits=[g_evs[dmt], xr_ev, cen_free.get(tt - 2)],
                                    incs=[(s_ve, 1)]))
                bank_free[key] = r_evs[-1]
            resid_ev[tt] = r_evs[-1]

            l1 = sch.op("vector",
                        lambda v, pi=pi: v.reduce_sum(nmean[pi][:, :], cen[pi][:, :], axis=AX.X, negate=True),
                        waits=r_evs + [ln_free.get(pi)], incs=[(s_ve, 1)])
            l2 = sch.op("vector",
                        lambda v, pi=pi: v.tensor_scalar(nmean[pi][:, :], nmean[pi][:, :], 1.0 / D, None, ALU.mult),
                        waits=[l1], incs=[(s_ve, 1)])
            l3 = sch.op("vector",
                        lambda v, pi=pi: v.tensor_scalar(cen[pi][:, :], cen[pi][:, :],
                                                         nmean[pi][:, :], None, ALU.add),
                        waits=[l2], incs=[(s_ve, 1)])
            l4 = sch.op("scalar",
                        lambda sc, pi=pi: sc.activation(sqd[:, :], cen[pi][:, :], AF.Square,
                                                        accum_out=sumsq[pi][:, :]),
                        waits=[l3], incs=[(s_sc, 1)])
            l5 = sch.op("scalar",
                        lambda sc, pi=pi: sc.activation(sumsq[pi][:, :], sumsq[pi][:, :], AF.Ln,
                                                 bias=eps_sb[:, 0:1], scale=1.0 / D),
                        waits=[l4], incs=[(s_sc, 1)])
            l6 = sch.op("scalar",
                        lambda sc, pi=pi: sc.activation(rstd[pi][:, :], sumsq[pi][:, :], AF.Exp, scale=-0.5),
                        waits=[l5], incs=[(s_sc, 1)])
            l7 = sch.op("vector",
                        lambda v, pi=pi: v.tensor_scalar(cen[pi][:, :], cen[pi][:, :],
                                                         rstd[pi][:, :], None, ALU.mult),
                        waits=[l6], incs=[(s_ve, 1)])
            l8 = sch.op("vector",
                        lambda v, pi=pi: v.tensor_copy(x2bf[pi][:, :], cen[pi][:, :]),
                        waits=[l7, x2bf_free.get(tt - 2)], incs=[(s_ve, 1)])
            ln_free[pi] = l7

            w1 = sch.op("scalar",
                        lambda sc, tt=tt, pi=pi: sc.dma_start(
                            x2_scr[tt * 128:(tt + 1) * 128, :], cen[pi][:, :]),
                        waits=[l7], incs=[(s_dq, 16)])
            x2w_ev[tt] = w1
            cen_free[tt] = w1
            if tt >= 8:
                w2 = sch.op("scalar",
                            lambda sc, tt=tt, pi=pi: sc.dma_start(
                                x2bf_scr[tt * 128:(tt + 1) * 128, :], x2bf[pi][:, :]),
                            waits=[l8], incs=[(s_dq, 16)])
                x2bf_free[tt] = w2
            else:
                tp = None
                for c in range(CC):
                    tp = sch.op("sync",
                                lambda s2, c=c, tt=tt, pi=pi: s2.dma_start(
                                    X2T0[:, c * 1024 + tt * 128:c * 1024 + (tt + 1) * 128],
                                    x2bf[pi][:, c * 128:(c + 1) * 128], transpose=True),
                                waits=[l8] if c == 0 else (), incs=[(s_sy, 16)])
                tposed_ev[tt] = tp
                x2bf_free[tt] = tp

            if dbg_x2:
                sch.op("gpsimd",
                       lambda g, tt=tt, pi=pi: g.dma_start(
                           dbgX[tt * 128:(tt + 1) * 128, :], cen[pi][:, :]),
                       waits=[l7], incs=[(s_out, 16)])

        phB = [(s_pe, s_pe.n), (s_ve, s_ve.n), (s_sc, s_sc.n), (s_dq, s_dq.n), (s_sy, s_sy.n)]

        st_b.close()
        st_ab.close()
        st_c = ExitStack()

        def sbc(name, shape, dt):
            return st_c.enter_context(nc.sbuf_tensor(name, shape, dt))

        acc = sbc("acc", [128, 8 * D], F32)
        X2T1 = sbc("X2T1", [128, CC * 1024], BF16)
        We_b = [sbc(f"Web{i}", [128, CC * 512], BF16) for i in range(2)]
        tmp = [sbc(f"tmp{i}", [128, 512], F32) for i in range(2)]
        yb = [sbc(f"yb{i}", [128, D], F32) for i in range(2)]
        gwb = sbc("gwb", [128, CC * NE], BF16)
        gbb = sbc("gbb", [1, NE], BF16)
        ebb = [sbc(f"ebb{i}", [1, D], BF16) for i in range(2)]
        zg = sbc("zg", [128, NE], F32)
        zz = sbc("zz", [128, NE], F32)
        m1 = sbc("m1", [128, 1], F32)
        m2 = sbc("m2", [128, 1], F32)
        mask = [sbc(f"mask{i}", [128, 8 * NE], F32) for i in range(2)]
        nmean2 = sbc("nmean2", [128, 1], F32)
        sumsq2 = sbc("sumsq2", [128, 1], F32)
        rstd2 = sbc("rstd2", [128, 1], F32)
        sqd2 = sbc("sqd2", [128, D], BF16)

        gw_ev = sch.op("gpsimd", lambda g: g.dma_start(gwb[:, :], gwt), waits=phB, incs=[(s_w, 16)])
        gb_ev = sch.op("gpsimd", lambda g: g.dma_start(gbb[:, :], gb), incs=[(s_w, 16)])

        grp_c = [0]
        acc_w = {}
        tmp_free = {}
        We_free = {}
        ebb_free = {}
        yb_free = {}
        acc_init_ev = {}
        ln2_chain = [None]
        mask_ev = {}
        halfC_pe_done = [None]
        tpose_c = {}
        x2bf_ld = {}
        last_y = []

        for ti in range(8):
            tt = 8 + ti
            pi = tt % 2
            ld = sch.op("scalar",
                        lambda sc, tt=tt, pi=pi: sc.dma_start(
                            x2bf[pi][:, :], x2bf_scr[tt * 128:(tt + 1) * 128, :]),
                        waits=[x2bf_free.get(tt - 2), x2w_ev.get(tt)], incs=[(s_dq, 16)])
            tp = None
            for c in range(CC):
                tp = sch.op("sync",
                            lambda s2, c=c, ti=ti, pi=pi: s2.dma_start(
                                X2T1[:, c * 1024 + ti * 128:c * 1024 + (ti + 1) * 128],
                                x2bf[pi][:, c * 128:(c + 1) * 128], transpose=True),
                            waits=[ld], incs=[(s_sy, 16)])
            tpose_c[tt] = tp
            x2bf_free[tt] = tp

        for H in range(2):
            X2Tc = X2T0 if H == 0 else X2T1
            for ti in range(8):
                tt = H * 8 + ti
                wv_ = [x2w_ev[tt], acc_w.get(("ln2", ti))]
                if H == 0 and ti == 0:
                    wv_ += phB
                acc_init_ev[ti] = sch.op("scalar",
                                         lambda sc, tt=tt, ti=ti: sc.dma_start(
                                             acc[:, ti * D:(ti + 1) * D],
                                             x2_scr[tt * 128:(tt + 1) * 128, :]),
                                         waits=wv_, incs=[(s_dq, 16)])

            for ti in range(8):
                tt = H * 8 + ti
                key, bank, boff = slots8[grp_c[0] % 8]
                grp_c[0] += 1
                wv_ = [gw_ev, gb_ev, wfree(key),
                       tposed_ev.get(tt) if H == 0 else tpose_c.get(tt)]

                def fng(t, ti=ti, bank=bank, boff=boff, X2Tc=X2Tc):
                    t.matmul(bank[:, boff:boff + NE], onesbf_sb[:, :], gbb[:, :], start=True, stop=False)
                    inst = None
                    for c in range(CC):
                        inst = t.matmul(bank[:, boff:boff + NE],
                                        X2Tc[:, c * 1024 + ti * 128:c * 1024 + (ti + 1) * 128],
                                        gwb[:, c * NE:(c + 1) * NE],
                                        start=False, stop=(c == CC - 1))
                    return inst

                gm = sch.op("tensor", fng, waits=wv_, incs=[(s_pe, 1)])
                zev = sch.op("scalar", lambda sc, bank=bank, boff=boff: sc.copy(
                                 zg[:, :], bank[:, boff:boff + NE]),
                             waits=[gm, mask_ev.get(H * 8 + ti - 1)], incs=[(s_sc, 1)])
                bank_free[key] = zev
                c1 = sch.op("vector", lambda v: v.reduce_max(m1[:, :], zg[:, :], axis=AX.X),
                            waits=[zev], incs=[(s_ve, 1)])
                c2 = sch.op("vector",
                            lambda v: v.tensor_scalar(zz[:, :], zg[:, :], m1[:, :], None, ALU.is_equal),
                            waits=[c1], incs=[(s_ve, 1)])
                c3 = sch.op("vector",
                            lambda v: v.scalar_tensor_tensor(zz[:, :], zz[:, :], -1e30, zg[:, :],
                                                             ALU.mult, ALU.add),
                            waits=[c2], incs=[(s_ve, 1)])
                c4 = sch.op("vector", lambda v: v.reduce_max(m2[:, :], zz[:, :], axis=AX.X),
                            waits=[c3], incs=[(s_ve, 1)])
                c5 = sch.op("vector",
                            lambda v, ti=ti, H=H: v.tensor_scalar(mask[H][:, ti * NE:(ti + 1) * NE],
                                                                  zg[:, :], m2[:, :], 0.5,
                                                                  ALU.is_ge, ALU.mult),
                            waits=[c4], incs=[(s_ve, 1)])
                mask_ev[H * 8 + ti] = c5

            for e in range(NE):
                wpi = e % 2
                eb_ev = sch.op("gpsimd",
                               lambda g, e=e, wpi=wpi: g.dma_start(
                                   ebb[wpi][:, :], eb[0:1, e * D:(e + 1) * D]),
                               waits=[ebb_free.get(wpi)], incs=[(s_w, 16)])
                for oq in range(4):
                    bpi = (e * 4 + oq) % 2
                    src = ew[e].rearrange("(c p) o -> p c o", p=128)[:, :, oq * 512:(oq + 1) * 512]
                    d3 = We_b[bpi][:, :].rearrange("p (c n) -> p c n", c=CC)
                    we_ev = None
                    for half in range(2):
                        cs = slice(half * 8, (half + 1) * 8)
                        we_ev = sch.op("gpsimd",
                                       lambda g, d=d3[:, cs, :], s2=src[:, cs, :]: g.dma_start(d, s2),
                                       waits=[We_free.get(bpi)],
                                       incs=[(s_w, 16)])
                    for ti in range(8):
                        key, bank, boff = slots8[grp_c[0] % 8]
                        grp_c[0] += 1
                        wv_ = [we_ev, eb_ev, wfree(key)]

                        def fne(t, ti=ti, bank=bank, boff=boff, e=e, oq=oq, wpi=wpi, bpi=bpi, X2Tc=X2Tc):
                            t.matmul(bank[:, boff:boff + 512], onesbf_sb[:, :],
                                     ebb[wpi][:, oq * 512:(oq + 1) * 512], start=True, stop=False)
                            inst = None
                            for c in range(CC):
                                inst = t.matmul(bank[:, boff:boff + 512],
                                                X2Tc[:, c * 1024 + ti * 128:c * 1024 + (ti + 1) * 128],
                                                We_b[bpi][:, c * 512:(c + 1) * 512],
                                                start=False, stop=(c == CC - 1))
                            return inst

                        hm = sch.op("tensor", fne, waits=wv_, incs=[(s_pe, 1)])
                        if ti == 7:
                            We_free[bpi] = hm
                            if oq == 3:
                                ebb_free[wpi] = hm
                        tpi = grp_c[0] % 2
                        rl = sch.op("scalar",
                                    lambda sc, bank=bank, boff=boff, ti=ti, e=e, tpi=tpi, H=H: sc.activation(
                                        tmp[tpi][:, :], bank[:, boff:boff + 512], AF.Relu,
                                        scale=mask[H][:, ti * NE + e:ti * NE + e + 1]),
                                    waits=[hm, mask_ev[H * 8 + ti], tmp_free.get(tpi)],
                                    incs=[(s_sc, 1)])
                        bank_free[key] = rl
                        ac = sch.op("vector",
                                    lambda v, ti=ti, oq=oq, tpi=tpi: v.tensor_tensor(
                                        acc[:, ti * D + oq * 512:ti * D + (oq + 1) * 512],
                                        acc[:, ti * D + oq * 512:ti * D + (oq + 1) * 512],
                                        tmp[tpi][:, :], ALU.add),
                                    waits=[rl, acc_w.get((ti, oq), acc_init_ev[ti])],
                                    incs=[(s_ve, 1)])
                        acc_w[(ti, oq)] = ac
                        tmp_free[tpi] = ac
            halfC_pe_done[0] = s_pe.n

            for ti in range(8):
                tt = H * 8 + ti
                pi = ti % 2
                aw = [acc_w[(ti, oq)] for oq in range(4)]
                l1 = sch.op("vector",
                            lambda v, ti=ti: v.reduce_sum(nmean2[:, :], acc[:, ti * D:(ti + 1) * D],
                                                          axis=AX.X, negate=True),
                            waits=aw + [ln2_chain[0]], incs=[(s_ve, 1)])
                l2 = sch.op("vector",
                            lambda v: v.tensor_scalar(nmean2[:, :], nmean2[:, :], 1.0 / D, None, ALU.mult),
                            waits=[l1], incs=[(s_ve, 1)])
                l3 = sch.op("vector",
                            lambda v, ti=ti: v.tensor_scalar(acc[:, ti * D:(ti + 1) * D],
                                                             acc[:, ti * D:(ti + 1) * D],
                                                             nmean2[:, :], None, ALU.add),
                            waits=[l2], incs=[(s_ve, 1)])
                l4 = sch.op("scalar",
                            lambda sc, ti=ti: sc.activation(sqd2[:, :], acc[:, ti * D:(ti + 1) * D],
                                                            AF.Square, accum_out=sumsq2[:, :]),
                            waits=[l3], incs=[(s_sc, 1)])
                l5 = sch.op("scalar",
                            lambda sc: sc.activation(sumsq2[:, :], sumsq2[:, :], AF.Ln,
                                                     bias=eps_sb[:, 0:1], scale=1.0 / D),
                            waits=[l4], incs=[(s_sc, 1)])
                l6 = sch.op("scalar",
                            lambda sc: sc.activation(rstd2[:, :], sumsq2[:, :], AF.Exp, scale=-0.5),
                            waits=[l5], incs=[(s_sc, 1)])
                l7 = sch.op("vector",
                            lambda v, ti=ti, pi=pi: v.tensor_scalar(
                                yb[pi][:, :], acc[:, ti * D:(ti + 1) * D],
                                rstd2[:, :], None, ALU.mult),
                            waits=[l6, yb_free.get(pi)], incs=[(s_ve, 1)])
                ln2_chain[0] = l7
                acc_w[("ln2", ti)] = l7
                yw = sch.op("scalar",
                            lambda sc, tt=tt, pi=pi: sc.dma_start(
                                y[tt * 128:(tt + 1) * 128, :], yb[pi][:, :]),
                            waits=[l7], incs=[(s_out, 16)])
                yb_free[pi] = yw
                last_y.append(yw)

        sch.op("sync", lambda s2: s2.nop(), waits=[(s_out, s_out.n)])

        sch.emit(blk)
        st_c.close()
    return nc


_NC_CACHE = {}


def _get_nc():
    if "nc" not in _NC_CACHE:
        _NC_CACHE["nc"] = build_program()
    return _NC_CACHE["nc"]


def kernel(L0_attn_hidden_states, L0_attn_Wq, L0_attn_Wk, L0_attn_Wv, L0_attn_Wo,
           L0_ffn_gate_w, L0_ffn_gate_b, L0_ffn_expert_w, L0_ffn_expert_b):
    x = np.ascontiguousarray(np.asarray(L0_attn_hidden_states, dtype=np.float32))
    Wq = np.asarray(L0_attn_Wq, dtype=np.float32)
    Wk = np.asarray(L0_attn_Wk, dtype=np.float32)
    Wv = np.asarray(L0_attn_Wv, dtype=np.float32)
    Wo = np.asarray(L0_attn_Wo, dtype=np.float32)
    gate_w = np.asarray(L0_ffn_gate_w, dtype=np.float32)
    gate_b = np.asarray(L0_ffn_gate_b, dtype=np.float32)
    expert_w = np.ascontiguousarray(np.asarray(L0_ffn_expert_w, dtype=np.float32))
    expert_b = np.asarray(L0_ffn_expert_b, dtype=np.float32)

    wqT = np.ascontiguousarray(Wq.T)
    wkT = np.ascontiguousarray(Wk.T)
    wvT = np.ascontiguousarray(Wv.T)
    woT = np.ascontiguousarray(Wo.T)
    gwt = np.ascontiguousarray(gate_w.reshape(CC, 128, NE).transpose(1, 0, 2).reshape(128, CC * NE))
    gb1 = np.ascontiguousarray(gate_b.reshape(1, NE))
    eb1 = np.ascontiguousarray(expert_b.reshape(1, NE * D))
    ones_bf = np.ones((1, 128), dtype=ml_dtypes.bfloat16)
    onescol_bf = np.ones((128, 1), dtype=ml_dtypes.bfloat16)
    onesmat_bf1 = np.ones((128, 128), dtype=ml_dtypes.bfloat16)
    ones_f = np.ones((1, 128), dtype=np.float32)
    epscol = np.full((128, 1), EPS, dtype=np.float32)

    nc = _get_nc()
    in_maps = []
    for core in range(8):
        b, hf = core // 2, core % 2
        own = x[b, hf * NQ:(hf + 1) * NQ]
        oth = x[b, (1 - hf) * NQ:(2 - hf) * NQ]
        xroll = np.concatenate([own, oth], axis=0)
        in_maps.append({
            "xT": np.ascontiguousarray(xroll.T),
            "xq": np.ascontiguousarray(own),
            "wqT": wqT, "wkT": wkT, "wvT": wvT, "woT": woT,
            "gwt": gwt, "gb": gb1, "ew": expert_w, "eb": eb1,
            "ones_bf": ones_bf, "onescol_bf": onescol_bf, "onesmat_bf": onesmat_bf1,
            "ones_f": ones_f, "epscol": epscol,
        })

    res = run_bass_kernel_spmd(nc, in_maps, core_ids=list(range(8)))
    out = np.empty((4, S, D), dtype=np.float32)
    for core in range(8):
        b, hf = core // 2, core % 2
        out[b, hf * NQ:(hf + 1) * NQ] = res.results[core]["y"]
    return out


if __name__ == "__main__":
    build_program()
    print("build ok")


# version 11
# speedup vs baseline: 1.1206x; 1.1206x over previous
import numpy as np
import ml_dtypes
from contextlib import ExitStack

import concourse.bass as bass
import concourse.mybir as mybir
from concourse.bass_utils import run_bass_kernel_spmd

F32 = mybir.dt.float32
BF16 = mybir.dt.bfloat16
AF = mybir.ActivationFunctionType
ALU = mybir.AluOpType
AX = mybir.AxisListType

D = 2048
S = 4096
NQ = 2048
NH = 16
HD = 128
G = 2
CC = D // 128
KC = S // 128
QTW = 512
NQT = NQ // QTW
STRIP = 512
NSTRIP = S // STRIP
NE = 8
SCALE = float(HD) ** -0.5
EPS = 1e-5
NTT = NQ // 128

ENGINES = ("tensor", "vector", "scalar", "sync", "gpsimd")


class Sem:
    def __init__(self, ctx, nc, name):
        self.h = ctx.enter_context(nc.semaphore(name))
        self.n = 0
        self.name = name


class Sched:
    def __init__(self):
        self.q = {e: [] for e in ENGINES}

    def op(self, eng, fn, waits=(), incs=()):
        ev = None
        inc_list = []
        for sem, amt in incs:
            sem.n += amt
            inc_list.append((sem, amt))
            if ev is None:
                ev = (sem, sem.n)
        self.q[eng].append((tuple(w for w in waits if w is not None), fn, tuple(inc_list)))
        return ev

    def emit(self, blk):
        for eng in ENGINES:
            items = self.q[eng]
            if not items:
                continue

            def body(engine, items=items):
                for waits, fn, incs in items:
                    for sem, val in waits:
                        engine.wait_ge(sem.h, val)
                    inst = fn(engine)
                    for sem, amt in incs:
                        inst.then_inc(sem.h, amt)

            getattr(blk, eng)(body)


def build_program(npass=8, do_bc=True, dbg_attn=False, dbg_x2=False, dbg_proj=False,
                  dbg_att1=False):
    nc = bass.Bass("TRN2", target_bir_lowering=False)

    xT = nc.dram_tensor("xT", [D, S], F32, kind="ExternalInput").ap()
    xq = nc.dram_tensor("xq", [NQ, D], F32, kind="ExternalInput").ap()
    wqT = nc.dram_tensor("wqT", [D, D], F32, kind="ExternalInput").ap()
    wkT = nc.dram_tensor("wkT", [D, D], F32, kind="ExternalInput").ap()
    wvT = nc.dram_tensor("wvT", [D, D], F32, kind="ExternalInput").ap()
    woT = nc.dram_tensor("woT", [D, D], F32, kind="ExternalInput").ap()
    gwt = nc.dram_tensor("gwt", [128, CC * NE], F32, kind="ExternalInput").ap()
    gb = nc.dram_tensor("gb", [1, NE], F32, kind="ExternalInput").ap()
    ew = nc.dram_tensor("ew", [NE, D, D], F32, kind="ExternalInput").ap()
    eb = nc.dram_tensor("eb", [1, NE * D], F32, kind="ExternalInput").ap()
    ones_bf = nc.dram_tensor("ones_bf", [1, 128], BF16, kind="ExternalInput").ap()
    onescol_bf = nc.dram_tensor("onescol_bf", [128, 1], BF16, kind="ExternalInput").ap()
    onesmat_bf = nc.dram_tensor("onesmat_bf", [128, 128], BF16, kind="ExternalInput").ap()
    ones_f = nc.dram_tensor("ones_f", [1, 128], F32, kind="ExternalInput").ap()
    epscol = nc.dram_tensor("epscol", [128, 1], F32, kind="ExternalInput").ap()

    y = nc.dram_tensor("y", [NQ, D], F32, kind="ExternalOutput").ap()
    dbgA = nc.dram_tensor("dbgA", [128, NH * NQ], F32, kind="ExternalOutput").ap() if dbg_attn else None
    dbgX = nc.dram_tensor("dbgX", [NQ, D], F32, kind="ExternalOutput").ap() if dbg_x2 else None
    if dbg_proj:
        dbgK = nc.dram_tensor("dbgK", [128, G * S], F32, kind="ExternalOutput").ap()
        dbgQ = nc.dram_tensor("dbgQ", [128, G * NQ], F32, kind="ExternalOutput").ap()
        dbgV = nc.dram_tensor("dbgV", [128, KC * G * HD], F32, kind="ExternalOutput").ap()
        dbgN = nc.dram_tensor("dbgN", [128, 16], F32, kind="ExternalOutput").ap()

    if dbg_att1:
        dbgD = nc.dram_tensor("dbgD", [1, 512], F32, kind="ExternalOutput").ap()
        dbgR = nc.dram_tensor("dbgR", [128, 512], F32, kind="ExternalOutput").ap()
        dbgE = nc.dram_tensor("dbgE", [128, 512], F32, kind="ExternalOutput").ap()
        dbgE0 = nc.dram_tensor("dbgE0", [128, 512], F32, kind="ExternalOutput").ap()
        dbgE1 = nc.dram_tensor("dbgE1", [128, 512], F32, kind="ExternalOutput").ap()
        dbgU = nc.dram_tensor("dbgU", [128, 512], F32, kind="ExternalOutput").ap()

    x2_scr = nc.dram_tensor("x2_scr", [NQ, D], F32, kind="Internal").ap()
    x2bf_scr = nc.dram_tensor("x2bf_scr", [NQ, D], BF16, kind="Internal").ap()

    ctx = ExitStack()

    onesbf_sb = ctx.enter_context(nc.sbuf_tensor("onesbf_sb", [1, 128], BF16, side="right"))
    onescol_sb = ctx.enter_context(nc.sbuf_tensor("onescol_sb", [128, 1], BF16, side="right"))
    onesmat_sb = ctx.enter_context(nc.sbuf_tensor("onesmat_sb", [128, 128], BF16, side="right"))
    onesf_sb = ctx.enter_context(nc.sbuf_tensor("onesf_sb", [1, 128], F32, side="right"))
    eps_sb = ctx.enter_context(nc.sbuf_tensor("eps_sb", [128, 1], F32, side="right"))

    psc = [ctx.enter_context(nc.psum_tensor(f"psc{i}", [128, 1024], F32)) for i in range(2)]
    pab = [ctx.enter_context(nc.psum_tensor(f"pa{i}", [128, 512], F32)) for i in range(2)]
    pdb = [ctx.enter_context(nc.psum_tensor(f"pd{i}", [128, 512], F32)) for i in range(2)]
    slots8 = [("a0", pab[0], 0), ("a1", pab[1], 0), ("d0", pdb[0], 0), ("d1", pdb[1], 0),
              ("s0L", psc[0], 0), ("s0R", psc[0], 512), ("s1L", psc[1], 0), ("s1R", psc[1], 512)]

    s_xt = Sem(ctx, nc, "s_xt")
    s_w = Sem(ctx, nc, "s_w")
    s_pe = Sem(ctx, nc, "s_pe")
    s_sc = Sem(ctx, nc, "s_sc")
    s_ve = Sem(ctx, nc, "s_ve")
    s_dq = Sem(ctx, nc, "s_dq")
    s_sy = Sem(ctx, nc, "s_sy")
    s_out = Sem(ctx, nc, "s_out")

    sch = Sched()
    blk = ctx.enter_context(nc.Block())

    bank_free = {}

    def wfree(bid):
        return bank_free.get(bid)

    with ctx:
        st_ab = ExitStack()
        attnT = st_ab.enter_context(nc.sbuf_tensor("attnT", [128, NH * NQ], BF16))
        st_a = ExitStack()

        def sba(name, shape, dt):
            return st_a.enter_context(nc.sbuf_tensor(name, shape, dt))

        st_a1 = ExitStack()
        wq_b = [st_a1.enter_context(nc.sbuf_tensor(f"wqb{i}", [128, CC * G * HD], BF16, side="right")) for i in range(2)]
        wk_b = [st_a1.enter_context(nc.sbuf_tensor(f"wkb{i}", [128, CC * G * HD], BF16, side="right")) for i in range(2)]
        wv_b = [st_a1.enter_context(nc.sbuf_tensor(f"wvb{i}", [128, CC * G * HD], BF16, side="right")) for i in range(2)]
        xt_b = [st_a1.enter_context(nc.sbuf_tensor(f"xtb{i}", [128, CC * STRIP], BF16, side="right")) for i in range(2)]
        KT = sba("KT", [128, G * S], BF16)
        QTm = sba("QTm", [128, G * NQ], BF16)
        Vm = sba("Vm", [128, KC * G * HD], BF16)
        nsc = sba("nsc", [128, 2 * 512], BF16)
        runmx = sba("runmx", [1, 8], F32)
        sc11 = sba("sc11", [1, 8], F32)
        negc = sba("negc", [1, G], F32)
        negc_col = sba("negc_col", [128, G], F32)
        expSp = [sba(f"expSp{i}", [128, 1024], BF16) for i in range(2)]
        dnln = [sba(f"dnln{i}", [128, 512], F32) for i in range(2)]
        recip_rep = [sba(f"recip_rep{i}", [128, 512], F32) for i in range(2)]

        for t_sb, t_dr in ((onesbf_sb, ones_bf), (onescol_sb, onescol_bf),
                           (onesmat_sb, onesmat_bf), (onesf_sb, ones_f), (eps_sb, epscol)):
            sch.op("gpsimd", lambda g, a=t_sb, b=t_dr: g.dma_start(a[:, :], b),
                   incs=[(s_w, 16)])

        def w_slice_dma(dst, srcT, p, waits=()):
            src = srcT.rearrange("(c p) d -> p c d", p=128)[:, :, p * G * HD:(p + 1) * G * HD]
            d3 = dst[:, :].rearrange("p (c n) -> p c n", c=CC)
            ev = None
            for half in range(2):
                cs = slice(half * 8, (half + 1) * 8)
                ev = sch.op("gpsimd",
                            lambda g, d=d3[:, cs, :], s2=src[:, cs, :]: g.dma_start(d, s2),
                            waits=waits, incs=[(s_w, 16)])
            return ev

        w_slice_dma(wq_b[0], wqT, 0)
        w_slice_dma(wk_b[0], wkT, 0)
        w_ready = [w_slice_dma(wv_b[0], wvT, 0)]

        xT_r = xT.rearrange("(c p) t -> p c t", p=128)

        strip_ev = {}
        strip_done = {}
        pass_proj_last = {}
        kq_evac_last = [None]
        attn_ev = []
        exp_free = {}
        expSp_free = {}
        gpair = [0]
        hqt_ctr = [0]
        dnln_free = {}
        rr_free = {}
        d_ev_last = [None]

        for p in range(npass):
            wq, wk, wv = wq_b[p % 2], wk_b[p % 2], wv_b[p % 2]
            grp_ctr = [0]
            norm_ctr = [0]
            nsc_free = {}

            mx_init_ev = sch.op("vector", lambda v: v.memset(runmx[:, 0:4], -1e30),
                                waits=[attn_ev[-1]] if attn_ev else (), incs=[(s_ve, 1)])
            norm_last = {0: mx_init_ev, 1: mx_init_ev, 2: mx_init_ev, 3: mx_init_ev}

            def proj_group(p, s, lhs_fn, rhs_fn, n_out, dst, doff, evac_eng):
                key, bank, boff = slots8[grp_ctr[0] % 8]
                grp_ctr[0] += 1
                waits = [strip_ev[(p, s)], wfree(key), w_ready[p]]

                def fn(t, bank=bank, boff=boff, lhs_fn=lhs_fn, rhs_fn=rhs_fn, n_out=n_out):
                    inst = None
                    for c in range(CC):
                        inst = t.matmul(bank[:, boff:boff + n_out], lhs_fn(c), rhs_fn(c),
                                        start=(c == 0), stop=(c == CC - 1))
                    return inst

                mm_ev = sch.op("tensor", fn, waits=waits, incs=[(s_pe, 1)])
                strip_done[(p, s)] = s_pe.n
                sem = s_ve if evac_eng == "vector" else s_sc

                def ev_fn(e, dst=dst, doff=doff, n_out=n_out, bank=bank, boff=boff):
                    if evac_eng == "vector":
                        return e.tensor_copy(dst[:, doff:doff + n_out], bank[:, boff:boff + n_out])
                    return e.copy(dst[:, doff:doff + n_out], bank[:, boff:boff + n_out])

                evac_ev = sch.op(evac_eng, ev_fn, waits=[mm_ev], incs=[(sem, 1)])
                bank_free[key] = evac_ev
                return evac_ev

            def sq_and_norm(p, src, soff, col, prior_ev):
                ping = norm_ctr[0] % 2
                norm_ctr[0] += 1
                waits = [prior_ev, nsc_free.get(ping)]
                sq_ev = sch.op("vector",
                               lambda v, src=src, soff=soff, ping=ping: v.tensor_tensor(
                                   nsc[:, ping * 512:(ping + 1) * 512],
                                   src[:, soff:soff + 512], src[:, soff:soff + 512],
                                   ALU.mult),
                               waits=waits, incs=[(s_ve, 1)])
                mm_ev = sch.op("tensor",
                               lambda t, ping=ping: t.matmul(
                                   pdb[0][0:1, :], onescol_sb[:, :],
                                   nsc[:, ping * 512:(ping + 1) * 512],
                                   start=True, stop=True),
                               waits=[sq_ev, wfree("d0")], incs=[(s_pe, 1)])
                nsc_free[ping] = mm_ev
                r_ev = sch.op("vector",
                              lambda v: v.reduce_max(sc11[:, 7:8], pdb[0][0:1, :], axis=AX.X),
                              waits=[mm_ev, norm_last[col]], incs=[(s_ve, 1)])
                bank_free["d0"] = r_ev
                m_ev = sch.op("vector",
                              lambda v, col=col: v.tensor_tensor(
                                  runmx[:, col:col + 1], runmx[:, col:col + 1],
                                  sc11[:, 7:8], ALU.max),
                              waits=[r_ev], incs=[(s_ve, 1)])
                norm_last[col] = m_ev

            for s in range(NSTRIP):
                waits = []
                if (p, s) != (0, 0) and (p, s) != (0, 1):
                    pp, ss = (p, s - 2) if s >= 2 else (p - 1, NSTRIP + s - 2)
                    waits = [(s_pe, strip_done[(pp, ss)])]
                dst = xt_b[s % 2][:, :].rearrange("p (c n) -> p c n", c=CC)
                src = xT_r[:, :, s * STRIP:(s + 1) * STRIP]
                evx = None
                for half in range(2):
                    cs = slice(half * 8, (half + 1) * 8)
                    evx = sch.op("gpsimd",
                                 lambda g, d=dst[:, cs, :], sr=src[:, cs, :]: g.dma_start(d, sr),
                                 waits=waits, incs=[(s_xt, 16)])
                strip_ev[(p, s)] = evx
                xb = xt_b[s % 2]

                for dc in range(G):
                    e1 = proj_group(
                        p, s,
                        lambda c, dc=dc, wk=wk: wk[:, c * G * HD + dc * HD: c * G * HD + (dc + 1) * HD],
                        lambda c, xb=xb: xb[:, c * STRIP:(c + 1) * STRIP],
                        STRIP, KT, dc * S + s * STRIP, "vector")
                    kq_evac_last[0] = e1
                    sq_and_norm(p, KT, dc * S + s * STRIP, 2 + dc, e1)
                if s < NQ // STRIP:
                    for dc in range(G):
                        e1 = proj_group(
                            p, s,
                            lambda c, dc=dc, wq=wq: wq[:, c * G * HD + dc * HD: c * G * HD + (dc + 1) * HD],
                            lambda c, xb=xb: xb[:, c * STRIP:(c + 1) * STRIP],
                            STRIP, QTm, dc * NQ + s * STRIP, "vector")
                        kq_evac_last[0] = e1
                        sq_and_norm(p, QTm, dc * NQ + s * STRIP, dc, e1)
                for tt in range(STRIP // 128):
                    ktile = s * (STRIP // 128) + tt
                    proj_group(
                        p, s,
                        lambda c, xb=xb, tt=tt: xb[:, c * STRIP + tt * 128: c * STRIP + (tt + 1) * 128],
                        lambda c, wv=wv: wv[:, c * G * HD:(c + 1) * G * HD],
                        G * HD, Vm, ktile * G * HD, "scalar")

            pass_proj_last[p] = s_pe.n

            if p + 1 < npass:
                nxt = (p + 1) % 2
                w0 = [(s_pe, pass_proj_last[p - 1])] if p >= 1 else ()
                w_slice_dma(wq_b[nxt], wqT, p + 1, waits=w0)
                w_slice_dma(wk_b[nxt], wkT, p + 1)
                w_ready.append(w_slice_dma(wv_b[nxt], wvT, p + 1))

            neg_evs = []
            for h in range(G):
                e1 = sch.op("scalar",
                            lambda sc, h=h: sc.activation(sc11[:, 0:1], runmx[:, h:h + 1], AF.Ln),
                            waits=[norm_last[h], norm_last[2 + h]], incs=[(s_sc, 1)])
                e2 = sch.op("scalar",
                            lambda sc, h=h: sc.activation(sc11[:, 1:2], runmx[:, 2 + h:3 + h], AF.Ln),
                            waits=[e1], incs=[(s_sc, 1)])
                e3 = sch.op("vector",
                            lambda v: v.tensor_tensor(sc11[:, 2:3], sc11[:, 0:1], sc11[:, 1:2], ALU.add),
                            waits=[e2], incs=[(s_ve, 1)])
                e4 = sch.op("scalar",
                            lambda sc: sc.activation(sc11[:, 3:4], sc11[:, 2:3], AF.Exp, scale=0.5),
                            waits=[e3], incs=[(s_sc, 1)])
                e5 = sch.op("vector",
                            lambda v, h=h: v.tensor_scalar(
                                negc[:, h:h + 1], sc11[:, 3:4], -1.01 * SCALE, None, ALU.mult),
                            waits=[e4], incs=[(s_ve, 1)])
                neg_evs.append(e5)
            rep_mm = sch.op("tensor",
                            lambda t: t.matmul(pab[0][:, 0:G], onesf_sb[:, :], negc[:, 0:G],
                                               start=True, stop=True),
                            waits=neg_evs + [wfree("a0")], incs=[(s_pe, 1)])
            rep_cp = sch.op("scalar",
                            lambda sc: sc.copy(negc_col[:, :], pab[0][:, 0:G]),
                            waits=[rep_mm], incs=[(s_sc, 1)])
            bank_free["a0"] = rep_cp

            psc_first = {0: True, 1: True}
            for h in range(G):
                hd = p * G + h
                for qt in range(NQT):
                    qoff = qt * QTW
                    par = hqt_ctr[0] % 2
                    hqt_ctr[0] += 1
                    at_key, at_bank, _ = slots8[par]
                    dn_key, dn_bank, _ = slots8[2 + par]
                    pend = []

                    def emit_attn_pair(pr2, e_ev, base_kc):
                        a_ev = d_ev = None
                        for half in range(2):
                            kc = base_kc + half
                            aw = [e_ev]
                            if kc == 0:
                                aw.append(wfree(at_key))

                            def mm_a(t, kc=kc, h=h, pr2=pr2, half=half, at_bank=at_bank):
                                return t.matmul(
                                    at_bank[:, :],
                                    Vm[:, kc * G * HD + h * HD: kc * G * HD + (h + 1) * HD],
                                    expSp[pr2][:, half * 512:(half + 1) * 512],
                                    start=(kc == 0), stop=(kc == KC - 1))

                            a_ev = sch.op("tensor", mm_a, waits=aw, incs=[(s_pe, 1)])
                            dw = [wfree(dn_key)] if kc == 0 else []

                            def mm_d(t, kc=kc, pr2=pr2, half=half, dn_bank=dn_bank):
                                return t.matmul(
                                    dn_bank[:, :], onesmat_sb[:, :],
                                    expSp[pr2][:, half * 512:(half + 1) * 512],
                                    start=(kc == 0), stop=(kc == KC - 1))

                            d_ev = sch.op("tensor", mm_d, waits=dw, incs=[(s_pe, 1)])
                        expSp_free[pr2] = d_ev
                        return a_ev, d_ev

                    a_ev = d_ev = None
                    for pr in range(KC // 2):
                        pr2 = gpair[0] % 2
                        gpair[0] += 1
                        scb = psc[0] if pr2 == 0 else psc[1]
                        s2_ev = None
                        for half in range(2):
                            kc = 2 * pr + half
                            sw = []
                            if half == 0:
                                sw.append(exp_free.get(pr2))
                                if psc_first[pr2]:
                                    sw += [wfree("s0L" if pr2 == 0 else "s1L"),
                                           wfree("s0R" if pr2 == 0 else "s1R")]
                                    psc_first[pr2] = False
                                if kc == 0 and qt == 0 and h == 0:
                                    sw.append(kq_evac_last[0])

                            def mm_s(t, h=h, kc=kc, qoff=qoff, half=half, scb=scb):
                                return t.matmul(
                                    scb[:, half * 512:(half + 1) * 512],
                                    KT[:, h * S + kc * 128:h * S + (kc + 1) * 128],
                                    QTm[:, h * NQ + qoff:h * NQ + qoff + QTW],
                                    start=True, stop=True)

                            s2_ev = sch.op("tensor", mm_s, waits=sw, incs=[(s_pe, 1)])

                        ew_ = [s2_ev, expSp_free.get(pr2)]
                        if pr == 0 and qt == 0 and h == 0:
                            ew_.append(rep_cp)
                        e_ev = sch.op("scalar",
                                      lambda sc, pr2=pr2, h=h, scb=scb: sc.activation(
                                          expSp[pr2][:, :], scb[:, :], AF.Exp,
                                          bias=negc_col[:, h:h + 1], scale=SCALE),
                                      waits=ew_, incs=[(s_sc, 1)])
                        exp_free[pr2] = e_ev
                        pend.append((pr, pr2, e_ev))
                        if len(pend) >= 2:
                            pr0, pr20, e0 = pend.pop(0)
                            a_ev, d_ev = emit_attn_pair(pr20, e0, 2 * pr0)
                    while pend:
                        pr0, pr20, e0 = pend.pop(0)
                        a_ev, d_ev = emit_attn_pair(pr20, e0, 2 * pr0)

                    r1 = sch.op("vector",
                                lambda v, dn_bank=dn_bank, par=par: v.tensor_copy(
                                    dnln[par][:, :], dn_bank[:, :]),
                                waits=[d_ev, dnln_free.get(par)], incs=[(s_ve, 1)])
                    bank_free[dn_key] = r1
                    r2w = [r1, rr_free.get(par)]
                    r2 = sch.op("vector",
                                lambda v, par=par: v.reciprocal(recip_rep[par][:, :],
                                                                dnln[par][:, :]),
                                waits=r2w, incs=[(s_ve, 1)])
                    dnln_free[par] = r2
                    t_ev = sch.op("vector",
                                  lambda v, hd=hd, qoff=qoff, at_bank=at_bank, par=par:
                                  v.tensor_tensor(
                                      attnT[:, hd * NQ + qoff: hd * NQ + qoff + QTW],
                                      at_bank[:, :], recip_rep[par][:, :], ALU.mult),
                                  waits=[a_ev, r2], incs=[(s_ve, 1)])
                    attn_ev.append(t_ev)
                    bank_free[at_key] = t_ev
                    rr_free[par] = t_ev

            for i2 in range(2):
                if exp_free.get(i2) is not None:
                    bank_free["s0L" if i2 == 0 else "s1L"] = exp_free[i2]
                    bank_free["s0R" if i2 == 0 else "s1R"] = exp_free[i2]

        phA = [(s_pe, s_pe.n), (s_ve, s_ve.n), (s_sc, s_sc.n)]

        if dbg_proj:
            wv_ = [(s_pe, s_pe.n), (s_ve, s_ve.n), (s_sc, s_sc.n)]
            sch.op("gpsimd", lambda g: g.dma_start(dbgK, KT[:, :]), waits=wv_, incs=[(s_out, 16)])
            sch.op("gpsimd", lambda g: g.dma_start(dbgQ, QTm[:, :]), incs=[(s_out, 16)])
            sch.op("gpsimd", lambda g: g.dma_start(dbgV, Vm[:, :]), incs=[(s_out, 16)])
            sch.op("gpsimd", lambda g: g.dma_start(dbgN[:, 0:G], negc_col[:, :]), incs=[(s_out, 16)])
            sch.op("gpsimd", lambda g: g.dma_start(dbgN[0:1, 8:16], runmx[:, :]), incs=[(s_out, 16)])

        if dbg_attn:
            dmp = sch.op("gpsimd", lambda g: g.dma_start(dbgA, attnT[:, :]),
                         waits=[attn_ev[-1]], incs=[(s_out, 16)])
            sch.op("sync", lambda s2: s2.nop(), waits=[(s_out, s_out.n)])

        if not do_bc:
            sch.emit(blk)
            st_a.close()
            st_ab.close()
            return nc

        st_a1.close()
        fenceA1 = [(s_pe, pass_proj_last[npass - 1]), (s_xt, s_xt.n), (s_w, s_w.n)]
        X2T0 = ctx.enter_context(nc.sbuf_tensor("X2T0", [128, CC * 1024], BF16, side="right"))
        x2bf = [ctx.enter_context(nc.sbuf_tensor(f"x2bf{i}", [128, D], BF16, side="right"))
                for i in range(2)]
        st_a.close()
        st_b = ExitStack()
        woT_b = st_b.enter_context(nc.sbuf_tensor("woT_b", [128, CC * D], BF16, side="right"))
        xres = [st_b.enter_context(nc.sbuf_tensor(f"xres{i}", [128, D], F32)) for i in range(2)]
        cen = [st_b.enter_context(nc.sbuf_tensor(f"cen{i}", [128, D], F32)) for i in range(2)]
        nmean = [st_b.enter_context(nc.sbuf_tensor(f"nmean{i}", [128, 1], F32)) for i in range(2)]
        sumsq = [st_b.enter_context(nc.sbuf_tensor(f"sumsq{i}", [128, 1], F32)) for i in range(2)]
        rstd = [st_b.enter_context(nc.sbuf_tensor(f"rstd{i}", [128, 1], F32)) for i in range(2)]
        sqd = st_b.enter_context(nc.sbuf_tensor("sqd", [128, D], BF16))

        woT_ev = []
        for hdc in range(CC):
            wv_ = phA if hdc == 0 else ()
            woT_ev.append(sch.op("gpsimd",
                                 lambda g, hdc=hdc: g.dma_start(
                                     woT_b[:, hdc * D:(hdc + 1) * D],
                                     woT[hdc * 128:(hdc + 1) * 128, :]),
                                 waits=wv_, incs=[(s_w, 16)]))

        resid_ev = {}
        cen_free = {}
        x2w_ev = {}
        x2bf_free = {}
        tposed_ev = {}
        ln_free = {}

        for tt in range(NTT):
            pi = tt % 2
            xr_ev = sch.op("scalar",
                           lambda sc, tt=tt, pi=pi: sc.dma_start(
                               xres[pi][:, :], xq[tt * 128:(tt + 1) * 128, :]),
                           waits=[resid_ev.get(tt - 2)] + (phA if tt < 2 else []),
                           incs=[(s_dq, 16)])
            g_evs = []
            slots_tt = [slots8[(tt * 4 + dmt) % 8] for dmt in range(4)]
            for dmt in range(4):
                key, bank, boff = slots_tt[dmt]
                waits = [wfree(key)]
                if tt == 0:
                    waits += phA

                def fn(t, dmt=dmt, tt=tt, bank=bank, boff=boff):
                    inst = None
                    for hdc in range(CC):
                        if tt == 0 and dmt == 0:
                            t.wait_ge(woT_ev[hdc][0].h, woT_ev[hdc][1])
                        inst = t.matmul(bank[:, boff:boff + 512],
                                        attnT[:, hdc * NQ + tt * 128:hdc * NQ + (tt + 1) * 128],
                                        woT_b[:, hdc * D + dmt * 512:hdc * D + (dmt + 1) * 512],
                                        start=(hdc == 0), stop=(hdc == CC - 1))
                    return inst

                g_evs.append(sch.op("tensor", fn, waits=waits, incs=[(s_pe, 1)]))
            r_evs = []
            for dmt in range(4):
                key, bank, boff = slots_tt[dmt]
                r_evs.append(sch.op("vector",
                                    lambda v, dmt=dmt, pi=pi, bank=bank, boff=boff: v.tensor_tensor(
                                        cen[pi][:, dmt * 512:(dmt + 1) * 512],
                                        bank[:, boff:boff + 512],
                                        xres[pi][:, dmt * 512:(dmt + 1) * 512], ALU.add),
                                    waits=[g_evs[dmt], xr_ev, cen_free.get(tt - 2)],
                                    incs=[(s_ve, 1)]))
                bank_free[key] = r_evs[-1]
            resid_ev[tt] = r_evs[-1]

            l1 = sch.op("vector",
                        lambda v, pi=pi: v.reduce_sum(nmean[pi][:, :], cen[pi][:, :], axis=AX.X, negate=True),
                        waits=r_evs + [ln_free.get(pi)], incs=[(s_ve, 1)])
            l2 = sch.op("vector",
                        lambda v, pi=pi: v.tensor_scalar(nmean[pi][:, :], nmean[pi][:, :], 1.0 / D, None, ALU.mult),
                        waits=[l1], incs=[(s_ve, 1)])
            l3 = sch.op("vector",
                        lambda v, pi=pi: v.tensor_scalar(cen[pi][:, :], cen[pi][:, :],
                                                         nmean[pi][:, :], None, ALU.add),
                        waits=[l2], incs=[(s_ve, 1)])
            l4 = sch.op("scalar",
                        lambda sc, pi=pi: sc.activation(sqd[:, :], cen[pi][:, :], AF.Square,
                                                        accum_out=sumsq[pi][:, :]),
                        waits=[l3], incs=[(s_sc, 1)])
            l5 = sch.op("scalar",
                        lambda sc, pi=pi: sc.activation(sumsq[pi][:, :], sumsq[pi][:, :], AF.Ln,
                                                 bias=eps_sb[:, 0:1], scale=1.0 / D),
                        waits=[l4], incs=[(s_sc, 1)])
            l6 = sch.op("scalar",
                        lambda sc, pi=pi: sc.activation(rstd[pi][:, :], sumsq[pi][:, :], AF.Exp, scale=-0.5),
                        waits=[l5], incs=[(s_sc, 1)])
            l7 = sch.op("vector",
                        lambda v, pi=pi: v.tensor_scalar(cen[pi][:, :], cen[pi][:, :],
                                                         rstd[pi][:, :], None, ALU.mult),
                        waits=[l6], incs=[(s_ve, 1)])
            l8 = sch.op("vector",
                        lambda v, pi=pi: v.tensor_copy(x2bf[pi][:, :], cen[pi][:, :]),
                        waits=[l7, x2bf_free.get(tt - 2), x2bf_free.get((tt - 2, "q"))],
                        incs=[(s_ve, 1)])
            ln_free[pi] = l7

            w1 = sch.op("scalar",
                        lambda sc, tt=tt, pi=pi: sc.dma_start(
                            x2_scr[tt * 128:(tt + 1) * 128, :], cen[pi][:, :]),
                        waits=[l7], incs=[(s_dq, 16)])
            x2w_ev[tt] = w1
            cen_free[tt] = w1
            if tt >= 8:
                w2 = sch.op("scalar",
                            lambda sc, tt=tt, pi=pi: sc.dma_start(
                                x2bf_scr[tt * 128:(tt + 1) * 128, :], x2bf[pi][:, :]),
                            waits=[l8], incs=[(s_dq, 16)])
                x2bf_free[tt] = w2
            else:
                tp = None
                for c in range(CC):
                    tp = sch.op("sync",
                                lambda s2, c=c, tt=tt, pi=pi: s2.dma_start(
                                    X2T0[:, c * 1024 + tt * 128:c * 1024 + (tt + 1) * 128],
                                    x2bf[pi][:, c * 128:(c + 1) * 128], transpose=True),
                                waits=[l8] if c == 0 else (), incs=[(s_sy, 16)])
                tposed_ev[tt] = tp
                x2bf_free[tt] = tp

            if dbg_x2:
                sch.op("gpsimd",
                       lambda g, tt=tt, pi=pi: g.dma_start(
                           dbgX[tt * 128:(tt + 1) * 128, :], cen[pi][:, :]),
                       waits=[l7], incs=[(s_out, 16)])

        phB = [(s_pe, s_pe.n), (s_ve, s_ve.n), (s_sc, s_sc.n), (s_dq, s_dq.n), (s_sy, s_sy.n)]

        st_b.close()
        st_ab.close()
        st_c = ExitStack()

        def sbc(name, shape, dt):
            return st_c.enter_context(nc.sbuf_tensor(name, shape, dt))

        acc = sbc("acc", [128, 8 * D], F32)
        X2T1 = sbc("X2T1", [128, CC * 1024], BF16)
        We_b = [sbc(f"Web{i}", [128, CC * 512], BF16) for i in range(2)]
        tmp = [sbc(f"tmp{i}", [128, 512], F32) for i in range(2)]
        yb = [sbc(f"yb{i}", [128, D], F32) for i in range(2)]
        gwb = sbc("gwb", [128, CC * NE], BF16)
        gbb = sbc("gbb", [1, NE], BF16)
        ebb = [sbc(f"ebb{i}", [1, D], BF16) for i in range(2)]
        zg = sbc("zg", [128, NE], F32)
        zz = sbc("zz", [128, NE], F32)
        m1 = sbc("m1", [128, 1], F32)
        m2 = sbc("m2", [128, 1], F32)
        mask = [sbc(f"mask{i}", [128, 8 * NE], F32) for i in range(2)]
        nmean2 = sbc("nmean2", [128, 1], F32)
        sumsq2 = sbc("sumsq2", [128, 1], F32)
        rstd2 = sbc("rstd2", [128, 1], F32)
        sqd2 = sbc("sqd2", [128, D], BF16)

        phBf = [phB[0], phB[1], phB[2], phB[3]]
        gw_ev = sch.op("gpsimd", lambda g: g.dma_start(gwb[:, :], gwt), waits=phBf, incs=[(s_w, 16)])
        gb_ev = sch.op("gpsimd", lambda g: g.dma_start(gbb[:, :], gb), incs=[(s_w, 16)])

        grp_c = [0]
        acc_w = {}
        tmp_free = {}
        We_free = {}
        ebb_free = {}
        yb_free = {}
        acc_init_ev = {}
        ln2_chain = [None]
        mask_ev = {}
        halfC_pe_done = [None]
        tpose_c = {}
        x2bf_ld = {}
        last_y = []

        for ti in range(8):
            tt = 8 + ti
            pi = tt % 2
            ld = sch.op("scalar",
                        lambda sc, tt=tt, pi=pi: sc.dma_start(
                            x2bf[pi][:, :], x2bf_scr[tt * 128:(tt + 1) * 128, :]),
                        waits=[x2bf_free.get(tt - 2), x2bf_free.get((tt - 2, "q")),
                               x2w_ev.get(tt)] + (phBf if ti == 0 else []),
                        incs=[(s_dq, 16)])
            tp = None
            for c in range(CC):
                tp = sch.op("sync",
                            lambda s2, c=c, ti=ti, pi=pi: s2.dma_start(
                                X2T1[:, c * 1024 + ti * 128:c * 1024 + (ti + 1) * 128],
                                x2bf[pi][:, c * 128:(c + 1) * 128], transpose=True),
                            waits=[ld] + (phBf if ti == 0 and c == 0 else []),
                            incs=[(s_sy, 16)])
            tpose_c[tt] = tp
            x2bf_free[tt] = tp

        for H in range(2):
            X2Tc = X2T0 if H == 0 else X2T1
            for ti in range(8):
                tt = H * 8 + ti
                wv_ = [x2w_ev[tt], acc_w.get(("ln2", ti))]
                if H == 0 and ti == 0:
                    wv_ += phB
                acc_init_ev[ti] = sch.op("scalar",
                                         lambda sc, tt=tt, ti=ti: sc.dma_start(
                                             acc[:, ti * D:(ti + 1) * D],
                                             x2_scr[tt * 128:(tt + 1) * 128, :]),
                                         waits=wv_, incs=[(s_dq, 16)])

            for ti in range(8):
                tt = H * 8 + ti
                key, bank, boff = slots8[grp_c[0] % 8]
                grp_c[0] += 1
                wv_ = [gw_ev, gb_ev, wfree(key)]
                if H == 0:
                    wv_ += [tposed_ev.get(tt), tposed_ev.get((tt, "q"))]
                else:
                    wv_.append(tpose_c.get(tt))

                def fng(t, ti=ti, bank=bank, boff=boff, X2Tc=X2Tc):
                    t.matmul(bank[:, boff:boff + NE], onesbf_sb[:, :], gbb[:, :], start=True, stop=False)
                    inst = None
                    for c in range(CC):
                        inst = t.matmul(bank[:, boff:boff + NE],
                                        X2Tc[:, c * 1024 + ti * 128:c * 1024 + (ti + 1) * 128],
                                        gwb[:, c * NE:(c + 1) * NE],
                                        start=False, stop=(c == CC - 1))
                    return inst

                gm = sch.op("tensor", fng, waits=wv_, incs=[(s_pe, 1)])
                zev = sch.op("scalar", lambda sc, bank=bank, boff=boff: sc.copy(
                                 zg[:, :], bank[:, boff:boff + NE]),
                             waits=[gm, mask_ev.get(H * 8 + ti - 1)], incs=[(s_sc, 1)])
                bank_free[key] = zev
                c1 = sch.op("vector", lambda v: v.reduce_max(m1[:, :], zg[:, :], axis=AX.X),
                            waits=[zev], incs=[(s_ve, 1)])
                c2 = sch.op("vector",
                            lambda v: v.tensor_scalar(zz[:, :], zg[:, :], m1[:, :], None, ALU.is_equal),
                            waits=[c1], incs=[(s_ve, 1)])
                c3 = sch.op("vector",
                            lambda v: v.scalar_tensor_tensor(zz[:, :], zz[:, :], -1e30, zg[:, :],
                                                             ALU.mult, ALU.add),
                            waits=[c2], incs=[(s_ve, 1)])
                c4 = sch.op("vector", lambda v: v.reduce_max(m2[:, :], zz[:, :], axis=AX.X),
                            waits=[c3], incs=[(s_ve, 1)])
                c5 = sch.op("vector",
                            lambda v, ti=ti, H=H: v.tensor_scalar(mask[H][:, ti * NE:(ti + 1) * NE],
                                                                  zg[:, :], m2[:, :], 0.5,
                                                                  ALU.is_ge, ALU.mult),
                            waits=[c4], incs=[(s_ve, 1)])
                mask_ev[H * 8 + ti] = c5

            for e in range(NE):
                wpi = e % 2
                eb_ev = sch.op("gpsimd",
                               lambda g, e=e, wpi=wpi: g.dma_start(
                                   ebb[wpi][:, :], eb[0:1, e * D:(e + 1) * D]),
                               waits=[ebb_free.get(wpi)], incs=[(s_w, 16)])
                for oq in range(4):
                    bpi = (e * 4 + oq) % 2
                    src = ew[e].rearrange("(c p) o -> p c o", p=128)[:, :, oq * 512:(oq + 1) * 512]
                    d3 = We_b[bpi][:, :].rearrange("p (c n) -> p c n", c=CC)
                    we_ev = None
                    for half in range(2):
                        cs = slice(half * 8, (half + 1) * 8)
                        we_ev = sch.op("gpsimd",
                                       lambda g, d=d3[:, cs, :], s2=src[:, cs, :]: g.dma_start(d, s2),
                                       waits=[We_free.get(bpi)],
                                       incs=[(s_w, 16)])
                    for ti in range(8):
                        key, bank, boff = slots8[grp_c[0] % 8]
                        grp_c[0] += 1
                        wv_ = [we_ev, eb_ev, wfree(key)]

                        def fne(t, ti=ti, bank=bank, boff=boff, e=e, oq=oq, wpi=wpi, bpi=bpi, X2Tc=X2Tc):
                            t.matmul(bank[:, boff:boff + 512], onesbf_sb[:, :],
                                     ebb[wpi][:, oq * 512:(oq + 1) * 512], start=True, stop=False)
                            inst = None
                            for c in range(CC):
                                inst = t.matmul(bank[:, boff:boff + 512],
                                                X2Tc[:, c * 1024 + ti * 128:c * 1024 + (ti + 1) * 128],
                                                We_b[bpi][:, c * 512:(c + 1) * 512],
                                                start=False, stop=(c == CC - 1))
                            return inst

                        hm = sch.op("tensor", fne, waits=wv_, incs=[(s_pe, 1)])
                        if ti == 7:
                            We_free[bpi] = hm
                            if oq == 3:
                                ebb_free[wpi] = hm
                        tpi = grp_c[0] % 2
                        rl = sch.op("scalar",
                                    lambda sc, bank=bank, boff=boff, ti=ti, e=e, tpi=tpi, H=H: sc.activation(
                                        tmp[tpi][:, :], bank[:, boff:boff + 512], AF.Relu,
                                        scale=mask[H][:, ti * NE + e:ti * NE + e + 1]),
                                    waits=[hm, mask_ev[H * 8 + ti], tmp_free.get(tpi)],
                                    incs=[(s_sc, 1)])
                        bank_free[key] = rl
                        ac = sch.op("vector",
                                    lambda v, ti=ti, oq=oq, tpi=tpi: v.tensor_tensor(
                                        acc[:, ti * D + oq * 512:ti * D + (oq + 1) * 512],
                                        acc[:, ti * D + oq * 512:ti * D + (oq + 1) * 512],
                                        tmp[tpi][:, :], ALU.add),
                                    waits=[rl, acc_w.get((ti, oq), acc_init_ev[ti])],
                                    incs=[(s_ve, 1)])
                        acc_w[(ti, oq)] = ac
                        tmp_free[tpi] = ac
            halfC_pe_done[0] = s_pe.n

            for ti in range(8):
                tt = H * 8 + ti
                pi = ti % 2
                aw = [acc_w[(ti, oq)] for oq in range(4)]
                l1 = sch.op("vector",
                            lambda v, ti=ti: v.reduce_sum(nmean2[:, :], acc[:, ti * D:(ti + 1) * D],
                                                          axis=AX.X, negate=True),
                            waits=aw + [ln2_chain[0]], incs=[(s_ve, 1)])
                l2 = sch.op("vector",
                            lambda v: v.tensor_scalar(nmean2[:, :], nmean2[:, :], 1.0 / D, None, ALU.mult),
                            waits=[l1], incs=[(s_ve, 1)])
                l3 = sch.op("vector",
                            lambda v, ti=ti: v.tensor_scalar(acc[:, ti * D:(ti + 1) * D],
                                                             acc[:, ti * D:(ti + 1) * D],
                                                             nmean2[:, :], None, ALU.add),
                            waits=[l2], incs=[(s_ve, 1)])
                l4 = sch.op("scalar",
                            lambda sc, ti=ti: sc.activation(sqd2[:, :], acc[:, ti * D:(ti + 1) * D],
                                                            AF.Square, accum_out=sumsq2[:, :]),
                            waits=[l3], incs=[(s_sc, 1)])
                l5 = sch.op("scalar",
                            lambda sc: sc.activation(sumsq2[:, :], sumsq2[:, :], AF.Ln,
                                                     bias=eps_sb[:, 0:1], scale=1.0 / D),
                            waits=[l4], incs=[(s_sc, 1)])
                l6 = sch.op("scalar",
                            lambda sc: sc.activation(rstd2[:, :], sumsq2[:, :], AF.Exp, scale=-0.5),
                            waits=[l5], incs=[(s_sc, 1)])
                l7 = sch.op("vector",
                            lambda v, ti=ti, pi=pi: v.tensor_scalar(
                                yb[pi][:, :], acc[:, ti * D:(ti + 1) * D],
                                rstd2[:, :], None, ALU.mult),
                            waits=[l6, yb_free.get(pi)], incs=[(s_ve, 1)])
                ln2_chain[0] = l7
                acc_w[("ln2", ti)] = l7
                yw = sch.op("scalar",
                            lambda sc, tt=tt, pi=pi: sc.dma_start(
                                y[tt * 128:(tt + 1) * 128, :], yb[pi][:, :]),
                            waits=[l7], incs=[(s_out, 16)])
                yb_free[pi] = yw
                last_y.append(yw)

        sch.op("sync", lambda s2: s2.nop(), waits=[(s_out, s_out.n)])

        sch.emit(blk)
        st_c.close()
    return nc


_NC_CACHE = {}


def _get_nc():
    if "nc" not in _NC_CACHE:
        _NC_CACHE["nc"] = build_program()
    return _NC_CACHE["nc"]


def kernel(L0_attn_hidden_states, L0_attn_Wq, L0_attn_Wk, L0_attn_Wv, L0_attn_Wo,
           L0_ffn_gate_w, L0_ffn_gate_b, L0_ffn_expert_w, L0_ffn_expert_b):
    x = np.ascontiguousarray(np.asarray(L0_attn_hidden_states, dtype=np.float32))
    Wq = np.asarray(L0_attn_Wq, dtype=np.float32)
    Wk = np.asarray(L0_attn_Wk, dtype=np.float32)
    Wv = np.asarray(L0_attn_Wv, dtype=np.float32)
    Wo = np.asarray(L0_attn_Wo, dtype=np.float32)
    gate_w = np.asarray(L0_ffn_gate_w, dtype=np.float32)
    gate_b = np.asarray(L0_ffn_gate_b, dtype=np.float32)
    expert_w = np.ascontiguousarray(np.asarray(L0_ffn_expert_w, dtype=np.float32))
    expert_b = np.asarray(L0_ffn_expert_b, dtype=np.float32)

    wqT = np.ascontiguousarray(Wq.T)
    wkT = np.ascontiguousarray(Wk.T)
    wvT = np.ascontiguousarray(Wv.T)
    woT = np.ascontiguousarray(Wo.T)
    gwt = np.ascontiguousarray(gate_w.reshape(CC, 128, NE).transpose(1, 0, 2).reshape(128, CC * NE))
    gb1 = np.ascontiguousarray(gate_b.reshape(1, NE))
    eb1 = np.ascontiguousarray(expert_b.reshape(1, NE * D))
    ones_bf = np.ones((1, 128), dtype=ml_dtypes.bfloat16)
    onescol_bf = np.ones((128, 1), dtype=ml_dtypes.bfloat16)
    onesmat_bf1 = np.ones((128, 128), dtype=ml_dtypes.bfloat16)
    ones_f = np.ones((1, 128), dtype=np.float32)
    epscol = np.full((128, 1), EPS, dtype=np.float32)

    nc = _get_nc()
    in_maps = []
    for core in range(8):
        b, hf = core // 2, core % 2
        own = x[b, hf * NQ:(hf + 1) * NQ]
        oth = x[b, (1 - hf) * NQ:(2 - hf) * NQ]
        xroll = np.concatenate([own, oth], axis=0)
        in_maps.append({
            "xT": np.ascontiguousarray(xroll.T),
            "xq": np.ascontiguousarray(own),
            "wqT": wqT, "wkT": wkT, "wvT": wvT, "woT": woT,
            "gwt": gwt, "gb": gb1, "ew": expert_w, "eb": eb1,
            "ones_bf": ones_bf, "onescol_bf": onescol_bf, "onesmat_bf": onesmat_bf1,
            "ones_f": ones_f, "epscol": epscol,
        })

    res = run_bass_kernel_spmd(nc, in_maps, core_ids=list(range(8)))
    out = np.empty((4, S, D), dtype=np.float32)
    for core in range(8):
        b, hf = core // 2, core % 2
        out[b, hf * NQ:(hf + 1) * NQ] = res.results[core]["y"]
    return out


if __name__ == "__main__":
    build_program()
    print("build ok")


# version 14
# speedup vs baseline: 1.1673x; 1.0417x over previous
import numpy as np
import ml_dtypes
from contextlib import ExitStack

import concourse.bass as bass
import concourse.mybir as mybir
from concourse.bass_utils import run_bass_kernel_spmd

F32 = mybir.dt.float32
BF16 = mybir.dt.bfloat16
AF = mybir.ActivationFunctionType
ALU = mybir.AluOpType
AX = mybir.AxisListType

D = 2048
S = 4096
NQ = 2048
NH = 16
HD = 128
G = 2
CC = D // 128
KC = S // 128
QTW = 512
NQT = NQ // QTW
STRIP = 512
NSTRIP = S // STRIP
NE = 8
SCALE = float(HD) ** -0.5
EPS = 1e-5
NTT = NQ // 128

ENGINES = ("tensor", "vector", "scalar", "sync", "gpsimd")


class Sem:
    def __init__(self, ctx, nc, name):
        self.h = ctx.enter_context(nc.semaphore(name))
        self.n = 0
        self.name = name


class Sched:
    def __init__(self):
        self.q = {e: [] for e in ENGINES}

    def op(self, eng, fn, waits=(), incs=()):
        ev = None
        inc_list = []
        for sem, amt in incs:
            sem.n += amt
            inc_list.append((sem, amt))
            if ev is None:
                ev = (sem, sem.n)
        self.q[eng].append((tuple(w for w in waits if w is not None), fn, tuple(inc_list)))
        return ev

    def emit(self, blk):
        for eng in ENGINES:
            items = self.q[eng]
            if not items:
                continue

            def body(engine, items=items):
                for waits, fn, incs in items:
                    for sem, val in waits:
                        engine.wait_ge(sem.h, val)
                    inst = fn(engine)
                    for sem, amt in incs:
                        inst.then_inc(sem.h, amt)

            getattr(blk, eng)(body)


def build_program(npass=8, do_bc=True, dbg_attn=False, dbg_x2=False, dbg_proj=False,
                  dbg_att1=False):
    nc = bass.Bass("TRN2", target_bir_lowering=False)

    xT = nc.dram_tensor("xT", [D, S], F32, kind="ExternalInput").ap()
    xq = nc.dram_tensor("xq", [NQ, D], F32, kind="ExternalInput").ap()
    wqT = nc.dram_tensor("wqT", [D, D], F32, kind="ExternalInput").ap()
    wkT = nc.dram_tensor("wkT", [D, D], F32, kind="ExternalInput").ap()
    wvT = nc.dram_tensor("wvT", [D, D], F32, kind="ExternalInput").ap()
    woT = nc.dram_tensor("woT", [D, D], F32, kind="ExternalInput").ap()
    gwt = nc.dram_tensor("gwt", [128, CC * NE], F32, kind="ExternalInput").ap()
    gb = nc.dram_tensor("gb", [1, NE], F32, kind="ExternalInput").ap()
    ew = nc.dram_tensor("ew", [NE, D, D], F32, kind="ExternalInput").ap()
    eb = nc.dram_tensor("eb", [1, NE * D], F32, kind="ExternalInput").ap()
    ones_bf = nc.dram_tensor("ones_bf", [1, 128], BF16, kind="ExternalInput").ap()
    onescol_bf = nc.dram_tensor("onescol_bf", [128, 1], BF16, kind="ExternalInput").ap()
    onesmat_bf = nc.dram_tensor("onesmat_bf", [128, 128], BF16, kind="ExternalInput").ap()
    ident_bf = nc.dram_tensor("ident_bf", [128, 128], BF16, kind="ExternalInput").ap()
    ones_f = nc.dram_tensor("ones_f", [1, 128], F32, kind="ExternalInput").ap()
    epscol = nc.dram_tensor("epscol", [128, 1], F32, kind="ExternalInput").ap()

    y = nc.dram_tensor("y", [NQ, D], F32, kind="ExternalOutput").ap()
    dbgA = nc.dram_tensor("dbgA", [128, NH * NQ], F32, kind="ExternalOutput").ap() if dbg_attn else None
    dbgX = nc.dram_tensor("dbgX", [NQ, D], F32, kind="ExternalOutput").ap() if dbg_x2 else None
    if dbg_proj:
        dbgK = nc.dram_tensor("dbgK", [128, G * S], F32, kind="ExternalOutput").ap()
        dbgQ = nc.dram_tensor("dbgQ", [128, G * NQ], F32, kind="ExternalOutput").ap()
        dbgV = nc.dram_tensor("dbgV", [128, KC * G * HD], F32, kind="ExternalOutput").ap()
        dbgN = nc.dram_tensor("dbgN", [128, 16], F32, kind="ExternalOutput").ap()

    if dbg_att1:
        dbgD = nc.dram_tensor("dbgD", [1, 512], F32, kind="ExternalOutput").ap()
        dbgR = nc.dram_tensor("dbgR", [128, 512], F32, kind="ExternalOutput").ap()
        dbgE = nc.dram_tensor("dbgE", [128, 512], F32, kind="ExternalOutput").ap()
        dbgE0 = nc.dram_tensor("dbgE0", [128, 512], F32, kind="ExternalOutput").ap()
        dbgE1 = nc.dram_tensor("dbgE1", [128, 512], F32, kind="ExternalOutput").ap()
        dbgU = nc.dram_tensor("dbgU", [128, 512], F32, kind="ExternalOutput").ap()

    x2_scr = nc.dram_tensor("x2_scr", [NQ, D], F32, kind="Internal").ap()
    x2bf_scr = nc.dram_tensor("x2bf_scr", [NQ, D], BF16, kind="Internal").ap()

    ctx = ExitStack()

    onesbf_sb = ctx.enter_context(nc.sbuf_tensor("onesbf_sb", [1, 128], BF16, side="right"))
    onescol_sb = ctx.enter_context(nc.sbuf_tensor("onescol_sb", [128, 1], BF16, side="right"))
    onesmat_sb = ctx.enter_context(nc.sbuf_tensor("onesmat_sb", [128, 128], BF16, side="right"))
    ident_sb = ctx.enter_context(nc.sbuf_tensor("ident_sb", [128, 128], BF16, side="right"))
    onesf_sb = ctx.enter_context(nc.sbuf_tensor("onesf_sb", [1, 128], F32, side="right"))
    eps_sb = ctx.enter_context(nc.sbuf_tensor("eps_sb", [128, 1], F32, side="right"))

    psc = [ctx.enter_context(nc.psum_tensor(f"psc{i}", [128, 1024], F32)) for i in range(2)]
    pab = [ctx.enter_context(nc.psum_tensor(f"pa{i}", [128, 512], F32)) for i in range(2)]
    pdb = [ctx.enter_context(nc.psum_tensor(f"pd{i}", [128, 512], F32)) for i in range(2)]
    slots8 = [("a0", pab[0], 0), ("a1", pab[1], 0), ("d0", pdb[0], 0), ("d1", pdb[1], 0),
              ("s0L", psc[0], 0), ("s0R", psc[0], 512), ("s1L", psc[1], 0), ("s1R", psc[1], 512)]

    s_xt = Sem(ctx, nc, "s_xt")
    s_w = Sem(ctx, nc, "s_w")
    s_pe = Sem(ctx, nc, "s_pe")
    s_sc = Sem(ctx, nc, "s_sc")
    s_ve = Sem(ctx, nc, "s_ve")
    s_dq = Sem(ctx, nc, "s_dq")
    s_sy = Sem(ctx, nc, "s_sy")
    s_out = Sem(ctx, nc, "s_out")

    sch = Sched()
    blk = ctx.enter_context(nc.Block())

    bank_free = {}

    def wfree(bid):
        return bank_free.get(bid)

    with ctx:
        st_ab = ExitStack()
        attnT = st_ab.enter_context(nc.sbuf_tensor("attnT", [128, NH * NQ], BF16))
        st_a = ExitStack()

        def sba(name, shape, dt):
            return st_a.enter_context(nc.sbuf_tensor(name, shape, dt))

        st_a1 = ExitStack()
        wq_b = [st_a1.enter_context(nc.sbuf_tensor(f"wqb{i}", [128, CC * G * HD], BF16, side="right")) for i in range(2)]
        wk_b = [st_a1.enter_context(nc.sbuf_tensor(f"wkb{i}", [128, CC * G * HD], BF16, side="right")) for i in range(2)]
        wv_b = [st_a1.enter_context(nc.sbuf_tensor(f"wvb{i}", [128, CC * G * HD], BF16, side="right")) for i in range(2)]
        xt_b = [st_a1.enter_context(nc.sbuf_tensor(f"xtb{i}", [128, CC * STRIP], BF16, side="right")) for i in range(2)]
        KT = sba("KT", [128, G * S], BF16)
        QTm = sba("QTm", [128, G * NQ], BF16)
        Vm = sba("Vm", [128, KC * G * HD], BF16)
        nsc = sba("nsc", [128, 2 * 512], BF16)
        runmx = sba("runmx", [1, 8], F32)
        sc11 = sba("sc11", [1, 8], F32)
        negc = sba("negc", [1, G], F32)
        negc_col = sba("negc_col", [128, G], F32)
        expSp = [sba(f"expSp{i}", [128, 1024], BF16) for i in range(2)]
        dnln = [sba(f"dnln{i}", [128, 512], F32) for i in range(2)]
        recip_rep = [sba(f"recip_rep{i}", [128, 512], F32) for i in range(2)]

        for t_sb, t_dr in ((onesbf_sb, ones_bf), (onescol_sb, onescol_bf),
                           (onesmat_sb, onesmat_bf), (ident_sb, ident_bf),
                           (onesf_sb, ones_f), (eps_sb, epscol)):
            sch.op("gpsimd", lambda g, a=t_sb, b=t_dr: g.dma_start(a[:, :], b),
                   incs=[(s_w, 16)])

        def w_slice_dma(dst, srcT, p, waits=()):
            src = srcT.rearrange("(c p) d -> p c d", p=128)[:, :, p * G * HD:(p + 1) * G * HD]
            d3 = dst[:, :].rearrange("p (c n) -> p c n", c=CC)
            ev = None
            for half in range(2):
                cs = slice(half * 8, (half + 1) * 8)
                ev = sch.op("gpsimd",
                            lambda g, d=d3[:, cs, :], s2=src[:, cs, :]: g.dma_start(d, s2),
                            waits=waits, incs=[(s_w, 16)])
            return ev

        w_slice_dma(wq_b[0], wqT, 0)
        w_slice_dma(wk_b[0], wkT, 0)
        w_ready = [w_slice_dma(wv_b[0], wvT, 0)]

        xT_r = xT.rearrange("(c p) t -> p c t", p=128)

        strip_ev = {}
        strip_done = {}
        pass_proj_last = {}
        kq_evac_last = [None]
        attn_ev = []
        exp_free = {}
        expSp_free = {}
        gpair = [0]
        hqt_ctr = [0]
        dnln_free = {}
        rr_free = {}
        d_ev_last = [None]

        for p in range(npass):
            wq, wk, wv = wq_b[p % 2], wk_b[p % 2], wv_b[p % 2]
            grp_ctr = [0]
            norm_ctr = [0]
            nsc_free = {}

            mx_init_ev = sch.op("vector", lambda v: v.memset(runmx[:, 0:4], -1e30),
                                waits=[attn_ev[-1]] if attn_ev else (), incs=[(s_ve, 1)])
            norm_last = {0: mx_init_ev, 1: mx_init_ev, 2: mx_init_ev, 3: mx_init_ev}

            norm_q = []

            def flush_norms():
                while norm_q:
                    sq_ev, col = norm_q.pop(0)
                    emit_norm_mm(sq_ev, col)

            def proj_group(p, s, lhs_fn, rhs_fn, n_out, dst, doff, evac_eng):
                key, bank, boff = slots8[grp_ctr[0] % 8]
                grp_ctr[0] += 1
                waits = [strip_ev[(p, s)], wfree(key), w_ready[p]]

                def fn(t, bank=bank, boff=boff, lhs_fn=lhs_fn, rhs_fn=rhs_fn, n_out=n_out):
                    inst = None
                    for c in range(CC):
                        inst = t.matmul(bank[:, boff:boff + n_out], lhs_fn(c), rhs_fn(c),
                                        start=(c == 0), stop=(c == CC - 1))
                    return inst

                mm_ev = sch.op("tensor", fn, waits=waits, incs=[(s_pe, 1)])
                strip_done[(p, s)] = s_pe.n
                sem = s_ve if evac_eng == "vector" else s_sc

                def ev_fn(e, dst=dst, doff=doff, n_out=n_out, bank=bank, boff=boff):
                    if evac_eng == "vector":
                        return e.tensor_copy(dst[:, doff:doff + n_out], bank[:, boff:boff + n_out])
                    return e.copy(dst[:, doff:doff + n_out], bank[:, boff:boff + n_out])

                evac_ev = sch.op(evac_eng, ev_fn, waits=[mm_ev], incs=[(sem, 1)])
                bank_free[key] = evac_ev
                flush_norms()
                return evac_ev

            def emit_norm_mm(sq_ev, col):
                ping = sq_ev[2]
                mm_ev = sch.op("tensor",
                               lambda t, ping=ping: t.matmul(
                                   pdb[0][0:1, :], onescol_sb[:, :],
                                   nsc[:, ping * 512:(ping + 1) * 512],
                                   start=True, stop=True),
                               waits=[sq_ev[:2], wfree("d0")], incs=[(s_pe, 1)])
                nsc_free[ping] = mm_ev
                r_ev = sch.op("vector",
                              lambda v: v.reduce_max(sc11[:, 7:8], pdb[0][0:1, :], axis=AX.X),
                              waits=[mm_ev, norm_last[col]], incs=[(s_ve, 1)])
                bank_free["d0"] = r_ev
                m_ev = sch.op("vector",
                              lambda v, col=col: v.tensor_tensor(
                                  runmx[:, col:col + 1], runmx[:, col:col + 1],
                                  sc11[:, 7:8], ALU.max),
                              waits=[r_ev], incs=[(s_ve, 1)])
                norm_last[col] = m_ev

            def sq_and_norm(p, src, soff, col, prior_ev):
                ping = norm_ctr[0] % 2
                norm_ctr[0] += 1
                waits = [prior_ev, nsc_free.get(ping)]
                sq_ev = sch.op("vector",
                               lambda v, src=src, soff=soff, ping=ping: v.tensor_tensor(
                                   nsc[:, ping * 512:(ping + 1) * 512],
                                   src[:, soff:soff + 512], src[:, soff:soff + 512],
                                   ALU.mult),
                               waits=waits, incs=[(s_ve, 1)])
                norm_q.append((sq_ev + (ping,), col))

            for s in range(NSTRIP):
                waits = []
                if (p, s) != (0, 0) and (p, s) != (0, 1):
                    pp, ss = (p, s - 2) if s >= 2 else (p - 1, NSTRIP + s - 2)
                    waits = [(s_pe, strip_done[(pp, ss)])]
                dst = xt_b[s % 2][:, :].rearrange("p (c n) -> p c n", c=CC)
                src = xT_r[:, :, s * STRIP:(s + 1) * STRIP]
                evx = None
                for half in range(2):
                    cs = slice(half * 8, (half + 1) * 8)
                    evx = sch.op("gpsimd",
                                 lambda g, d=dst[:, cs, :], sr=src[:, cs, :]: g.dma_start(d, sr),
                                 waits=waits, incs=[(s_xt, 16)])
                strip_ev[(p, s)] = evx
                xb = xt_b[s % 2]

                for dc in range(G):
                    e1 = proj_group(
                        p, s,
                        lambda c, dc=dc, wk=wk: wk[:, c * G * HD + dc * HD: c * G * HD + (dc + 1) * HD],
                        lambda c, xb=xb: xb[:, c * STRIP:(c + 1) * STRIP],
                        STRIP, KT, dc * S + s * STRIP, "vector")
                    kq_evac_last[0] = e1
                    sq_and_norm(p, KT, dc * S + s * STRIP, 2 + dc, e1)
                if s < NQ // STRIP:
                    for dc in range(G):
                        e1 = proj_group(
                            p, s,
                            lambda c, dc=dc, wq=wq: wq[:, c * G * HD + dc * HD: c * G * HD + (dc + 1) * HD],
                            lambda c, xb=xb: xb[:, c * STRIP:(c + 1) * STRIP],
                            STRIP, QTm, dc * NQ + s * STRIP, "vector")
                        kq_evac_last[0] = e1
                        sq_and_norm(p, QTm, dc * NQ + s * STRIP, dc, e1)
                for tt in range(STRIP // 128):
                    ktile = s * (STRIP // 128) + tt
                    proj_group(
                        p, s,
                        lambda c, xb=xb, tt=tt: xb[:, c * STRIP + tt * 128: c * STRIP + (tt + 1) * 128],
                        lambda c, wv=wv: wv[:, c * G * HD:(c + 1) * G * HD],
                        G * HD, Vm, ktile * G * HD, "scalar")

            flush_norms()
            pass_proj_last[p] = s_pe.n

            if p + 1 < npass:
                nxt = (p + 1) % 2
                w0 = [(s_pe, pass_proj_last[p - 1])] if p >= 1 else ()
                w_slice_dma(wq_b[nxt], wqT, p + 1, waits=w0)
                w_slice_dma(wk_b[nxt], wkT, p + 1)
                w_ready.append(w_slice_dma(wv_b[nxt], wvT, p + 1))

            neg_evs = []
            for h in range(G):
                e1 = sch.op("scalar",
                            lambda sc, h=h: sc.activation(sc11[:, 0:1], runmx[:, h:h + 1], AF.Ln),
                            waits=[norm_last[h], norm_last[2 + h]], incs=[(s_sc, 1)])
                e2 = sch.op("scalar",
                            lambda sc, h=h: sc.activation(sc11[:, 1:2], runmx[:, 2 + h:3 + h], AF.Ln),
                            waits=[e1], incs=[(s_sc, 1)])
                e3 = sch.op("vector",
                            lambda v: v.tensor_tensor(sc11[:, 2:3], sc11[:, 0:1], sc11[:, 1:2], ALU.add),
                            waits=[e2], incs=[(s_ve, 1)])
                e4 = sch.op("scalar",
                            lambda sc: sc.activation(sc11[:, 3:4], sc11[:, 2:3], AF.Exp, scale=0.5),
                            waits=[e3], incs=[(s_sc, 1)])
                e5 = sch.op("vector",
                            lambda v, h=h: v.tensor_scalar(
                                negc[:, h:h + 1], sc11[:, 3:4], -1.01 * SCALE, None, ALU.mult),
                            waits=[e4], incs=[(s_ve, 1)])
                neg_evs.append(e5)
            rep_mm = sch.op("tensor",
                            lambda t: t.matmul(pab[0][:, 0:G], onesf_sb[:, :], negc[:, 0:G],
                                               start=True, stop=True),
                            waits=neg_evs + [wfree("a0")], incs=[(s_pe, 1)])
            rep_cp = sch.op("scalar",
                            lambda sc: sc.copy(negc_col[:, :], pab[0][:, 0:G]),
                            waits=[rep_mm], incs=[(s_sc, 1)])
            bank_free["a0"] = rep_cp

            psc_first = {0: True, 1: True}
            for h in range(G):
                hd = p * G + h
                for qt in range(NQT):
                    qoff = qt * QTW
                    par = hqt_ctr[0] % 2
                    hqt_ctr[0] += 1
                    at_key, at_bank, _ = slots8[par]
                    dn_key, dn_bank, _ = slots8[2 + par]
                    pend = []

                    def emit_attn_pair(pr2, e_ev, base_kc):
                        a_ev = d_ev = None
                        for half in range(2):
                            kc = base_kc + half
                            aw = [e_ev]
                            if kc == 0:
                                aw.append(wfree(at_key))

                            def mm_a(t, kc=kc, h=h, pr2=pr2, half=half, at_bank=at_bank):
                                return t.matmul(
                                    at_bank[:, :],
                                    Vm[:, kc * G * HD + h * HD: kc * G * HD + (h + 1) * HD],
                                    expSp[pr2][:, half * 512:(half + 1) * 512],
                                    start=(kc == 0), stop=(kc == KC - 1))

                            a_ev = sch.op("tensor", mm_a, waits=aw, incs=[(s_pe, 1)])
                            dw = [wfree(dn_key)] if kc == 0 else []

                            def mm_d(t, kc=kc, pr2=pr2, half=half, dn_bank=dn_bank):
                                return t.matmul(
                                    dn_bank[:, :], onesmat_sb[:, :],
                                    expSp[pr2][:, half * 512:(half + 1) * 512],
                                    start=(kc == 0), stop=(kc == KC - 1))

                            d_ev = sch.op("tensor", mm_d, waits=dw, incs=[(s_pe, 1)])
                        expSp_free[pr2] = d_ev
                        return a_ev, d_ev

                    a_ev = d_ev = None
                    for pr in range(KC // 2):
                        pr2 = gpair[0] % 2
                        gpair[0] += 1
                        scb = psc[0] if pr2 == 0 else psc[1]
                        s2_ev = None
                        for half in range(2):
                            kc = 2 * pr + half
                            sw = []
                            if half == 0:
                                sw.append(exp_free.get(pr2))
                                if psc_first[pr2]:
                                    sw += [wfree("s0L" if pr2 == 0 else "s1L"),
                                           wfree("s0R" if pr2 == 0 else "s1R")]
                                    psc_first[pr2] = False
                                if kc == 0 and qt == 0 and h == 0:
                                    sw.append(kq_evac_last[0])

                            def mm_s(t, h=h, kc=kc, qoff=qoff, half=half, scb=scb):
                                return t.matmul(
                                    scb[:, half * 512:(half + 1) * 512],
                                    KT[:, h * S + kc * 128:h * S + (kc + 1) * 128],
                                    QTm[:, h * NQ + qoff:h * NQ + qoff + QTW],
                                    start=True, stop=True)

                            s2_ev = sch.op("tensor", mm_s, waits=sw, incs=[(s_pe, 1)])

                        ew_ = [s2_ev, expSp_free.get(pr2)]
                        if pr == 0 and qt == 0 and h == 0:
                            ew_.append(rep_cp)
                        e_ev = sch.op("scalar",
                                      lambda sc, pr2=pr2, h=h, scb=scb: sc.activation(
                                          expSp[pr2][:, :], scb[:, :], AF.Exp,
                                          bias=negc_col[:, h:h + 1], scale=SCALE),
                                      waits=ew_, incs=[(s_sc, 1)])
                        exp_free[pr2] = e_ev
                        pend.append((pr, pr2, e_ev))
                        if len(pend) >= 2:
                            pr0, pr20, e0 = pend.pop(0)
                            a_ev, d_ev = emit_attn_pair(pr20, e0, 2 * pr0)
                    while pend:
                        pr0, pr20, e0 = pend.pop(0)
                        a_ev, d_ev = emit_attn_pair(pr20, e0, 2 * pr0)

                    r1 = sch.op("vector",
                                lambda v, dn_bank=dn_bank, par=par: v.tensor_copy(
                                    dnln[par][:, :], dn_bank[:, :]),
                                waits=[d_ev, dnln_free.get(par)], incs=[(s_ve, 1)])
                    bank_free[dn_key] = r1
                    r2w = [r1, rr_free.get(par)]
                    r2 = sch.op("vector",
                                lambda v, par=par: v.reciprocal(recip_rep[par][:, :],
                                                                dnln[par][:, :]),
                                waits=r2w, incs=[(s_ve, 1)])
                    dnln_free[par] = r2
                    t_ev = sch.op("vector",
                                  lambda v, hd=hd, qoff=qoff, at_bank=at_bank, par=par:
                                  v.tensor_tensor(
                                      attnT[:, hd * NQ + qoff: hd * NQ + qoff + QTW],
                                      at_bank[:, :], recip_rep[par][:, :], ALU.mult),
                                  waits=[a_ev, r2], incs=[(s_ve, 1)])
                    attn_ev.append(t_ev)
                    bank_free[at_key] = t_ev
                    rr_free[par] = t_ev

            for i2 in range(2):
                if exp_free.get(i2) is not None:
                    bank_free["s0L" if i2 == 0 else "s1L"] = exp_free[i2]
                    bank_free["s0R" if i2 == 0 else "s1R"] = exp_free[i2]

        phA = [(s_pe, s_pe.n), (s_ve, s_ve.n), (s_sc, s_sc.n)]

        if dbg_proj:
            wv_ = [(s_pe, s_pe.n), (s_ve, s_ve.n), (s_sc, s_sc.n)]
            sch.op("gpsimd", lambda g: g.dma_start(dbgK, KT[:, :]), waits=wv_, incs=[(s_out, 16)])
            sch.op("gpsimd", lambda g: g.dma_start(dbgQ, QTm[:, :]), incs=[(s_out, 16)])
            sch.op("gpsimd", lambda g: g.dma_start(dbgV, Vm[:, :]), incs=[(s_out, 16)])
            sch.op("gpsimd", lambda g: g.dma_start(dbgN[:, 0:G], negc_col[:, :]), incs=[(s_out, 16)])
            sch.op("gpsimd", lambda g: g.dma_start(dbgN[0:1, 8:16], runmx[:, :]), incs=[(s_out, 16)])

        if dbg_attn:
            dmp = sch.op("gpsimd", lambda g: g.dma_start(dbgA, attnT[:, :]),
                         waits=[attn_ev[-1]], incs=[(s_out, 16)])
            sch.op("sync", lambda s2: s2.nop(), waits=[(s_out, s_out.n)])

        if not do_bc:
            sch.emit(blk)
            st_a.close()
            st_ab.close()
            return nc

        st_a1.close()
        fenceA1 = [(s_pe, pass_proj_last[npass - 1]), (s_xt, s_xt.n), (s_w, s_w.n)]
        X2T0 = ctx.enter_context(nc.sbuf_tensor("X2T0", [128, CC * 1024], BF16, side="right"))
        x2bf = [ctx.enter_context(nc.sbuf_tensor(f"x2bf{i}", [128, D], BF16, side="right"))
                for i in range(2)]
        st_a.close()
        st_b = ExitStack()
        woT_b = st_b.enter_context(nc.sbuf_tensor("woT_b", [128, CC * D], BF16, side="right"))
        xres = [st_b.enter_context(nc.sbuf_tensor(f"xres{i}", [128, D], F32)) for i in range(2)]
        cen = [st_b.enter_context(nc.sbuf_tensor(f"cen{i}", [128, D], F32)) for i in range(2)]
        nmean = [st_b.enter_context(nc.sbuf_tensor(f"nmean{i}", [128, 1], F32)) for i in range(2)]
        sumsq = [st_b.enter_context(nc.sbuf_tensor(f"sumsq{i}", [128, 1], F32)) for i in range(2)]
        rstd = [st_b.enter_context(nc.sbuf_tensor(f"rstd{i}", [128, 1], F32)) for i in range(2)]
        sqd = st_b.enter_context(nc.sbuf_tensor("sqd", [128, D], BF16))


        def tpose_group4(x2src, cbase, dstT, trow, waits_first, grp_ctr_ref):
            key, bank, boff = slots8[grp_ctr_ref[0] % 8]
            grp_ctr_ref[0] += 1
            bfv = bank.bitcast(BF16)

            def fn(t, bfv=bfv, boff=boff, x2src=x2src, cbase=cbase):
                inst = None
                for j in range(4):
                    inst = t.transpose(bfv[:, boff * 2 + j * 128:boff * 2 + (j + 1) * 128],
                                       x2src[:, (cbase + j) * 128:(cbase + j + 1) * 128],
                                       ident_sb[:, :])
                return inst

            mm_ev = sch.op("tensor", fn, waits=list(waits_first) + [wfree(key)],
                           incs=[(s_pe, 1)])

            def ev_fn(v, bfv=bfv, boff=boff, dstT=dstT, cbase=cbase, trow=trow):
                d3 = dstT[:, :].rearrange("p (c n) -> p c n", c=CC)[:, cbase:cbase + 4,
                                                                   trow * 128:(trow + 1) * 128]
                s3 = bfv[:, boff * 2:boff * 2 + 512].rearrange("p (c n) -> p c n", c=4)
                return v.tensor_copy(d3, s3)

            evac_ev = sch.op("vector", ev_fn, waits=[mm_ev], incs=[(s_ve, 1)])
            bank_free[key] = evac_ev
            return evac_ev

        woT_ev = []
        for hdc in range(CC):
            wv_ = phA if hdc == 0 else ()
            woT_ev.append(sch.op("gpsimd",
                                 lambda g, hdc=hdc: g.dma_start(
                                     woT_b[:, hdc * D:(hdc + 1) * D],
                                     woT[hdc * 128:(hdc + 1) * 128, :]),
                                 waits=wv_, incs=[(s_w, 16)]))

        grpB = [0]
        resid_ev = {}
        cen_free = {}
        x2w_ev = {}
        x2bf_free = {}
        tposed_ev = {}
        ln_free = {}

        for tt in range(NTT):
            pi = tt % 2
            xr_ev = sch.op("scalar",
                           lambda sc, tt=tt, pi=pi: sc.dma_start(
                               xres[pi][:, :], xq[tt * 128:(tt + 1) * 128, :]),
                           waits=[resid_ev.get(tt - 2)] + (phA if tt < 2 else []),
                           incs=[(s_dq, 16)])
            g_evs = []
            slots_tt = []
            for dmt in range(4):
                slots_tt.append(slots8[grpB[0] % 8])
                grpB[0] += 1
            for dmt in range(4):
                key, bank, boff = slots_tt[dmt]
                waits = [wfree(key)]
                if tt == 0:
                    waits += phA

                def fn(t, dmt=dmt, tt=tt, bank=bank, boff=boff):
                    inst = None
                    for hdc in range(CC):
                        if tt == 0 and dmt == 0:
                            t.wait_ge(woT_ev[hdc][0].h, woT_ev[hdc][1])
                        inst = t.matmul(bank[:, boff:boff + 512],
                                        attnT[:, hdc * NQ + tt * 128:hdc * NQ + (tt + 1) * 128],
                                        woT_b[:, hdc * D + dmt * 512:hdc * D + (dmt + 1) * 512],
                                        start=(hdc == 0), stop=(hdc == CC - 1))
                    return inst

                g_evs.append(sch.op("tensor", fn, waits=waits, incs=[(s_pe, 1)]))
            r_evs = []
            for dmt in range(4):
                key, bank, boff = slots_tt[dmt]
                r_evs.append(sch.op("vector",
                                    lambda v, dmt=dmt, pi=pi, bank=bank, boff=boff: v.tensor_tensor(
                                        cen[pi][:, dmt * 512:(dmt + 1) * 512],
                                        bank[:, boff:boff + 512],
                                        xres[pi][:, dmt * 512:(dmt + 1) * 512], ALU.add),
                                    waits=[g_evs[dmt], xr_ev, cen_free.get(tt - 2)],
                                    incs=[(s_ve, 1)]))
                bank_free[key] = r_evs[-1]
            resid_ev[tt] = r_evs[-1]

            l1 = sch.op("vector",
                        lambda v, pi=pi: v.reduce_sum(nmean[pi][:, :], cen[pi][:, :], axis=AX.X, negate=True),
                        waits=r_evs + [ln_free.get(pi)], incs=[(s_ve, 1)])
            l2 = sch.op("vector",
                        lambda v, pi=pi: v.tensor_scalar(nmean[pi][:, :], nmean[pi][:, :], 1.0 / D, None, ALU.mult),
                        waits=[l1], incs=[(s_ve, 1)])
            l3 = sch.op("vector",
                        lambda v, pi=pi: v.tensor_scalar(cen[pi][:, :], cen[pi][:, :],
                                                         nmean[pi][:, :], None, ALU.add),
                        waits=[l2], incs=[(s_ve, 1)])
            l4 = sch.op("scalar",
                        lambda sc, pi=pi: sc.activation(sqd[:, :], cen[pi][:, :], AF.Square,
                                                        accum_out=sumsq[pi][:, :]),
                        waits=[l3], incs=[(s_sc, 1)])
            l5 = sch.op("scalar",
                        lambda sc, pi=pi: sc.activation(sumsq[pi][:, :], sumsq[pi][:, :], AF.Ln,
                                                 bias=eps_sb[:, 0:1], scale=1.0 / D),
                        waits=[l4], incs=[(s_sc, 1)])
            l6 = sch.op("scalar",
                        lambda sc, pi=pi: sc.activation(rstd[pi][:, :], sumsq[pi][:, :], AF.Exp, scale=-0.5),
                        waits=[l5], incs=[(s_sc, 1)])
            l7 = sch.op("vector",
                        lambda v, pi=pi: v.tensor_scalar(cen[pi][:, :], cen[pi][:, :],
                                                         rstd[pi][:, :], None, ALU.mult),
                        waits=[l6], incs=[(s_ve, 1)])
            l8 = sch.op("vector",
                        lambda v, pi=pi: v.tensor_copy(x2bf[pi][:, :], cen[pi][:, :]),
                        waits=[l7, x2bf_free.get(tt - 2), x2bf_free.get((tt - 2, "q"))],
                        incs=[(s_ve, 1)])
            ln_free[pi] = l7

            w1 = sch.op("scalar",
                        lambda sc, tt=tt, pi=pi: sc.dma_start(
                            x2_scr[tt * 128:(tt + 1) * 128, :], cen[pi][:, :]),
                        waits=[l7], incs=[(s_dq, 16)])
            x2w_ev[tt] = w1
            cen_free[tt] = w1
            if tt >= 8:
                w2 = sch.op("scalar",
                            lambda sc, tt=tt, pi=pi: sc.dma_start(
                                x2bf_scr[tt * 128:(tt + 1) * 128, :], x2bf[pi][:, :]),
                            waits=[l8], incs=[(s_dq, 16)])
                x2bf_free[tt] = w2
            else:
                tp = None
                for cb in range(0, CC, 4):
                    tp = tpose_group4(x2bf[pi], cb, X2T0, tt, [l8], grpB)
                tposed_ev[tt] = tp
                x2bf_free[tt] = tp

            if dbg_x2:
                sch.op("gpsimd",
                       lambda g, tt=tt, pi=pi: g.dma_start(
                           dbgX[tt * 128:(tt + 1) * 128, :], cen[pi][:, :]),
                       waits=[l7], incs=[(s_out, 16)])

        phB = [(s_pe, s_pe.n), (s_ve, s_ve.n), (s_sc, s_sc.n), (s_dq, s_dq.n), (s_sy, s_sy.n)]

        st_b.close()
        st_ab.close()
        st_c = ExitStack()

        def sbc(name, shape, dt):
            return st_c.enter_context(nc.sbuf_tensor(name, shape, dt))

        acc = sbc("acc", [128, 8 * D], F32)
        X2T1 = sbc("X2T1", [128, CC * 1024], BF16)
        We_b = [sbc(f"Web{i}", [128, CC * 512], BF16) for i in range(2)]
        tmp = [sbc(f"tmp{i}", [128, 512], F32) for i in range(2)]
        yb = [sbc(f"yb{i}", [128, D], F32) for i in range(2)]
        gwb = sbc("gwb", [128, CC * NE], BF16)
        gbb = sbc("gbb", [1, NE], BF16)
        ebb = [sbc(f"ebb{i}", [1, D], BF16) for i in range(2)]
        zg = sbc("zg", [128, NE], F32)
        zz = sbc("zz", [128, NE], F32)
        m1 = sbc("m1", [128, 1], F32)
        m2 = sbc("m2", [128, 1], F32)
        mask = [sbc(f"mask{i}", [128, 8 * NE], F32) for i in range(2)]
        nmean2 = sbc("nmean2", [128, 1], F32)
        sumsq2 = sbc("sumsq2", [128, 1], F32)
        rstd2 = sbc("rstd2", [128, 1], F32)
        sqd2 = sbc("sqd2", [128, D], BF16)

        phBf = [phB[0], phB[1], phB[2], phB[3]]
        gw_ev = sch.op("gpsimd", lambda g: g.dma_start(gwb[:, :], gwt), waits=phBf, incs=[(s_w, 16)])
        gb_ev = sch.op("gpsimd", lambda g: g.dma_start(gbb[:, :], gb), incs=[(s_w, 16)])

        grp_c = [0]
        acc_w = {}
        tmp_free = {}
        We_free = {}
        ebb_free = {}
        yb_free = {}
        acc_init_ev = {}
        ln2_chain = [None]
        mask_ev = {}
        halfC_pe_done = [None]
        tpose_c = {}
        x2bf_ld = {}
        last_y = []

        tp_slot = [0]
        tp_lds = {}

        def ensure_ld(tt):
            if tt in tp_lds or tt >= 16:
                return
            pi = tt % 2
            tp_lds[tt] = sch.op("scalar",
                                lambda sc, tt=tt, pi=pi: sc.dma_start(
                                    x2bf[pi][:, :], x2bf_scr[tt * 128:(tt + 1) * 128, :]),
                                waits=[x2bf_free.get(tt - 2), x2w_ev.get(tt)]
                                + (phBf if tt == 8 else []),
                                incs=[(s_dq, 16)])

        def flush_tp(n, grp_ref):
            while n > 0 and tp_slot[0] < 32:
                i = tp_slot[0]
                tp_slot[0] += 1
                tt = 8 + i // 4
                ti = tt - 8
                pi = tt % 2
                cb = (i % 4) * 4
                if i % 4 == 0:
                    ensure_ld(tt)
                    ensure_ld(tt + 1)
                w0 = [tp_lds[tt]] + (phBf if (ti == 0 and cb == 0) else [])
                ev2 = tpose_group4(x2bf[pi], cb, X2T1, ti, w0, grp_ref)
                tpose_c[tt] = ev2
                if cb == 12:
                    x2bf_free[tt] = ev2
                n -= 1

        for H in range(2):
            X2Tc = X2T0 if H == 0 else X2T1
            if H == 1:
                flush_tp(32, grp_c)
            for ti in range(8):
                tt = H * 8 + ti
                wv_ = [x2w_ev[tt], acc_w.get(("ln2", ti))]
                if H == 0 and ti == 0:
                    wv_ += phB
                acc_init_ev[ti] = sch.op("scalar",
                                         lambda sc, tt=tt, ti=ti: sc.dma_start(
                                             acc[:, ti * D:(ti + 1) * D],
                                             x2_scr[tt * 128:(tt + 1) * 128, :]),
                                         waits=wv_, incs=[(s_dq, 16)])

            for ti in range(8):
                tt = H * 8 + ti
                key, bank, boff = slots8[grp_c[0] % 8]
                grp_c[0] += 1
                wv_ = [gw_ev, gb_ev, wfree(key)]
                if H == 0:
                    wv_ += [tposed_ev.get(tt), tposed_ev.get((tt, "q"))]
                else:
                    wv_.append(tpose_c.get(tt))

                def fng(t, ti=ti, bank=bank, boff=boff, X2Tc=X2Tc):
                    t.matmul(bank[:, boff:boff + NE], onesbf_sb[:, :], gbb[:, :], start=True, stop=False)
                    inst = None
                    for c in range(CC):
                        inst = t.matmul(bank[:, boff:boff + NE],
                                        X2Tc[:, c * 1024 + ti * 128:c * 1024 + (ti + 1) * 128],
                                        gwb[:, c * NE:(c + 1) * NE],
                                        start=False, stop=(c == CC - 1))
                    return inst

                gm = sch.op("tensor", fng, waits=wv_, incs=[(s_pe, 1)])
                zev = sch.op("scalar", lambda sc, bank=bank, boff=boff: sc.copy(
                                 zg[:, :], bank[:, boff:boff + NE]),
                             waits=[gm, mask_ev.get(H * 8 + ti - 1)], incs=[(s_sc, 1)])
                bank_free[key] = zev
                c1 = sch.op("vector", lambda v: v.reduce_max(m1[:, :], zg[:, :], axis=AX.X),
                            waits=[zev], incs=[(s_ve, 1)])
                c2 = sch.op("vector",
                            lambda v: v.tensor_scalar(zz[:, :], zg[:, :], m1[:, :], None, ALU.is_equal),
                            waits=[c1], incs=[(s_ve, 1)])
                c3 = sch.op("vector",
                            lambda v: v.scalar_tensor_tensor(zz[:, :], zz[:, :], -1e30, zg[:, :],
                                                             ALU.mult, ALU.add),
                            waits=[c2], incs=[(s_ve, 1)])
                c4 = sch.op("vector", lambda v: v.reduce_max(m2[:, :], zz[:, :], axis=AX.X),
                            waits=[c3], incs=[(s_ve, 1)])
                c5 = sch.op("vector",
                            lambda v, ti=ti, H=H: v.tensor_scalar(mask[H][:, ti * NE:(ti + 1) * NE],
                                                                  zg[:, :], m2[:, :], 0.5,
                                                                  ALU.is_ge, ALU.mult),
                            waits=[c4], incs=[(s_ve, 1)])
                mask_ev[H * 8 + ti] = c5

            for e in range(NE):
                wpi = e % 2
                eb_ev = sch.op("gpsimd",
                               lambda g, e=e, wpi=wpi: g.dma_start(
                                   ebb[wpi][:, :], eb[0:1, e * D:(e + 1) * D]),
                               waits=[ebb_free.get(wpi)], incs=[(s_w, 16)])
                for oq in range(4):
                    if H == 0:
                        flush_tp(1, grp_c)
                    bpi = (e * 4 + oq) % 2
                    src = ew[e].rearrange("(c p) o -> p c o", p=128)[:, :, oq * 512:(oq + 1) * 512]
                    d3 = We_b[bpi][:, :].rearrange("p (c n) -> p c n", c=CC)
                    we_ev = None
                    for half in range(2):
                        cs = slice(half * 8, (half + 1) * 8)
                        we_ev = sch.op("gpsimd",
                                       lambda g, d=d3[:, cs, :], s2=src[:, cs, :]: g.dma_start(d, s2),
                                       waits=[We_free.get(bpi)],
                                       incs=[(s_w, 16)])
                    for ti in range(8):
                        key, bank, boff = slots8[grp_c[0] % 8]
                        grp_c[0] += 1
                        wv_ = [we_ev, eb_ev, wfree(key)]

                        def fne(t, ti=ti, bank=bank, boff=boff, e=e, oq=oq, wpi=wpi, bpi=bpi, X2Tc=X2Tc):
                            t.matmul(bank[:, boff:boff + 512], onesbf_sb[:, :],
                                     ebb[wpi][:, oq * 512:(oq + 1) * 512], start=True, stop=False)
                            inst = None
                            for c in range(CC):
                                inst = t.matmul(bank[:, boff:boff + 512],
                                                X2Tc[:, c * 1024 + ti * 128:c * 1024 + (ti + 1) * 128],
                                                We_b[bpi][:, c * 512:(c + 1) * 512],
                                                start=False, stop=(c == CC - 1))
                            return inst

                        hm = sch.op("tensor", fne, waits=wv_, incs=[(s_pe, 1)])
                        if ti == 7:
                            We_free[bpi] = hm
                            if oq == 3:
                                ebb_free[wpi] = hm
                        tpi = grp_c[0] % 2
                        rl = sch.op("scalar",
                                    lambda sc, bank=bank, boff=boff, ti=ti, e=e, tpi=tpi, H=H: sc.activation(
                                        tmp[tpi][:, :], bank[:, boff:boff + 512], AF.Relu,
                                        scale=mask[H][:, ti * NE + e:ti * NE + e + 1]),
                                    waits=[hm, mask_ev[H * 8 + ti], tmp_free.get(tpi)],
                                    incs=[(s_sc, 1)])
                        bank_free[key] = rl
                        ac = sch.op("vector",
                                    lambda v, ti=ti, oq=oq, tpi=tpi: v.tensor_tensor(
                                        acc[:, ti * D + oq * 512:ti * D + (oq + 1) * 512],
                                        acc[:, ti * D + oq * 512:ti * D + (oq + 1) * 512],
                                        tmp[tpi][:, :], ALU.add),
                                    waits=[rl, acc_w.get((ti, oq), acc_init_ev[ti])],
                                    incs=[(s_ve, 1)])
                        acc_w[(ti, oq)] = ac
                        tmp_free[tpi] = ac
            halfC_pe_done[0] = s_pe.n

            for ti in range(8):
                tt = H * 8 + ti
                pi = ti % 2
                aw = [acc_w[(ti, oq)] for oq in range(4)]
                l1 = sch.op("vector",
                            lambda v, ti=ti: v.reduce_sum(nmean2[:, :], acc[:, ti * D:(ti + 1) * D],
                                                          axis=AX.X, negate=True),
                            waits=aw + [ln2_chain[0]], incs=[(s_ve, 1)])
                l2 = sch.op("vector",
                            lambda v: v.tensor_scalar(nmean2[:, :], nmean2[:, :], 1.0 / D, None, ALU.mult),
                            waits=[l1], incs=[(s_ve, 1)])
                l3 = sch.op("vector",
                            lambda v, ti=ti: v.tensor_scalar(acc[:, ti * D:(ti + 1) * D],
                                                             acc[:, ti * D:(ti + 1) * D],
                                                             nmean2[:, :], None, ALU.add),
                            waits=[l2], incs=[(s_ve, 1)])
                l4 = sch.op("scalar",
                            lambda sc, ti=ti: sc.activation(sqd2[:, :], acc[:, ti * D:(ti + 1) * D],
                                                            AF.Square, accum_out=sumsq2[:, :]),
                            waits=[l3], incs=[(s_sc, 1)])
                l5 = sch.op("scalar",
                            lambda sc: sc.activation(sumsq2[:, :], sumsq2[:, :], AF.Ln,
                                                     bias=eps_sb[:, 0:1], scale=1.0 / D),
                            waits=[l4], incs=[(s_sc, 1)])
                l6 = sch.op("scalar",
                            lambda sc: sc.activation(rstd2[:, :], sumsq2[:, :], AF.Exp, scale=-0.5),
                            waits=[l5], incs=[(s_sc, 1)])
                l7 = sch.op("vector",
                            lambda v, ti=ti, pi=pi: v.tensor_scalar(
                                yb[pi][:, :], acc[:, ti * D:(ti + 1) * D],
                                rstd2[:, :], None, ALU.mult),
                            waits=[l6, yb_free.get(pi)], incs=[(s_ve, 1)])
                ln2_chain[0] = l7
                acc_w[("ln2", ti)] = l7
                yw = sch.op("scalar",
                            lambda sc, tt=tt, pi=pi: sc.dma_start(
                                y[tt * 128:(tt + 1) * 128, :], yb[pi][:, :]),
                            waits=[l7], incs=[(s_out, 16)])
                yb_free[pi] = yw
                last_y.append(yw)

        sch.op("sync", lambda s2: s2.nop(), waits=[(s_out, s_out.n)])

        sch.emit(blk)
        st_c.close()
    return nc


_NC_CACHE = {}


def _get_nc():
    if "nc" not in _NC_CACHE:
        _NC_CACHE["nc"] = build_program()
    return _NC_CACHE["nc"]


def kernel(L0_attn_hidden_states, L0_attn_Wq, L0_attn_Wk, L0_attn_Wv, L0_attn_Wo,
           L0_ffn_gate_w, L0_ffn_gate_b, L0_ffn_expert_w, L0_ffn_expert_b):
    x = np.ascontiguousarray(np.asarray(L0_attn_hidden_states, dtype=np.float32))
    Wq = np.asarray(L0_attn_Wq, dtype=np.float32)
    Wk = np.asarray(L0_attn_Wk, dtype=np.float32)
    Wv = np.asarray(L0_attn_Wv, dtype=np.float32)
    Wo = np.asarray(L0_attn_Wo, dtype=np.float32)
    gate_w = np.asarray(L0_ffn_gate_w, dtype=np.float32)
    gate_b = np.asarray(L0_ffn_gate_b, dtype=np.float32)
    expert_w = np.ascontiguousarray(np.asarray(L0_ffn_expert_w, dtype=np.float32))
    expert_b = np.asarray(L0_ffn_expert_b, dtype=np.float32)

    wqT = np.ascontiguousarray(Wq.T)
    wkT = np.ascontiguousarray(Wk.T)
    wvT = np.ascontiguousarray(Wv.T)
    woT = np.ascontiguousarray(Wo.T)
    gwt = np.ascontiguousarray(gate_w.reshape(CC, 128, NE).transpose(1, 0, 2).reshape(128, CC * NE))
    gb1 = np.ascontiguousarray(gate_b.reshape(1, NE))
    eb1 = np.ascontiguousarray(expert_b.reshape(1, NE * D))
    ones_bf = np.ones((1, 128), dtype=ml_dtypes.bfloat16)
    onescol_bf = np.ones((128, 1), dtype=ml_dtypes.bfloat16)
    onesmat_bf1 = np.ones((128, 128), dtype=ml_dtypes.bfloat16)
    ident_bf1 = np.eye(128, dtype=ml_dtypes.bfloat16)
    ones_f = np.ones((1, 128), dtype=np.float32)
    epscol = np.full((128, 1), EPS, dtype=np.float32)

    nc = _get_nc()
    in_maps = []
    for core in range(8):
        b, hf = core // 2, core % 2
        own = x[b, hf * NQ:(hf + 1) * NQ]
        oth = x[b, (1 - hf) * NQ:(2 - hf) * NQ]
        xroll = np.concatenate([own, oth], axis=0)
        in_maps.append({
            "xT": np.ascontiguousarray(xroll.T),
            "xq": np.ascontiguousarray(own),
            "wqT": wqT, "wkT": wkT, "wvT": wvT, "woT": woT,
            "gwt": gwt, "gb": gb1, "ew": expert_w, "eb": eb1,
            "ones_bf": ones_bf, "onescol_bf": onescol_bf, "onesmat_bf": onesmat_bf1,
            "ident_bf": ident_bf1,
            "ones_f": ones_f, "epscol": epscol,
        })

    res = run_bass_kernel_spmd(nc, in_maps, core_ids=list(range(8)))
    out = np.empty((4, S, D), dtype=np.float32)
    for core in range(8):
        b, hf = core // 2, core % 2
        out[b, hf * NQ:(hf + 1) * NQ] = res.results[core]["y"]
    return out


if __name__ == "__main__":
    build_program()
    print("build ok")


# version 19
# speedup vs baseline: 1.1709x; 1.0031x over previous
import numpy as np
import ml_dtypes
from contextlib import ExitStack

import concourse.bass as bass
import concourse.mybir as mybir
from concourse.bass_utils import run_bass_kernel_spmd

F32 = mybir.dt.float32
BF16 = mybir.dt.bfloat16
AF = mybir.ActivationFunctionType
ALU = mybir.AluOpType
AX = mybir.AxisListType

D = 2048
S = 4096
NQ = 2048
NH = 16
HD = 128
G = 2
CC = D // 128
KC = S // 128
QTW = 512
NQT = NQ // QTW
STRIP = 512
NSTRIP = S // STRIP
NE = 8
SCALE = float(HD) ** -0.5
EPS = 1e-5
NTT = NQ // 128

ENGINES = ("tensor", "vector", "scalar", "sync", "gpsimd")


class Sem:
    def __init__(self, ctx, nc, name):
        self.h = ctx.enter_context(nc.semaphore(name))
        self.n = 0
        self.name = name


class Sched:
    def __init__(self):
        self.q = {e: [] for e in ENGINES}

    def op(self, eng, fn, waits=(), incs=()):
        ev = None
        inc_list = []
        for sem, amt in incs:
            sem.n += amt
            inc_list.append((sem, amt))
            if ev is None:
                ev = (sem, sem.n)
        self.q[eng].append((tuple(w for w in waits if w is not None), fn, tuple(inc_list)))
        return ev

    def emit(self, blk):
        for eng in ENGINES:
            items = self.q[eng]
            if not items:
                continue

            def body(engine, items=items):
                for waits, fn, incs in items:
                    for sem, val in waits:
                        engine.wait_ge(sem.h, val)
                    inst = fn(engine)
                    for sem, amt in incs:
                        inst.then_inc(sem.h, amt)

            getattr(blk, eng)(body)


def build_program(npass=8, do_bc=True, dbg_attn=False, dbg_x2=False, dbg_proj=False,
                  dbg_att1=False):
    nc = bass.Bass("TRN2", target_bir_lowering=False)

    xT = nc.dram_tensor("xT", [D, S], F32, kind="ExternalInput").ap()
    xq = nc.dram_tensor("xq", [NQ, D], F32, kind="ExternalInput").ap()
    wqT = nc.dram_tensor("wqT", [D, D], F32, kind="ExternalInput").ap()
    wkT = nc.dram_tensor("wkT", [D, D], F32, kind="ExternalInput").ap()
    wvT = nc.dram_tensor("wvT", [D, D], F32, kind="ExternalInput").ap()
    woT = nc.dram_tensor("woT", [D, D], F32, kind="ExternalInput").ap()
    gwt = nc.dram_tensor("gwt", [128, CC * NE], F32, kind="ExternalInput").ap()
    gb = nc.dram_tensor("gb", [1, NE], F32, kind="ExternalInput").ap()
    ew = nc.dram_tensor("ew", [NE, D, D], F32, kind="ExternalInput").ap()
    eb = nc.dram_tensor("eb", [1, NE * D], F32, kind="ExternalInput").ap()
    ones_bf = nc.dram_tensor("ones_bf", [1, 128], BF16, kind="ExternalInput").ap()
    onescol_bf = nc.dram_tensor("onescol_bf", [128, 1], BF16, kind="ExternalInput").ap()
    onesmat_bf = nc.dram_tensor("onesmat_bf", [128, 128], BF16, kind="ExternalInput").ap()
    ident_bf = nc.dram_tensor("ident_bf", [128, 128], BF16, kind="ExternalInput").ap()
    ones_f = nc.dram_tensor("ones_f", [1, 128], F32, kind="ExternalInput").ap()
    epscol = nc.dram_tensor("epscol", [128, 1], F32, kind="ExternalInput").ap()

    y = nc.dram_tensor("y", [NQ, D], F32, kind="ExternalOutput").ap()
    dbgA = nc.dram_tensor("dbgA", [128, NH * NQ], F32, kind="ExternalOutput").ap() if dbg_attn else None
    dbgX = nc.dram_tensor("dbgX", [NQ, D], F32, kind="ExternalOutput").ap() if dbg_x2 else None
    if dbg_proj:
        dbgK = nc.dram_tensor("dbgK", [128, G * S], F32, kind="ExternalOutput").ap()
        dbgQ = nc.dram_tensor("dbgQ", [128, G * NQ], F32, kind="ExternalOutput").ap()
        dbgV = nc.dram_tensor("dbgV", [128, KC * G * HD], F32, kind="ExternalOutput").ap()
        dbgN = nc.dram_tensor("dbgN", [128, 16], F32, kind="ExternalOutput").ap()

    if dbg_att1:
        dbgD = nc.dram_tensor("dbgD", [1, 512], F32, kind="ExternalOutput").ap()
        dbgR = nc.dram_tensor("dbgR", [128, 512], F32, kind="ExternalOutput").ap()
        dbgE = nc.dram_tensor("dbgE", [128, 512], F32, kind="ExternalOutput").ap()
        dbgE0 = nc.dram_tensor("dbgE0", [128, 512], F32, kind="ExternalOutput").ap()
        dbgE1 = nc.dram_tensor("dbgE1", [128, 512], F32, kind="ExternalOutput").ap()
        dbgU = nc.dram_tensor("dbgU", [128, 512], F32, kind="ExternalOutput").ap()

    x2_scr = nc.dram_tensor("x2_scr", [NQ, D], F32, kind="Internal").ap()
    x2bf_scr = nc.dram_tensor("x2bf_scr", [NQ, D], BF16, kind="Internal").ap()

    ctx = ExitStack()

    onesbf_sb = ctx.enter_context(nc.sbuf_tensor("onesbf_sb", [1, 128], BF16, side="right"))
    onescol_sb = ctx.enter_context(nc.sbuf_tensor("onescol_sb", [128, 1], BF16, side="right"))
    onesmat_sb = ctx.enter_context(nc.sbuf_tensor("onesmat_sb", [128, 128], BF16, side="right"))
    ident_sb = ctx.enter_context(nc.sbuf_tensor("ident_sb", [128, 128], BF16, side="right"))
    onesf_sb = ctx.enter_context(nc.sbuf_tensor("onesf_sb", [1, 128], F32, side="right"))
    eps_sb = ctx.enter_context(nc.sbuf_tensor("eps_sb", [128, 1], F32, side="right"))

    psc = [ctx.enter_context(nc.psum_tensor(f"psc{i}", [128, 1024], F32)) for i in range(2)]
    pab = [ctx.enter_context(nc.psum_tensor(f"pa{i}", [128, 512], F32)) for i in range(2)]
    pdb = [ctx.enter_context(nc.psum_tensor(f"pd{i}", [128, 512], F32)) for i in range(2)]
    slots8 = [("a0", pab[0], 0), ("a1", pab[1], 0), ("d0", pdb[0], 0), ("d1", pdb[1], 0),
              ("s0L", psc[0], 0), ("s0R", psc[0], 512), ("s1L", psc[1], 0), ("s1R", psc[1], 512)]

    s_xt = Sem(ctx, nc, "s_xt")
    s_w = Sem(ctx, nc, "s_w")
    s_pe = Sem(ctx, nc, "s_pe")
    s_sc = Sem(ctx, nc, "s_sc")
    s_ve = Sem(ctx, nc, "s_ve")
    s_dq = Sem(ctx, nc, "s_dq")
    s_sy = Sem(ctx, nc, "s_sy")
    s_out = Sem(ctx, nc, "s_out")

    sch = Sched()
    blk = ctx.enter_context(nc.Block())

    bank_free = {}

    def wfree(bid):
        return bank_free.get(bid)

    with ctx:
        st_ab = ExitStack()
        attnT = st_ab.enter_context(nc.sbuf_tensor("attnT", [128, NH * NQ], BF16))
        st_a = ExitStack()

        def sba(name, shape, dt):
            return st_a.enter_context(nc.sbuf_tensor(name, shape, dt))

        st_a1 = ExitStack()
        wq_b = [st_a1.enter_context(nc.sbuf_tensor(f"wqb{i}", [128, CC * G * HD], BF16, side="right")) for i in range(2)]
        wk_b = [st_a1.enter_context(nc.sbuf_tensor(f"wkb{i}", [128, CC * G * HD], BF16, side="right")) for i in range(2)]
        wv_b = [st_a1.enter_context(nc.sbuf_tensor(f"wvb{i}", [128, CC * G * HD], BF16, side="right")) for i in range(2)]
        xt_b = [st_a1.enter_context(nc.sbuf_tensor(f"xtb{i}", [128, CC * STRIP], BF16, side="right")) for i in range(2)]
        KT = sba("KT", [128, G * S], BF16)
        QTm = sba("QTm", [128, G * NQ], BF16)
        Vm = sba("Vm", [128, KC * G * HD], BF16)
        nsc = sba("nsc", [128, 2 * 512], BF16)
        runmx = sba("runmx", [1, 8], F32)
        sc11 = sba("sc11", [1, 8], F32)
        negc = sba("negc", [1, G], F32)
        negc_col = sba("negc_col", [128, G], F32)
        expSp = [sba(f"expSp{i}", [128, 1024], BF16) for i in range(2)]
        dnln = [sba(f"dnln{i}", [128, 512], F32) for i in range(2)]
        recip_rep = [sba(f"recip_rep{i}", [128, 512], F32) for i in range(2)]

        for t_sb, t_dr in ((onesbf_sb, ones_bf), (onescol_sb, onescol_bf),
                           (onesmat_sb, onesmat_bf), (ident_sb, ident_bf),
                           (onesf_sb, ones_f), (eps_sb, epscol)):
            sch.op("gpsimd", lambda g, a=t_sb, b=t_dr: g.dma_start(a[:, :], b),
                   incs=[(s_w, 16)])

        def w_slice_dma(dst, srcT, p, waits=()):
            src = srcT.rearrange("(c p) d -> p c d", p=128)[:, :, p * G * HD:(p + 1) * G * HD]
            d3 = dst[:, :].rearrange("p (c n) -> p c n", c=CC)
            ev = None
            for half in range(2):
                cs = slice(half * 8, (half + 1) * 8)
                ev = sch.op("gpsimd",
                            lambda g, d=d3[:, cs, :], s2=src[:, cs, :]: g.dma_start(d, s2),
                            waits=waits, incs=[(s_w, 16)])
            return ev

        w_slice_dma(wq_b[0], wqT, 0)
        w_slice_dma(wk_b[0], wkT, 0)
        w_ready = [w_slice_dma(wv_b[0], wvT, 0)]

        xT_r = xT.rearrange("(c p) t -> p c t", p=128)

        strip_ev = {}
        strip_done = {}
        pass_proj_last = {}
        kq_evac_last = [None]
        attn_ev = []
        exp_free = {}
        expSp_free = {}
        gpair = [0]
        hqt_ctr = [0]
        dnln_free = {}
        rr_free = {}
        d_ev_last = [None]

        for p in range(npass):
            wq, wk, wv = wq_b[p % 2], wk_b[p % 2], wv_b[p % 2]
            grp_ctr = [0]
            norm_ctr = [0]
            nsc_free = {}

            mx_init_ev = sch.op("vector", lambda v: v.memset(runmx[:, 0:4], -1e30),
                                waits=[attn_ev[-1]] if attn_ev else (), incs=[(s_ve, 1)])
            norm_last = {0: mx_init_ev, 1: mx_init_ev, 2: mx_init_ev, 3: mx_init_ev}

            norm_q = []

            def flush_norms():
                while norm_q:
                    sq_ev, col = norm_q.pop(0)
                    emit_norm_mm(sq_ev, col)

            def proj_group(p, s, lhs_fn, rhs_fn, n_out, dst, doff, evac_eng):
                key, bank, boff = slots8[grp_ctr[0] % 8]
                grp_ctr[0] += 1
                waits = [strip_ev[(p, s)], wfree(key), w_ready[p]]

                def fn(t, bank=bank, boff=boff, lhs_fn=lhs_fn, rhs_fn=rhs_fn, n_out=n_out):
                    inst = None
                    for c in range(CC):
                        inst = t.matmul(bank[:, boff:boff + n_out], lhs_fn(c), rhs_fn(c),
                                        start=(c == 0), stop=(c == CC - 1))
                    return inst

                mm_ev = sch.op("tensor", fn, waits=waits, incs=[(s_pe, 1)])
                strip_done[(p, s)] = s_pe.n
                sem = s_ve if evac_eng == "vector" else s_sc

                def ev_fn(e, dst=dst, doff=doff, n_out=n_out, bank=bank, boff=boff):
                    if evac_eng == "vector":
                        return e.tensor_copy(dst[:, doff:doff + n_out], bank[:, boff:boff + n_out])
                    return e.copy(dst[:, doff:doff + n_out], bank[:, boff:boff + n_out])

                evac_ev = sch.op(evac_eng, ev_fn, waits=[mm_ev], incs=[(sem, 1)])
                bank_free[key] = evac_ev
                flush_norms()
                return evac_ev

            def emit_norm_mm(sq_ev, col):
                ping = sq_ev[2]
                mm_ev = sch.op("tensor",
                               lambda t, ping=ping: t.matmul(
                                   pdb[0][0:1, :], onescol_sb[:, :],
                                   nsc[:, ping * 512:(ping + 1) * 512],
                                   start=True, stop=True),
                               waits=[sq_ev[:2], wfree("d0")], incs=[(s_pe, 1)])
                nsc_free[ping] = mm_ev
                r_ev = sch.op("vector",
                              lambda v: v.reduce_max(sc11[:, 7:8], pdb[0][0:1, :], axis=AX.X),
                              waits=[mm_ev, norm_last[col]], incs=[(s_ve, 1)])
                bank_free["d0"] = r_ev
                m_ev = sch.op("vector",
                              lambda v, col=col: v.tensor_tensor(
                                  runmx[:, col:col + 1], runmx[:, col:col + 1],
                                  sc11[:, 7:8], ALU.max),
                              waits=[r_ev], incs=[(s_ve, 1)])
                norm_last[col] = m_ev

            def sq_and_norm(p, src, soff, col, prior_ev):
                ping = norm_ctr[0] % 2
                norm_ctr[0] += 1
                waits = [prior_ev, nsc_free.get(ping)]
                sq_ev = sch.op("vector",
                               lambda v, src=src, soff=soff, ping=ping: v.tensor_tensor(
                                   nsc[:, ping * 512:(ping + 1) * 512],
                                   src[:, soff:soff + 512], src[:, soff:soff + 512],
                                   ALU.mult),
                               waits=waits, incs=[(s_ve, 1)])
                norm_q.append((sq_ev + (ping,), col))

            for s in range(NSTRIP):
                waits = []
                if (p, s) != (0, 0) and (p, s) != (0, 1):
                    pp, ss = (p, s - 2) if s >= 2 else (p - 1, NSTRIP + s - 2)
                    waits = [(s_pe, strip_done[(pp, ss)])]
                dst = xt_b[s % 2][:, :].rearrange("p (c n) -> p c n", c=CC)
                src = xT_r[:, :, s * STRIP:(s + 1) * STRIP]
                evx = None
                for half in range(2):
                    cs = slice(half * 8, (half + 1) * 8)
                    evx = sch.op("gpsimd",
                                 lambda g, d=dst[:, cs, :], sr=src[:, cs, :]: g.dma_start(d, sr),
                                 waits=waits, incs=[(s_xt, 16)])
                strip_ev[(p, s)] = evx
                xb = xt_b[s % 2]

                for dc in range(G):
                    e1 = proj_group(
                        p, s,
                        lambda c, dc=dc, wk=wk: wk[:, c * G * HD + dc * HD: c * G * HD + (dc + 1) * HD],
                        lambda c, xb=xb: xb[:, c * STRIP:(c + 1) * STRIP],
                        STRIP, KT, dc * S + s * STRIP, "vector")
                    kq_evac_last[0] = e1
                    sq_and_norm(p, KT, dc * S + s * STRIP, 2 + dc, e1)
                if s < NQ // STRIP:
                    for dc in range(G):
                        e1 = proj_group(
                            p, s,
                            lambda c, dc=dc, wq=wq: wq[:, c * G * HD + dc * HD: c * G * HD + (dc + 1) * HD],
                            lambda c, xb=xb: xb[:, c * STRIP:(c + 1) * STRIP],
                            STRIP, QTm, dc * NQ + s * STRIP, "vector")
                        kq_evac_last[0] = e1
                        sq_and_norm(p, QTm, dc * NQ + s * STRIP, dc, e1)
                for tt in range(STRIP // 128):
                    ktile = s * (STRIP // 128) + tt
                    proj_group(
                        p, s,
                        lambda c, xb=xb, tt=tt: xb[:, c * STRIP + tt * 128: c * STRIP + (tt + 1) * 128],
                        lambda c, wv=wv: wv[:, c * G * HD:(c + 1) * G * HD],
                        G * HD, Vm, ktile * G * HD, "scalar")

            flush_norms()
            pass_proj_last[p] = s_pe.n

            if p + 1 < npass:
                nxt = (p + 1) % 2
                w0 = [(s_pe, pass_proj_last[p - 1])] if p >= 1 else ()
                w_slice_dma(wq_b[nxt], wqT, p + 1, waits=w0)
                w_slice_dma(wk_b[nxt], wkT, p + 1)
                w_ready.append(w_slice_dma(wv_b[nxt], wvT, p + 1))

            neg_evs = []
            for h in range(G):
                e1 = sch.op("scalar",
                            lambda sc, h=h: sc.activation(sc11[:, 0:1], runmx[:, h:h + 1], AF.Ln),
                            waits=[norm_last[h], norm_last[2 + h]], incs=[(s_sc, 1)])
                e2 = sch.op("scalar",
                            lambda sc, h=h: sc.activation(sc11[:, 1:2], runmx[:, 2 + h:3 + h], AF.Ln),
                            waits=[e1], incs=[(s_sc, 1)])
                e3 = sch.op("vector",
                            lambda v: v.tensor_tensor(sc11[:, 2:3], sc11[:, 0:1], sc11[:, 1:2], ALU.add),
                            waits=[e2], incs=[(s_ve, 1)])
                e4 = sch.op("scalar",
                            lambda sc: sc.activation(sc11[:, 3:4], sc11[:, 2:3], AF.Exp, scale=0.5),
                            waits=[e3], incs=[(s_sc, 1)])
                e5 = sch.op("vector",
                            lambda v, h=h: v.tensor_scalar(
                                negc[:, h:h + 1], sc11[:, 3:4], -1.01 * SCALE, None, ALU.mult),
                            waits=[e4], incs=[(s_ve, 1)])
                neg_evs.append(e5)
            rep_mm = sch.op("tensor",
                            lambda t: t.matmul(pab[0][:, 0:G], onesf_sb[:, :], negc[:, 0:G],
                                               start=True, stop=True),
                            waits=neg_evs + [wfree("a0")], incs=[(s_pe, 1)])
            rep_cp = sch.op("scalar",
                            lambda sc: sc.copy(negc_col[:, :], pab[0][:, 0:G]),
                            waits=[rep_mm], incs=[(s_sc, 1)])
            bank_free["a0"] = rep_cp

            psc_first = {0: True, 1: True}
            for h in range(G):
                hd = p * G + h
                for qt in range(NQT):
                    qoff = qt * QTW
                    par = hqt_ctr[0] % 2
                    hqt_ctr[0] += 1
                    at_key, at_bank, _ = slots8[par]
                    dn_key, dn_bank, _ = slots8[2 + par]
                    pend = []

                    def emit_attn_pair(pr2, e_ev, base_kc):
                        a_ev = d_ev = None
                        for half in range(2):
                            kc = base_kc + half
                            aw = [e_ev]
                            if kc == 0:
                                aw.append(wfree(at_key))

                            def mm_a(t, kc=kc, h=h, pr2=pr2, half=half, at_bank=at_bank):
                                return t.matmul(
                                    at_bank[:, :],
                                    Vm[:, kc * G * HD + h * HD: kc * G * HD + (h + 1) * HD],
                                    expSp[pr2][:, half * 512:(half + 1) * 512],
                                    start=(kc == 0), stop=(kc == KC - 1))

                            a_ev = sch.op("tensor", mm_a, waits=aw, incs=[(s_pe, 1)])
                            dw = [wfree(dn_key)] if kc == 0 else []

                            def mm_d(t, kc=kc, pr2=pr2, half=half, dn_bank=dn_bank):
                                return t.matmul(
                                    dn_bank[:, :], onesmat_sb[:, :],
                                    expSp[pr2][:, half * 512:(half + 1) * 512],
                                    start=(kc == 0), stop=(kc == KC - 1))

                            d_ev = sch.op("tensor", mm_d, waits=dw, incs=[(s_pe, 1)])
                        expSp_free[pr2] = d_ev
                        return a_ev, d_ev

                    a_ev = d_ev = None
                    for pr in range(KC // 2):
                        pr2 = gpair[0] % 2
                        gpair[0] += 1
                        scb = psc[0] if pr2 == 0 else psc[1]
                        s2_ev = None
                        for half in range(2):
                            kc = 2 * pr + half
                            sw = []
                            if half == 0:
                                sw.append(exp_free.get(pr2))
                                if psc_first[pr2]:
                                    sw += [wfree("s0L" if pr2 == 0 else "s1L"),
                                           wfree("s0R" if pr2 == 0 else "s1R")]
                                    psc_first[pr2] = False
                                if kc == 0 and qt == 0 and h == 0:
                                    sw.append(kq_evac_last[0])

                            def mm_s(t, h=h, kc=kc, qoff=qoff, half=half, scb=scb):
                                return t.matmul(
                                    scb[:, half * 512:(half + 1) * 512],
                                    KT[:, h * S + kc * 128:h * S + (kc + 1) * 128],
                                    QTm[:, h * NQ + qoff:h * NQ + qoff + QTW],
                                    start=True, stop=True)

                            s2_ev = sch.op("tensor", mm_s, waits=sw, incs=[(s_pe, 1)])

                        ew_ = [s2_ev, expSp_free.get(pr2)]
                        if pr == 0 and qt == 0 and h == 0:
                            ew_.append(rep_cp)
                        e_ev = sch.op("scalar",
                                      lambda sc, pr2=pr2, h=h, scb=scb: sc.activation(
                                          expSp[pr2][:, :], scb[:, :], AF.Exp,
                                          bias=negc_col[:, h:h + 1], scale=SCALE),
                                      waits=ew_, incs=[(s_sc, 1)])
                        exp_free[pr2] = e_ev
                        pend.append((pr, pr2, e_ev))
                        if len(pend) >= 2:
                            pr0, pr20, e0 = pend.pop(0)
                            a_ev, d_ev = emit_attn_pair(pr20, e0, 2 * pr0)
                    while pend:
                        pr0, pr20, e0 = pend.pop(0)
                        a_ev, d_ev = emit_attn_pair(pr20, e0, 2 * pr0)

                    r1 = sch.op("vector",
                                lambda v, dn_bank=dn_bank, par=par: v.tensor_copy(
                                    dnln[par][:, :], dn_bank[:, :]),
                                waits=[d_ev, dnln_free.get(par)], incs=[(s_ve, 1)])
                    bank_free[dn_key] = r1
                    r2w = [r1, rr_free.get(par)]
                    r2 = sch.op("vector",
                                lambda v, par=par: v.reciprocal(recip_rep[par][:, :],
                                                                dnln[par][:, :]),
                                waits=r2w, incs=[(s_ve, 1)])
                    dnln_free[par] = r2
                    t_ev = sch.op("vector",
                                  lambda v, hd=hd, qoff=qoff, at_bank=at_bank, par=par:
                                  v.tensor_tensor(
                                      attnT[:, hd * NQ + qoff: hd * NQ + qoff + QTW],
                                      at_bank[:, :], recip_rep[par][:, :], ALU.mult),
                                  waits=[a_ev, r2], incs=[(s_ve, 1)])
                    attn_ev.append(t_ev)
                    bank_free[at_key] = t_ev
                    rr_free[par] = t_ev

            for i2 in range(2):
                if exp_free.get(i2) is not None:
                    bank_free["s0L" if i2 == 0 else "s1L"] = exp_free[i2]
                    bank_free["s0R" if i2 == 0 else "s1R"] = exp_free[i2]

        phA = [(s_pe, s_pe.n), (s_ve, s_ve.n), (s_sc, s_sc.n)]

        if dbg_proj:
            wv_ = [(s_pe, s_pe.n), (s_ve, s_ve.n), (s_sc, s_sc.n)]
            sch.op("gpsimd", lambda g: g.dma_start(dbgK, KT[:, :]), waits=wv_, incs=[(s_out, 16)])
            sch.op("gpsimd", lambda g: g.dma_start(dbgQ, QTm[:, :]), incs=[(s_out, 16)])
            sch.op("gpsimd", lambda g: g.dma_start(dbgV, Vm[:, :]), incs=[(s_out, 16)])
            sch.op("gpsimd", lambda g: g.dma_start(dbgN[:, 0:G], negc_col[:, :]), incs=[(s_out, 16)])
            sch.op("gpsimd", lambda g: g.dma_start(dbgN[0:1, 8:16], runmx[:, :]), incs=[(s_out, 16)])

        if dbg_attn:
            dmp = sch.op("gpsimd", lambda g: g.dma_start(dbgA, attnT[:, :]),
                         waits=[attn_ev[-1]], incs=[(s_out, 16)])
            sch.op("sync", lambda s2: s2.nop(), waits=[(s_out, s_out.n)])

        if not do_bc:
            sch.emit(blk)
            st_a.close()
            st_ab.close()
            return nc

        st_a1.close()
        fenceA1 = [(s_pe, pass_proj_last[npass - 1]), (s_xt, s_xt.n), (s_w, s_w.n)]
        X2T0 = ctx.enter_context(nc.sbuf_tensor("X2T0", [128, CC * 1024], BF16, side="right"))
        x2bf = [ctx.enter_context(nc.sbuf_tensor(f"x2bf{i}", [128, D], BF16, side="right"))
                for i in range(2)]
        st_a.close()
        st_b = ExitStack()
        woT_b = st_b.enter_context(nc.sbuf_tensor("woT_b", [128, CC * D], BF16, side="right"))
        xres = [st_b.enter_context(nc.sbuf_tensor(f"xres{i}", [128, D], F32)) for i in range(2)]
        cen = [st_b.enter_context(nc.sbuf_tensor(f"cen{i}", [128, D], F32)) for i in range(2)]
        nmean = [st_b.enter_context(nc.sbuf_tensor(f"nmean{i}", [128, 1], F32)) for i in range(2)]
        sumsq = [st_b.enter_context(nc.sbuf_tensor(f"sumsq{i}", [128, 1], F32)) for i in range(2)]
        rstd = [st_b.enter_context(nc.sbuf_tensor(f"rstd{i}", [128, 1], F32)) for i in range(2)]
        sqd = st_b.enter_context(nc.sbuf_tensor("sqd", [128, D], BF16))


        def tpose_group4(x2src, cbase, dstT, trow, waits_first, grp_ctr_ref):
            key, bank, boff = slots8[grp_ctr_ref[0] % 8]
            grp_ctr_ref[0] += 1
            bfv = bank.bitcast(BF16)

            def fn(t, bfv=bfv, boff=boff, x2src=x2src, cbase=cbase):
                inst = None
                for j in range(4):
                    inst = t.transpose(bfv[:, boff * 2 + j * 128:boff * 2 + (j + 1) * 128],
                                       x2src[:, (cbase + j) * 128:(cbase + j + 1) * 128],
                                       ident_sb[:, :])
                return inst

            mm_ev = sch.op("tensor", fn, waits=list(waits_first) + [wfree(key)],
                           incs=[(s_pe, 1)])

            def ev_fn(v, bfv=bfv, boff=boff, dstT=dstT, cbase=cbase, trow=trow):
                d3 = dstT[:, :].rearrange("p (c n) -> p c n", c=CC)[:, cbase:cbase + 4,
                                                                   trow * 128:(trow + 1) * 128]
                s3 = bfv[:, boff * 2:boff * 2 + 512].rearrange("p (c n) -> p c n", c=4)
                return v.tensor_copy(d3, s3)

            evac_ev = sch.op("vector", ev_fn, waits=[mm_ev], incs=[(s_ve, 1)])
            bank_free[key] = evac_ev
            return evac_ev

        woT_ev = []
        for hdc in range(CC):
            wv_ = phA if hdc == 0 else ()
            woT_ev.append(sch.op("gpsimd",
                                 lambda g, hdc=hdc: g.dma_start(
                                     woT_b[:, hdc * D:(hdc + 1) * D],
                                     woT[hdc * 128:(hdc + 1) * 128, :]),
                                 waits=wv_, incs=[(s_w, 16)]))

        grpB = [0]
        resid_ev = {}
        cen_free = {}
        x2w_ev = {}
        x2bf_free = {}
        tposed_ev = {}
        ln_free = {}

        for tt in range(NTT):
            pi = tt % 2
            xr_ev = sch.op("scalar",
                           lambda sc, tt=tt, pi=pi: sc.dma_start(
                               xres[pi][:, :], xq[tt * 128:(tt + 1) * 128, :]),
                           waits=[resid_ev.get(tt - 2)] + (phA if tt < 2 else []),
                           incs=[(s_dq, 16)])
            g_evs = []
            slots_tt = []
            for dmt in range(4):
                slots_tt.append(slots8[grpB[0] % 8])
                grpB[0] += 1
            for dmt in range(4):
                key, bank, boff = slots_tt[dmt]
                waits = [wfree(key)]
                if tt == 0:
                    waits += phA

                def fn(t, dmt=dmt, tt=tt, bank=bank, boff=boff):
                    inst = None
                    for hdc in range(CC):
                        if tt == 0 and dmt == 0:
                            t.wait_ge(woT_ev[hdc][0].h, woT_ev[hdc][1])
                        inst = t.matmul(bank[:, boff:boff + 512],
                                        attnT[:, hdc * NQ + tt * 128:hdc * NQ + (tt + 1) * 128],
                                        woT_b[:, hdc * D + dmt * 512:hdc * D + (dmt + 1) * 512],
                                        start=(hdc == 0), stop=(hdc == CC - 1))
                    return inst

                g_evs.append(sch.op("tensor", fn, waits=waits, incs=[(s_pe, 1)]))
            r_evs = []
            for dmt in range(4):
                key, bank, boff = slots_tt[dmt]
                r_evs.append(sch.op("vector",
                                    lambda v, dmt=dmt, pi=pi, bank=bank, boff=boff: v.tensor_tensor(
                                        cen[pi][:, dmt * 512:(dmt + 1) * 512],
                                        bank[:, boff:boff + 512],
                                        xres[pi][:, dmt * 512:(dmt + 1) * 512], ALU.add),
                                    waits=[g_evs[dmt], xr_ev, cen_free.get(tt - 2)],
                                    incs=[(s_ve, 1)]))
                bank_free[key] = r_evs[-1]
            resid_ev[tt] = r_evs[-1]

            l1 = sch.op("vector",
                        lambda v, pi=pi: v.reduce_sum(nmean[pi][:, :], cen[pi][:, :], axis=AX.X, negate=True),
                        waits=r_evs + [ln_free.get(pi)], incs=[(s_ve, 1)])
            l2 = sch.op("vector",
                        lambda v, pi=pi: v.tensor_scalar(nmean[pi][:, :], nmean[pi][:, :], 1.0 / D, None, ALU.mult),
                        waits=[l1], incs=[(s_ve, 1)])
            l3 = sch.op("vector",
                        lambda v, pi=pi: v.tensor_scalar(cen[pi][:, :], cen[pi][:, :],
                                                         nmean[pi][:, :], None, ALU.add),
                        waits=[l2], incs=[(s_ve, 1)])
            l4 = sch.op("scalar",
                        lambda sc, pi=pi: sc.activation(sqd[:, :], cen[pi][:, :], AF.Square,
                                                        accum_out=sumsq[pi][:, :]),
                        waits=[l3], incs=[(s_sc, 1)])
            l5 = sch.op("scalar",
                        lambda sc, pi=pi: sc.activation(sumsq[pi][:, :], sumsq[pi][:, :], AF.Ln,
                                                 bias=eps_sb[:, 0:1], scale=1.0 / D),
                        waits=[l4], incs=[(s_sc, 1)])
            l6 = sch.op("scalar",
                        lambda sc, pi=pi: sc.activation(rstd[pi][:, :], sumsq[pi][:, :], AF.Exp, scale=-0.5),
                        waits=[l5], incs=[(s_sc, 1)])
            l7 = sch.op("vector",
                        lambda v, pi=pi: v.tensor_scalar(cen[pi][:, :], cen[pi][:, :],
                                                         rstd[pi][:, :], None, ALU.mult),
                        waits=[l6], incs=[(s_ve, 1)])
            l8 = sch.op("vector",
                        lambda v, pi=pi: v.tensor_copy(x2bf[pi][:, :], cen[pi][:, :]),
                        waits=[l7, x2bf_free.get(tt - 2), x2bf_free.get((tt - 2, "q"))],
                        incs=[(s_ve, 1)])
            ln_free[pi] = l7

            w1 = sch.op("scalar",
                        lambda sc, tt=tt, pi=pi: sc.dma_start(
                            x2_scr[tt * 128:(tt + 1) * 128, :], cen[pi][:, :]),
                        waits=[l7], incs=[(s_dq, 16)])
            x2w_ev[tt] = w1
            cen_free[tt] = w1
            if tt >= 8:
                w2 = sch.op("scalar",
                            lambda sc, tt=tt, pi=pi: sc.dma_start(
                                x2bf_scr[tt * 128:(tt + 1) * 128, :], x2bf[pi][:, :]),
                            waits=[l8], incs=[(s_dq, 16)])
                x2bf_free[tt] = w2
            else:
                tp = None
                for cb in range(0, CC, 4):
                    tp = tpose_group4(x2bf[pi], cb, X2T0, tt, [l8], grpB)
                tposed_ev[tt] = tp
                x2bf_free[tt] = tp

            if dbg_x2:
                sch.op("gpsimd",
                       lambda g, tt=tt, pi=pi: g.dma_start(
                           dbgX[tt * 128:(tt + 1) * 128, :], cen[pi][:, :]),
                       waits=[l7], incs=[(s_out, 16)])

        phB = [(s_pe, s_pe.n), (s_ve, s_ve.n), (s_sc, s_sc.n), (s_dq, s_dq.n), (s_sy, s_sy.n)]

        st_b.close()
        st_ab.close()
        st_c = ExitStack()

        def sbc(name, shape, dt):
            return st_c.enter_context(nc.sbuf_tensor(name, shape, dt))

        acc = sbc("acc", [128, 8 * D], F32)
        X2T1 = sbc("X2T1", [128, CC * 1024], BF16)
        We_b = [sbc(f"Web{i}", [128, CC * 512], BF16) for i in range(2)]
        tmp = [sbc(f"tmp{i}", [128, 512], F32) for i in range(2)]
        yb = [sbc("yb0", [128, D], F32)]
        gwb = st_c.enter_context(nc.sbuf_tensor("gwb", [128, CC * NE], BF16, side="right"))
        gbb = st_c.enter_context(nc.sbuf_tensor("gbb", [1, NE], BF16, side="right"))
        ebb = [st_c.enter_context(nc.sbuf_tensor(f"ebb{i}", [1, D], BF16, side="right")) for i in range(2)]
        zg = sbc("zg", [128, NE], F32)
        zz = sbc("zz", [128, NE], F32)
        m1 = sbc("m1", [128, 1], F32)
        m2 = sbc("m2", [128, 1], F32)
        mask = [sbc(f"mask{i}", [128, 8 * NE], F32) for i in range(2)]
        nmean2 = sbc("nmean2", [128, 1], F32)
        sumsq2 = sbc("sumsq2", [128, 1], F32)
        rstd2 = sbc("rstd2", [128, 1], F32)
        sqd2 = sbc("sqd2", [128, D], BF16)

        phBf = [phB[0], phB[1], phB[2], phB[3]]
        gw_ev = sch.op("gpsimd", lambda g: g.dma_start(gwb[:, :], gwt), waits=phBf, incs=[(s_w, 16)])
        gb_ev = sch.op("gpsimd", lambda g: g.dma_start(gbb[:, :], gb), incs=[(s_w, 16)])

        grp_c = [0]
        acc_w = {}
        tmp_free = {}
        We_free = {}
        ebb_free = {}
        yb_free = {}
        acc_init_ev = {}
        ln2_chain = [None]
        mask_ev = {}
        halfC_pe_done = [None]
        tpose_c = {}
        x2bf_ld = {}
        last_y = []

        tp_slot = [0]
        tp_lds = {}

        def ensure_ld(tt):
            if tt in tp_lds or tt >= 16:
                return
            pi = tt % 2
            tp_lds[tt] = sch.op("scalar",
                                lambda sc, tt=tt, pi=pi: sc.dma_start(
                                    x2bf[pi][:, :], x2bf_scr[tt * 128:(tt + 1) * 128, :]),
                                waits=[x2bf_free.get(tt - 2), x2w_ev.get(tt)]
                                + (phBf if tt == 8 else []),
                                incs=[(s_dq, 16)])

        def flush_tp(n, grp_ref):
            while n > 0 and tp_slot[0] < 32:
                i = tp_slot[0]
                tp_slot[0] += 1
                tt = 8 + i // 4
                ti = tt - 8
                pi = tt % 2
                cb = (i % 4) * 4
                if i % 4 == 0:
                    ensure_ld(tt)
                    ensure_ld(tt + 1)
                w0 = [tp_lds[tt]] + (phBf if (ti == 0 and cb == 0) else [])
                ev2 = tpose_group4(x2bf[pi], cb, X2T1, ti, w0, grp_ref)
                tpose_c[tt] = ev2
                if cb == 12:
                    x2bf_free[tt] = ev2
                n -= 1

        for H in range(2):
            X2Tc = X2T0 if H == 0 else X2T1
            if H == 1:
                flush_tp(32, grp_c)
            for ti in range(8):
                tt = H * 8 + ti
                wv_ = [x2w_ev[tt], acc_w.get(("ln2", ti))]
                if H == 0 and ti == 0:
                    wv_ += phB
                acc_init_ev[ti] = sch.op("scalar",
                                         lambda sc, tt=tt, ti=ti: sc.dma_start(
                                             acc[:, ti * D:(ti + 1) * D],
                                             x2_scr[tt * 128:(tt + 1) * 128, :]),
                                         waits=wv_, incs=[(s_dq, 16)])

            for ti in range(8):
                tt = H * 8 + ti
                key, bank, boff = slots8[grp_c[0] % 8]
                grp_c[0] += 1
                wv_ = [gw_ev, gb_ev, wfree(key)]
                if H == 0:
                    wv_ += [tposed_ev.get(tt), tposed_ev.get((tt, "q"))]
                else:
                    wv_.append(tpose_c.get(tt))

                def fng(t, ti=ti, bank=bank, boff=boff, X2Tc=X2Tc):
                    t.matmul(bank[:, boff:boff + NE], onesbf_sb[:, :], gbb[:, :], start=True, stop=False)
                    inst = None
                    for c in range(CC):
                        inst = t.matmul(bank[:, boff:boff + NE],
                                        X2Tc[:, c * 1024 + ti * 128:c * 1024 + (ti + 1) * 128],
                                        gwb[:, c * NE:(c + 1) * NE],
                                        start=False, stop=(c == CC - 1))
                    return inst

                gm = sch.op("tensor", fng, waits=wv_, incs=[(s_pe, 1)])
                zev = sch.op("scalar", lambda sc, bank=bank, boff=boff: sc.copy(
                                 zg[:, :], bank[:, boff:boff + NE]),
                             waits=[gm, mask_ev.get(H * 8 + ti - 1)], incs=[(s_sc, 1)])
                bank_free[key] = zev
                c1 = sch.op("vector", lambda v: v.reduce_max(m1[:, :], zg[:, :], axis=AX.X),
                            waits=[zev], incs=[(s_ve, 1)])
                c2 = sch.op("vector",
                            lambda v: v.tensor_scalar(zz[:, :], zg[:, :], m1[:, :], None, ALU.is_equal),
                            waits=[c1], incs=[(s_ve, 1)])
                c3 = sch.op("vector",
                            lambda v: v.scalar_tensor_tensor(zz[:, :], zz[:, :], -1e30, zg[:, :],
                                                             ALU.mult, ALU.add),
                            waits=[c2], incs=[(s_ve, 1)])
                c4 = sch.op("vector", lambda v: v.reduce_max(m2[:, :], zz[:, :], axis=AX.X),
                            waits=[c3], incs=[(s_ve, 1)])
                c5 = sch.op("vector",
                            lambda v, ti=ti, H=H: v.tensor_scalar(mask[H][:, ti * NE:(ti + 1) * NE],
                                                                  zg[:, :], m2[:, :], 0.5,
                                                                  ALU.is_ge, ALU.mult),
                            waits=[c4], incs=[(s_ve, 1)])
                mask_ev[H * 8 + ti] = c5

            for e in range(NE):
                wpi = e % 2
                eb_ev = sch.op("gpsimd",
                               lambda g, e=e, wpi=wpi: g.dma_start(
                                   ebb[wpi][:, :], eb[0:1, e * D:(e + 1) * D]),
                               waits=[ebb_free.get(wpi)], incs=[(s_w, 16)])
                for oq in range(4):
                    if H == 0:
                        flush_tp(1, grp_c)
                    bpi = (e * 4 + oq) % 2
                    src = ew[e].rearrange("(c p) o -> p c o", p=128)[:, :, oq * 512:(oq + 1) * 512]
                    d3 = We_b[bpi][:, :].rearrange("p (c n) -> p c n", c=CC)
                    we_ev = None
                    for half in range(2):
                        cs = slice(half * 8, (half + 1) * 8)
                        we_ev = sch.op("gpsimd",
                                       lambda g, d=d3[:, cs, :], s2=src[:, cs, :]: g.dma_start(d, s2),
                                       waits=[We_free.get(bpi)],
                                       incs=[(s_w, 16)])
                    for ti in range(8):
                        key, bank, boff = slots8[grp_c[0] % 8]
                        grp_c[0] += 1
                        wv_ = [we_ev, eb_ev, wfree(key)]

                        def fne(t, ti=ti, bank=bank, boff=boff, e=e, oq=oq, wpi=wpi, bpi=bpi, X2Tc=X2Tc):
                            t.matmul(bank[:, boff:boff + 512], onesbf_sb[:, :],
                                     ebb[wpi][:, oq * 512:(oq + 1) * 512], start=True, stop=False)
                            inst = None
                            for c in range(CC):
                                inst = t.matmul(bank[:, boff:boff + 512],
                                                X2Tc[:, c * 1024 + ti * 128:c * 1024 + (ti + 1) * 128],
                                                We_b[bpi][:, c * 512:(c + 1) * 512],
                                                start=False, stop=(c == CC - 1))
                            return inst

                        hm = sch.op("tensor", fne, waits=wv_, incs=[(s_pe, 1)])
                        if ti == 7:
                            We_free[bpi] = hm
                            if oq == 3:
                                ebb_free[wpi] = hm
                        tpi = grp_c[0] % 2
                        rl = sch.op("scalar",
                                    lambda sc, bank=bank, boff=boff, ti=ti, e=e, tpi=tpi, H=H: sc.activation(
                                        tmp[tpi][:, :], bank[:, boff:boff + 512], AF.Relu,
                                        scale=mask[H][:, ti * NE + e:ti * NE + e + 1]),
                                    waits=[hm, mask_ev[H * 8 + ti], tmp_free.get(tpi)],
                                    incs=[(s_sc, 1)])
                        bank_free[key] = rl
                        ac = sch.op("vector",
                                    lambda v, ti=ti, oq=oq, tpi=tpi: v.tensor_tensor(
                                        acc[:, ti * D + oq * 512:ti * D + (oq + 1) * 512],
                                        acc[:, ti * D + oq * 512:ti * D + (oq + 1) * 512],
                                        tmp[tpi][:, :], ALU.add),
                                    waits=[rl, acc_w.get((ti, oq), acc_init_ev[ti])],
                                    incs=[(s_ve, 1)])
                        acc_w[(ti, oq)] = ac
                        tmp_free[tpi] = ac
            halfC_pe_done[0] = s_pe.n

            for ti in range(8):
                tt = H * 8 + ti
                pi = 0
                aw = [acc_w[(ti, oq)] for oq in range(4)]
                l1 = sch.op("vector",
                            lambda v, ti=ti: v.reduce_sum(nmean2[:, :], acc[:, ti * D:(ti + 1) * D],
                                                          axis=AX.X, negate=True),
                            waits=aw + [ln2_chain[0]], incs=[(s_ve, 1)])
                l2 = sch.op("vector",
                            lambda v: v.tensor_scalar(nmean2[:, :], nmean2[:, :], 1.0 / D, None, ALU.mult),
                            waits=[l1], incs=[(s_ve, 1)])
                l3 = sch.op("vector",
                            lambda v, ti=ti: v.tensor_scalar(acc[:, ti * D:(ti + 1) * D],
                                                             acc[:, ti * D:(ti + 1) * D],
                                                             nmean2[:, :], None, ALU.add),
                            waits=[l2], incs=[(s_ve, 1)])
                l4 = sch.op("scalar",
                            lambda sc, ti=ti: sc.activation(sqd2[:, :], acc[:, ti * D:(ti + 1) * D],
                                                            AF.Square, accum_out=sumsq2[:, :]),
                            waits=[l3], incs=[(s_sc, 1)])
                l5 = sch.op("scalar",
                            lambda sc: sc.activation(sumsq2[:, :], sumsq2[:, :], AF.Ln,
                                                     bias=eps_sb[:, 0:1], scale=1.0 / D),
                            waits=[l4], incs=[(s_sc, 1)])
                l6 = sch.op("scalar",
                            lambda sc: sc.activation(rstd2[:, :], sumsq2[:, :], AF.Exp, scale=-0.5),
                            waits=[l5], incs=[(s_sc, 1)])
                l7 = sch.op("vector",
                            lambda v, ti=ti, pi=pi: v.tensor_scalar(
                                yb[pi][:, :], acc[:, ti * D:(ti + 1) * D],
                                rstd2[:, :], None, ALU.mult),
                            waits=[l6, yb_free.get(pi)], incs=[(s_ve, 1)])
                ln2_chain[0] = l7
                acc_w[("ln2", ti)] = l7
                yw = sch.op("scalar",
                            lambda sc, tt=tt, pi=pi: sc.dma_start(
                                y[tt * 128:(tt + 1) * 128, :], yb[pi][:, :]),
                            waits=[l7], incs=[(s_out, 16)])
                yb_free[pi] = yw
                last_y.append(yw)

        sch.op("sync", lambda s2: s2.nop(), waits=[(s_out, s_out.n)])

        sch.emit(blk)
        st_c.close()
    return nc


_NC_CACHE = {}


def _get_nc():
    if "nc" not in _NC_CACHE:
        _NC_CACHE["nc"] = build_program()
    return _NC_CACHE["nc"]


def kernel(L0_attn_hidden_states, L0_attn_Wq, L0_attn_Wk, L0_attn_Wv, L0_attn_Wo,
           L0_ffn_gate_w, L0_ffn_gate_b, L0_ffn_expert_w, L0_ffn_expert_b):
    x = np.ascontiguousarray(np.asarray(L0_attn_hidden_states, dtype=np.float32))
    Wq = np.asarray(L0_attn_Wq, dtype=np.float32)
    Wk = np.asarray(L0_attn_Wk, dtype=np.float32)
    Wv = np.asarray(L0_attn_Wv, dtype=np.float32)
    Wo = np.asarray(L0_attn_Wo, dtype=np.float32)
    gate_w = np.asarray(L0_ffn_gate_w, dtype=np.float32)
    gate_b = np.asarray(L0_ffn_gate_b, dtype=np.float32)
    expert_w = np.ascontiguousarray(np.asarray(L0_ffn_expert_w, dtype=np.float32))
    expert_b = np.asarray(L0_ffn_expert_b, dtype=np.float32)

    wqT = np.ascontiguousarray(Wq.T)
    wkT = np.ascontiguousarray(Wk.T)
    wvT = np.ascontiguousarray(Wv.T)
    woT = np.ascontiguousarray(Wo.T)
    gwt = np.ascontiguousarray(gate_w.reshape(CC, 128, NE).transpose(1, 0, 2).reshape(128, CC * NE))
    gb1 = np.ascontiguousarray(gate_b.reshape(1, NE))
    eb1 = np.ascontiguousarray(expert_b.reshape(1, NE * D))
    ones_bf = np.ones((1, 128), dtype=ml_dtypes.bfloat16)
    onescol_bf = np.ones((128, 1), dtype=ml_dtypes.bfloat16)
    onesmat_bf1 = np.ones((128, 128), dtype=ml_dtypes.bfloat16)
    ident_bf1 = np.eye(128, dtype=ml_dtypes.bfloat16)
    ones_f = np.ones((1, 128), dtype=np.float32)
    epscol = np.full((128, 1), EPS, dtype=np.float32)

    nc = _get_nc()
    in_maps = []
    for core in range(8):
        b, hf = core // 2, core % 2
        own = x[b, hf * NQ:(hf + 1) * NQ]
        oth = x[b, (1 - hf) * NQ:(2 - hf) * NQ]
        xroll = np.concatenate([own, oth], axis=0)
        in_maps.append({
            "xT": np.ascontiguousarray(xroll.T),
            "xq": np.ascontiguousarray(own),
            "wqT": wqT, "wkT": wkT, "wvT": wvT, "woT": woT,
            "gwt": gwt, "gb": gb1, "ew": expert_w, "eb": eb1,
            "ones_bf": ones_bf, "onescol_bf": onescol_bf, "onesmat_bf": onesmat_bf1,
            "ident_bf": ident_bf1,
            "ones_f": ones_f, "epscol": epscol,
        })

    res = run_bass_kernel_spmd(nc, in_maps, core_ids=list(range(8)))
    out = np.empty((4, S, D), dtype=np.float32)
    for core in range(8):
        b, hf = core // 2, core % 2
        out[b, hf * NQ:(hf + 1) * NQ] = res.results[core]["y"]
    return out


if __name__ == "__main__":
    build_program()
    print("build ok")


# version 20
# speedup vs baseline: 1.1914x; 1.0175x over previous
import numpy as np
import ml_dtypes
from contextlib import ExitStack

import concourse.bass as bass
import concourse.mybir as mybir
from concourse.bass_utils import run_bass_kernel_spmd

F32 = mybir.dt.float32
BF16 = mybir.dt.bfloat16
AF = mybir.ActivationFunctionType
ALU = mybir.AluOpType
AX = mybir.AxisListType

D = 2048
S = 4096
NQ = 2048
NH = 16
HD = 128
G = 2
CC = D // 128
KC = S // 128
QTW = 512
NQT = NQ // QTW
STRIP = 512
NSTRIP = S // STRIP
NE = 8
SCALE = float(HD) ** -0.5
EPS = 1e-5
NTT = NQ // 128

ENGINES = ("tensor", "vector", "scalar", "sync", "gpsimd")


class Sem:
    def __init__(self, ctx, nc, name):
        self.h = ctx.enter_context(nc.semaphore(name))
        self.n = 0
        self.name = name


class Sched:
    def __init__(self):
        self.q = {e: [] for e in ENGINES}

    def op(self, eng, fn, waits=(), incs=()):
        ev = None
        inc_list = []
        for sem, amt in incs:
            sem.n += amt
            inc_list.append((sem, amt))
            if ev is None:
                ev = (sem, sem.n)
        self.q[eng].append((tuple(w for w in waits if w is not None), fn, tuple(inc_list)))
        return ev

    def emit(self, blk):
        for eng in ENGINES:
            items = self.q[eng]
            if not items:
                continue

            def body(engine, items=items):
                for waits, fn, incs in items:
                    for sem, val in waits:
                        engine.wait_ge(sem.h, val)
                    inst = fn(engine)
                    for sem, amt in incs:
                        inst.then_inc(sem.h, amt)

            getattr(blk, eng)(body)


def build_program(npass=8, do_bc=True, dbg_attn=False, dbg_x2=False, dbg_proj=False,
                  dbg_att1=False):
    nc = bass.Bass("TRN2", target_bir_lowering=False)

    xT = nc.dram_tensor("xT", [D, S], F32, kind="ExternalInput").ap()
    xq = nc.dram_tensor("xq", [NQ, D], F32, kind="ExternalInput").ap()
    wqT = nc.dram_tensor("wqT", [D, D], F32, kind="ExternalInput").ap()
    wkT = nc.dram_tensor("wkT", [D, D], F32, kind="ExternalInput").ap()
    wvT = nc.dram_tensor("wvT", [D, D], F32, kind="ExternalInput").ap()
    woT = nc.dram_tensor("woT", [D, D], F32, kind="ExternalInput").ap()
    gwt = nc.dram_tensor("gwt", [128, CC * NE], F32, kind="ExternalInput").ap()
    gb = nc.dram_tensor("gb", [1, NE], F32, kind="ExternalInput").ap()
    ew = nc.dram_tensor("ew", [NE, D, D], F32, kind="ExternalInput").ap()
    eb = nc.dram_tensor("eb", [1, NE * D], F32, kind="ExternalInput").ap()
    ones_bf = nc.dram_tensor("ones_bf", [1, 128], BF16, kind="ExternalInput").ap()
    onescol_bf = nc.dram_tensor("onescol_bf", [128, 1], BF16, kind="ExternalInput").ap()
    onesmat_bf = nc.dram_tensor("onesmat_bf", [128, 128], BF16, kind="ExternalInput").ap()
    ident_bf = nc.dram_tensor("ident_bf", [128, 128], BF16, kind="ExternalInput").ap()
    ones_f = nc.dram_tensor("ones_f", [1, 128], F32, kind="ExternalInput").ap()
    epscol = nc.dram_tensor("epscol", [128, 1], F32, kind="ExternalInput").ap()

    y = nc.dram_tensor("y", [NQ, D], F32, kind="ExternalOutput").ap()
    dbgA = nc.dram_tensor("dbgA", [128, NH * NQ], F32, kind="ExternalOutput").ap() if dbg_attn else None
    dbgX = nc.dram_tensor("dbgX", [NQ, D], F32, kind="ExternalOutput").ap() if dbg_x2 else None
    if dbg_proj:
        dbgK = nc.dram_tensor("dbgK", [128, G * S], F32, kind="ExternalOutput").ap()
        dbgQ = nc.dram_tensor("dbgQ", [128, G * NQ], F32, kind="ExternalOutput").ap()
        dbgV = nc.dram_tensor("dbgV", [128, KC * G * HD], F32, kind="ExternalOutput").ap()
        dbgN = nc.dram_tensor("dbgN", [128, 16], F32, kind="ExternalOutput").ap()

    if dbg_att1:
        dbgD = nc.dram_tensor("dbgD", [1, 512], F32, kind="ExternalOutput").ap()
        dbgR = nc.dram_tensor("dbgR", [128, 512], F32, kind="ExternalOutput").ap()
        dbgE = nc.dram_tensor("dbgE", [128, 512], F32, kind="ExternalOutput").ap()
        dbgE0 = nc.dram_tensor("dbgE0", [128, 512], F32, kind="ExternalOutput").ap()
        dbgE1 = nc.dram_tensor("dbgE1", [128, 512], F32, kind="ExternalOutput").ap()
        dbgU = nc.dram_tensor("dbgU", [128, 512], F32, kind="ExternalOutput").ap()

    x2_scr = nc.dram_tensor("x2_scr", [NQ, D], F32, kind="Internal").ap()
    x2bf_scr = nc.dram_tensor("x2bf_scr", [NQ, D], BF16, kind="Internal").ap()

    ctx = ExitStack()

    onesbf_sb = ctx.enter_context(nc.sbuf_tensor("onesbf_sb", [1, 128], BF16, side="right"))
    onescol_sb = ctx.enter_context(nc.sbuf_tensor("onescol_sb", [128, 1], BF16, side="right"))
    onesmat_sb = ctx.enter_context(nc.sbuf_tensor("onesmat_sb", [128, 128], BF16, side="right"))
    ident_sb = ctx.enter_context(nc.sbuf_tensor("ident_sb", [128, 128], BF16, side="right"))
    onesf_sb = ctx.enter_context(nc.sbuf_tensor("onesf_sb", [1, 128], F32, side="right"))
    eps_sb = ctx.enter_context(nc.sbuf_tensor("eps_sb", [128, 1], F32, side="right"))

    psc = [ctx.enter_context(nc.psum_tensor(f"psc{i}", [128, 1024], F32)) for i in range(2)]
    pab = [ctx.enter_context(nc.psum_tensor(f"pa{i}", [128, 512], F32)) for i in range(2)]
    pdb = [ctx.enter_context(nc.psum_tensor(f"pd{i}", [128, 512], F32)) for i in range(2)]
    slots8 = [("a0", pab[0], 0), ("a1", pab[1], 0), ("d0", pdb[0], 0), ("d1", pdb[1], 0),
              ("s0L", psc[0], 0), ("s0R", psc[0], 512), ("s1L", psc[1], 0), ("s1R", psc[1], 512)]

    s_xt = Sem(ctx, nc, "s_xt")
    s_w = Sem(ctx, nc, "s_w")
    s_pe = Sem(ctx, nc, "s_pe")
    s_sc = Sem(ctx, nc, "s_sc")
    s_ve = Sem(ctx, nc, "s_ve")
    s_dq = Sem(ctx, nc, "s_dq")
    s_sy = Sem(ctx, nc, "s_sy")
    s_out = Sem(ctx, nc, "s_out")

    sch = Sched()
    blk = ctx.enter_context(nc.Block())

    bank_free = {}

    def wfree(bid):
        return bank_free.get(bid)

    with ctx:
        st_ab = ExitStack()
        attnT = st_ab.enter_context(nc.sbuf_tensor("attnT", [128, NH * NQ], BF16))
        st_a = ExitStack()

        def sba(name, shape, dt):
            return st_a.enter_context(nc.sbuf_tensor(name, shape, dt))

        st_a1 = ExitStack()
        wq_b = [st_a1.enter_context(nc.sbuf_tensor(f"wqb{i}", [128, CC * G * HD], BF16, side="right")) for i in range(2)]
        wk_b = [st_a1.enter_context(nc.sbuf_tensor(f"wkb{i}", [128, CC * G * HD], BF16, side="right")) for i in range(2)]
        wv_b = [st_a1.enter_context(nc.sbuf_tensor(f"wvb{i}", [128, CC * G * HD], BF16, side="right")) for i in range(2)]
        xt_b = [st_a1.enter_context(nc.sbuf_tensor(f"xtb{i}", [128, CC * STRIP], BF16, side="right")) for i in range(2)]
        KT = sba("KT", [128, G * S], BF16)
        QTm = sba("QTm", [128, G * NQ], BF16)
        Vm = sba("Vm", [128, KC * G * HD], BF16)
        nsc = sba("nsc", [128, 2 * 512], BF16)
        runmx = sba("runmx", [1, 8], F32)
        sc11 = sba("sc11", [1, 8], F32)
        negc = sba("negc", [1, G], F32)
        negc_col = sba("negc_col", [128, G], F32)
        expSp = [sba(f"expSp{i}", [128, 1024], BF16) for i in range(2)]
        dnln = [sba(f"dnln{i}", [128, 512], F32) for i in range(2)]
        recip_rep = [sba(f"recip_rep{i}", [128, 512], F32) for i in range(2)]

        for t_sb, t_dr in ((onesbf_sb, ones_bf), (onescol_sb, onescol_bf),
                           (onesmat_sb, onesmat_bf), (ident_sb, ident_bf),
                           (onesf_sb, ones_f), (eps_sb, epscol)):
            sch.op("gpsimd", lambda g, a=t_sb, b=t_dr: g.dma_start(a[:, :], b),
                   incs=[(s_w, 16)])

        def w_slice_dma(dst, srcT, p, waits=()):
            src = srcT.rearrange("(c p) d -> p c d", p=128)[:, :, p * G * HD:(p + 1) * G * HD]
            d3 = dst[:, :].rearrange("p (c n) -> p c n", c=CC)
            ev = None
            for half in range(2):
                cs = slice(half * 8, (half + 1) * 8)
                ev = sch.op("gpsimd",
                            lambda g, d=d3[:, cs, :], s2=src[:, cs, :]: g.dma_start(d, s2),
                            waits=waits, incs=[(s_w, 16)])
            return ev

        w_slice_dma(wq_b[0], wqT, 0)
        w_slice_dma(wk_b[0], wkT, 0)
        w_ready = [w_slice_dma(wv_b[0], wvT, 0)]

        xT_r = xT.rearrange("(c p) t -> p c t", p=128)

        strip_ev = {}
        strip_done = {}
        pass_proj_last = {}
        kq_evac_last = [None]
        attn_ev = []
        exp_free = {}
        expSp_free = {}
        gpair = [0]
        hqt_ctr = [0]
        dnln_free = {}
        rr_free = {}
        d_ev_last = [None]

        for p in range(npass):
            wq, wk, wv = wq_b[p % 2], wk_b[p % 2], wv_b[p % 2]
            grp_ctr = [0]
            norm_ctr = [0]
            nsc_free = {}

            mx_init_ev = sch.op("vector", lambda v: v.memset(runmx[:, 0:4], -1e30),
                                waits=[attn_ev[-1]] if attn_ev else (), incs=[(s_ve, 1)])
            norm_last = {0: mx_init_ev, 1: mx_init_ev, 2: mx_init_ev, 3: mx_init_ev}

            norm_q = []

            def flush_norms():
                while norm_q:
                    sq_ev, col = norm_q.pop(0)
                    emit_norm_mm(sq_ev, col)

            def proj_group(p, s, lhs_fn, rhs_fn, n_out, dst, doff, evac_eng):
                key, bank, boff = slots8[grp_ctr[0] % 8]
                grp_ctr[0] += 1
                waits = [strip_ev[(p, s)], wfree(key), w_ready[p]]

                def fn(t, bank=bank, boff=boff, lhs_fn=lhs_fn, rhs_fn=rhs_fn, n_out=n_out):
                    inst = None
                    for c in range(CC):
                        inst = t.matmul(bank[:, boff:boff + n_out], lhs_fn(c), rhs_fn(c),
                                        start=(c == 0), stop=(c == CC - 1))
                    return inst

                mm_ev = sch.op("tensor", fn, waits=waits, incs=[(s_pe, 1)])
                strip_done[(p, s)] = s_pe.n
                sem = s_ve if evac_eng == "vector" else s_sc

                def ev_fn(e, dst=dst, doff=doff, n_out=n_out, bank=bank, boff=boff):
                    if evac_eng == "vector":
                        return e.tensor_copy(dst[:, doff:doff + n_out], bank[:, boff:boff + n_out])
                    return e.copy(dst[:, doff:doff + n_out], bank[:, boff:boff + n_out])

                evac_ev = sch.op(evac_eng, ev_fn, waits=[mm_ev], incs=[(sem, 1)])
                bank_free[key] = evac_ev
                flush_norms()
                return evac_ev

            def emit_norm_mm(sq_ev, col):
                ping = sq_ev[2]
                mm_ev = sch.op("tensor",
                               lambda t, ping=ping: t.matmul(
                                   pdb[0][0:1, :], onescol_sb[:, :],
                                   nsc[:, ping * 512:(ping + 1) * 512],
                                   start=True, stop=True),
                               waits=[sq_ev[:2], wfree("d0")], incs=[(s_pe, 1)])
                nsc_free[ping] = mm_ev
                r_ev = sch.op("vector",
                              lambda v: v.reduce_max(sc11[:, 7:8], pdb[0][0:1, :], axis=AX.X),
                              waits=[mm_ev, norm_last[col]], incs=[(s_ve, 1)])
                bank_free["d0"] = r_ev
                m_ev = sch.op("vector",
                              lambda v, col=col: v.tensor_tensor(
                                  runmx[:, col:col + 1], runmx[:, col:col + 1],
                                  sc11[:, 7:8], ALU.max),
                              waits=[r_ev], incs=[(s_ve, 1)])
                norm_last[col] = m_ev

            def sq_and_norm(p, src, soff, col, prior_ev):
                ping = norm_ctr[0] % 2
                norm_ctr[0] += 1
                waits = [prior_ev, nsc_free.get(ping)]
                sq_ev = sch.op("vector",
                               lambda v, src=src, soff=soff, ping=ping: v.tensor_tensor(
                                   nsc[:, ping * 512:(ping + 1) * 512],
                                   src[:, soff:soff + 512], src[:, soff:soff + 512],
                                   ALU.mult),
                               waits=waits, incs=[(s_ve, 1)])
                norm_q.append((sq_ev + (ping,), col))

            for s in range(NSTRIP):
                waits = []
                if (p, s) != (0, 0) and (p, s) != (0, 1):
                    pp, ss = (p, s - 2) if s >= 2 else (p - 1, NSTRIP + s - 2)
                    waits = [(s_pe, strip_done[(pp, ss)])]
                dst = xt_b[s % 2][:, :].rearrange("p (c n) -> p c n", c=CC)
                src = xT_r[:, :, s * STRIP:(s + 1) * STRIP]
                evx = None
                for half in range(2):
                    cs = slice(half * 8, (half + 1) * 8)
                    evx = sch.op("gpsimd",
                                 lambda g, d=dst[:, cs, :], sr=src[:, cs, :]: g.dma_start(d, sr),
                                 waits=waits, incs=[(s_xt, 16)])
                strip_ev[(p, s)] = evx
                xb = xt_b[s % 2]

                for dc in range(G):
                    e1 = proj_group(
                        p, s,
                        lambda c, dc=dc, wk=wk: wk[:, c * G * HD + dc * HD: c * G * HD + (dc + 1) * HD],
                        lambda c, xb=xb: xb[:, c * STRIP:(c + 1) * STRIP],
                        STRIP, KT, dc * S + s * STRIP, "vector")
                    kq_evac_last[0] = e1
                    sq_and_norm(p, KT, dc * S + s * STRIP, 2 + dc, e1)
                if s < NQ // STRIP:
                    for dc in range(G):
                        e1 = proj_group(
                            p, s,
                            lambda c, dc=dc, wq=wq: wq[:, c * G * HD + dc * HD: c * G * HD + (dc + 1) * HD],
                            lambda c, xb=xb: xb[:, c * STRIP:(c + 1) * STRIP],
                            STRIP, QTm, dc * NQ + s * STRIP, "vector")
                        kq_evac_last[0] = e1
                        sq_and_norm(p, QTm, dc * NQ + s * STRIP, dc, e1)
                for tt in range(STRIP // 128):
                    ktile = s * (STRIP // 128) + tt
                    proj_group(
                        p, s,
                        lambda c, xb=xb, tt=tt: xb[:, c * STRIP + tt * 128: c * STRIP + (tt + 1) * 128],
                        lambda c, wv=wv: wv[:, c * G * HD:(c + 1) * G * HD],
                        G * HD, Vm, ktile * G * HD, "scalar")

            flush_norms()
            pass_proj_last[p] = s_pe.n

            if p + 1 < npass:
                nxt = (p + 1) % 2
                w0 = [(s_pe, pass_proj_last[p - 1])] if p >= 1 else ()
                w_slice_dma(wq_b[nxt], wqT, p + 1, waits=w0)
                w_slice_dma(wk_b[nxt], wkT, p + 1)
                w_ready.append(w_slice_dma(wv_b[nxt], wvT, p + 1))

            neg_evs = []
            for h in range(G):
                e1 = sch.op("scalar",
                            lambda sc, h=h: sc.activation(sc11[:, 0:1], runmx[:, h:h + 1], AF.Ln),
                            waits=[norm_last[h], norm_last[2 + h]], incs=[(s_sc, 1)])
                e2 = sch.op("scalar",
                            lambda sc, h=h: sc.activation(sc11[:, 1:2], runmx[:, 2 + h:3 + h], AF.Ln),
                            waits=[e1], incs=[(s_sc, 1)])
                e3 = sch.op("vector",
                            lambda v: v.tensor_tensor(sc11[:, 2:3], sc11[:, 0:1], sc11[:, 1:2], ALU.add),
                            waits=[e2], incs=[(s_ve, 1)])
                e4 = sch.op("scalar",
                            lambda sc: sc.activation(sc11[:, 3:4], sc11[:, 2:3], AF.Exp, scale=0.5),
                            waits=[e3], incs=[(s_sc, 1)])
                e5 = sch.op("vector",
                            lambda v, h=h: v.tensor_scalar(
                                negc[:, h:h + 1], sc11[:, 3:4], -1.01 * SCALE, None, ALU.mult),
                            waits=[e4], incs=[(s_ve, 1)])
                neg_evs.append(e5)
            rep_mm = sch.op("tensor",
                            lambda t: t.matmul(pab[0][:, 0:G], onesf_sb[:, :], negc[:, 0:G],
                                               start=True, stop=True),
                            waits=neg_evs + [wfree("a0")], incs=[(s_pe, 1)])
            rep_cp = sch.op("scalar",
                            lambda sc: sc.copy(negc_col[:, :], pab[0][:, 0:G]),
                            waits=[rep_mm], incs=[(s_sc, 1)])
            bank_free["a0"] = rep_cp

            psc_first = {0: True, 1: True}
            for h in range(G):
                hd = p * G + h
                for qt in range(NQT):
                    qoff = qt * QTW
                    par = hqt_ctr[0] % 2
                    hqt_ctr[0] += 1
                    at_key, at_bank, _ = slots8[par]
                    dn_key, dn_bank, _ = slots8[2 + par]
                    pend = []

                    def emit_attn_pair(pr2, e_ev, base_kc):
                        a_ev = d_ev = None
                        for half in range(2):
                            kc = base_kc + half
                            aw = [e_ev]
                            if kc == 0:
                                aw.append(wfree(at_key))

                            def mm_a(t, kc=kc, h=h, pr2=pr2, half=half, at_bank=at_bank):
                                return t.matmul(
                                    at_bank[:, :],
                                    Vm[:, kc * G * HD + h * HD: kc * G * HD + (h + 1) * HD],
                                    expSp[pr2][:, half * 512:(half + 1) * 512],
                                    start=(kc == 0), stop=(kc == KC - 1))

                            a_ev = sch.op("tensor", mm_a, waits=aw, incs=[(s_pe, 1)])
                            dw = [wfree(dn_key)] if kc == 0 else []

                            def mm_d(t, kc=kc, pr2=pr2, half=half, dn_bank=dn_bank):
                                return t.matmul(
                                    dn_bank[:, :], onesmat_sb[:, :],
                                    expSp[pr2][:, half * 512:(half + 1) * 512],
                                    start=(kc == 0), stop=(kc == KC - 1))

                            d_ev = sch.op("tensor", mm_d, waits=dw, incs=[(s_pe, 1)])
                        expSp_free[pr2] = d_ev
                        return a_ev, d_ev

                    a_ev = d_ev = None
                    for pr in range(KC // 2):
                        pr2 = gpair[0] % 2
                        gpair[0] += 1
                        scb = psc[0] if pr2 == 0 else psc[1]
                        s2_ev = None
                        for half in range(2):
                            kc = 2 * pr + half
                            sw = []
                            if half == 0:
                                sw.append(exp_free.get(pr2))
                                if psc_first[pr2]:
                                    sw += [wfree("s0L" if pr2 == 0 else "s1L"),
                                           wfree("s0R" if pr2 == 0 else "s1R")]
                                    psc_first[pr2] = False
                                if kc == 0 and qt == 0 and h == 0:
                                    sw.append(kq_evac_last[0])

                            def mm_s(t, h=h, kc=kc, qoff=qoff, half=half, scb=scb):
                                return t.matmul(
                                    scb[:, half * 512:(half + 1) * 512],
                                    KT[:, h * S + kc * 128:h * S + (kc + 1) * 128],
                                    QTm[:, h * NQ + qoff:h * NQ + qoff + QTW],
                                    start=True, stop=True)

                            s2_ev = sch.op("tensor", mm_s, waits=sw, incs=[(s_pe, 1)])

                        ew_ = [s2_ev, expSp_free.get(pr2)]
                        if pr == 0 and qt == 0 and h == 0:
                            ew_.append(rep_cp)
                        e_ev = sch.op("scalar",
                                      lambda sc, pr2=pr2, h=h, scb=scb: sc.activation(
                                          expSp[pr2][:, :], scb[:, :], AF.Exp,
                                          bias=negc_col[:, h:h + 1], scale=SCALE),
                                      waits=ew_, incs=[(s_sc, 1)])
                        exp_free[pr2] = e_ev
                        pend.append((pr, pr2, e_ev))
                        if len(pend) >= 2:
                            pr0, pr20, e0 = pend.pop(0)
                            a_ev, d_ev = emit_attn_pair(pr20, e0, 2 * pr0)
                    while pend:
                        pr0, pr20, e0 = pend.pop(0)
                        a_ev, d_ev = emit_attn_pair(pr20, e0, 2 * pr0)

                    r1 = sch.op("vector",
                                lambda v, dn_bank=dn_bank, par=par: v.tensor_copy(
                                    dnln[par][:, :], dn_bank[:, :]),
                                waits=[d_ev, dnln_free.get(par)], incs=[(s_ve, 1)])
                    bank_free[dn_key] = r1
                    r2w = [r1, rr_free.get(par)]
                    r2 = sch.op("vector",
                                lambda v, par=par: v.reciprocal(recip_rep[par][:, :],
                                                                dnln[par][:, :]),
                                waits=r2w, incs=[(s_ve, 1)])
                    dnln_free[par] = r2
                    t_ev = sch.op("vector",
                                  lambda v, hd=hd, qoff=qoff, at_bank=at_bank, par=par:
                                  v.tensor_tensor(
                                      attnT[:, hd * NQ + qoff: hd * NQ + qoff + QTW],
                                      at_bank[:, :], recip_rep[par][:, :], ALU.mult),
                                  waits=[a_ev, r2], incs=[(s_ve, 1)])
                    attn_ev.append(t_ev)
                    bank_free[at_key] = t_ev
                    rr_free[par] = t_ev

            for i2 in range(2):
                if exp_free.get(i2) is not None:
                    bank_free["s0L" if i2 == 0 else "s1L"] = exp_free[i2]
                    bank_free["s0R" if i2 == 0 else "s1R"] = exp_free[i2]

        phA = [(s_pe, s_pe.n), (s_ve, s_ve.n), (s_sc, s_sc.n)]

        if dbg_proj:
            wv_ = [(s_pe, s_pe.n), (s_ve, s_ve.n), (s_sc, s_sc.n)]
            sch.op("gpsimd", lambda g: g.dma_start(dbgK, KT[:, :]), waits=wv_, incs=[(s_out, 16)])
            sch.op("gpsimd", lambda g: g.dma_start(dbgQ, QTm[:, :]), incs=[(s_out, 16)])
            sch.op("gpsimd", lambda g: g.dma_start(dbgV, Vm[:, :]), incs=[(s_out, 16)])
            sch.op("gpsimd", lambda g: g.dma_start(dbgN[:, 0:G], negc_col[:, :]), incs=[(s_out, 16)])
            sch.op("gpsimd", lambda g: g.dma_start(dbgN[0:1, 8:16], runmx[:, :]), incs=[(s_out, 16)])

        if dbg_attn:
            dmp = sch.op("gpsimd", lambda g: g.dma_start(dbgA, attnT[:, :]),
                         waits=[attn_ev[-1]], incs=[(s_out, 16)])
            sch.op("sync", lambda s2: s2.nop(), waits=[(s_out, s_out.n)])

        if not do_bc:
            sch.emit(blk)
            st_a.close()
            st_ab.close()
            return nc

        st_a1.close()
        fenceA1 = [(s_pe, pass_proj_last[npass - 1]), (s_xt, s_xt.n), (s_w, s_w.n)]
        X2T0 = ctx.enter_context(nc.sbuf_tensor("X2T0", [128, CC * 1024], BF16, side="right"))
        x2bf = [ctx.enter_context(nc.sbuf_tensor(f"x2bf{i}", [128, D], BF16, side="right"))
                for i in range(2)]
        st_a.close()
        st_b = ExitStack()
        woT_b = st_b.enter_context(nc.sbuf_tensor("woT_b", [128, CC * D], BF16, side="right"))
        xres = [st_b.enter_context(nc.sbuf_tensor(f"xres{i}", [128, D], F32)) for i in range(2)]
        cen = [st_b.enter_context(nc.sbuf_tensor(f"cen{i}", [128, D], F32)) for i in range(2)]
        nmean = [st_b.enter_context(nc.sbuf_tensor(f"nmean{i}", [128, 1], F32)) for i in range(2)]
        sumsq = [st_b.enter_context(nc.sbuf_tensor(f"sumsq{i}", [128, 1], F32)) for i in range(2)]
        rstd = [st_b.enter_context(nc.sbuf_tensor(f"rstd{i}", [128, 1], F32)) for i in range(2)]
        sqd = st_b.enter_context(nc.sbuf_tensor("sqd", [128, D], BF16))


        def tpose_group4(x2src, cbase, dstT, trow, waits_first, grp_ctr_ref):
            key, bank, boff = slots8[grp_ctr_ref[0] % 8]
            grp_ctr_ref[0] += 1
            bfv = bank.bitcast(BF16)

            def fn(t, bfv=bfv, boff=boff, x2src=x2src, cbase=cbase):
                inst = None
                for j in range(4):
                    inst = t.transpose(bfv[:, boff * 2 + j * 128:boff * 2 + (j + 1) * 128],
                                       x2src[:, (cbase + j) * 128:(cbase + j + 1) * 128],
                                       ident_sb[:, :])
                return inst

            mm_ev = sch.op("tensor", fn, waits=list(waits_first) + [wfree(key)],
                           incs=[(s_pe, 1)])

            def ev_fn(v, bfv=bfv, boff=boff, dstT=dstT, cbase=cbase, trow=trow):
                d3 = dstT[:, :].rearrange("p (c n) -> p c n", c=CC)[:, cbase:cbase + 4,
                                                                   trow * 128:(trow + 1) * 128]
                s3 = bfv[:, boff * 2:boff * 2 + 512].rearrange("p (c n) -> p c n", c=4)
                return v.tensor_copy(d3, s3)

            evac_ev = sch.op("vector", ev_fn, waits=[mm_ev], incs=[(s_ve, 1)])
            bank_free[key] = evac_ev
            return evac_ev

        woT_ev = []
        woT_r = woT.rearrange("(c p) d -> p c d", p=128)
        woT_b3 = woT_b[:, :].rearrange("p (c n) -> p c n", c=CC)
        for q4 in range(4):
            wv_ = phA if q4 == 0 else ()
            ev4 = sch.op("gpsimd",
                         lambda g, q4=q4: g.dma_start(
                             woT_b3[:, q4 * 4:(q4 + 1) * 4, :],
                             woT_r[:, q4 * 4:(q4 + 1) * 4, :]),
                         waits=wv_, incs=[(s_w, 16)])
            woT_ev += [ev4] * 4

        grpB = [0]
        resid_ev = {}
        cen_free = {}
        x2w_ev = {}
        x2bf_free = {}
        tposed_ev = {}
        ln_free = {}

        for tt in range(NTT):
            pi = tt % 2
            xr_ev = sch.op("scalar",
                           lambda sc, tt=tt, pi=pi: sc.dma_start(
                               xres[pi][:, :], xq[tt * 128:(tt + 1) * 128, :]),
                           waits=[resid_ev.get(tt - 2)] + (phA if tt < 2 else []),
                           incs=[(s_dq, 16)])
            g_evs = []
            slots_tt = []
            for dmt in range(4):
                slots_tt.append(slots8[grpB[0] % 8])
                grpB[0] += 1
            for dmt in range(4):
                key, bank, boff = slots_tt[dmt]
                waits = [wfree(key)]
                if tt == 0:
                    waits += phA

                def fn(t, dmt=dmt, tt=tt, bank=bank, boff=boff):
                    inst = None
                    for hdc in range(CC):
                        if tt == 0 and dmt == 0:
                            t.wait_ge(woT_ev[hdc][0].h, woT_ev[hdc][1])
                        inst = t.matmul(bank[:, boff:boff + 512],
                                        attnT[:, hdc * NQ + tt * 128:hdc * NQ + (tt + 1) * 128],
                                        woT_b[:, hdc * D + dmt * 512:hdc * D + (dmt + 1) * 512],
                                        start=(hdc == 0), stop=(hdc == CC - 1))
                    return inst

                g_evs.append(sch.op("tensor", fn, waits=waits, incs=[(s_pe, 1)]))
            r_evs = []
            for dmt in range(4):
                key, bank, boff = slots_tt[dmt]
                r_evs.append(sch.op("vector",
                                    lambda v, dmt=dmt, pi=pi, bank=bank, boff=boff: v.tensor_tensor(
                                        cen[pi][:, dmt * 512:(dmt + 1) * 512],
                                        bank[:, boff:boff + 512],
                                        xres[pi][:, dmt * 512:(dmt + 1) * 512], ALU.add),
                                    waits=[g_evs[dmt], xr_ev, cen_free.get(tt - 2)],
                                    incs=[(s_ve, 1)]))
                bank_free[key] = r_evs[-1]
            resid_ev[tt] = r_evs[-1]

            l1 = sch.op("vector",
                        lambda v, pi=pi: v.reduce_sum(nmean[pi][:, :], cen[pi][:, :], axis=AX.X, negate=True),
                        waits=r_evs + [ln_free.get(pi)], incs=[(s_ve, 1)])
            l2 = sch.op("vector",
                        lambda v, pi=pi: v.tensor_scalar(nmean[pi][:, :], nmean[pi][:, :], 1.0 / D, None, ALU.mult),
                        waits=[l1], incs=[(s_ve, 1)])
            l3 = sch.op("vector",
                        lambda v, pi=pi: v.tensor_scalar(cen[pi][:, :], cen[pi][:, :],
                                                         nmean[pi][:, :], None, ALU.add),
                        waits=[l2], incs=[(s_ve, 1)])
            l4 = sch.op("scalar",
                        lambda sc, pi=pi: sc.activation(sqd[:, :], cen[pi][:, :], AF.Square,
                                                        accum_out=sumsq[pi][:, :]),
                        waits=[l3], incs=[(s_sc, 1)])
            l5 = sch.op("scalar",
                        lambda sc, pi=pi: sc.activation(sumsq[pi][:, :], sumsq[pi][:, :], AF.Ln,
                                                 bias=eps_sb[:, 0:1], scale=1.0 / D),
                        waits=[l4], incs=[(s_sc, 1)])
            l6 = sch.op("scalar",
                        lambda sc, pi=pi: sc.activation(rstd[pi][:, :], sumsq[pi][:, :], AF.Exp, scale=-0.5),
                        waits=[l5], incs=[(s_sc, 1)])
            l7 = sch.op("vector",
                        lambda v, pi=pi: v.tensor_scalar(cen[pi][:, :], cen[pi][:, :],
                                                         rstd[pi][:, :], None, ALU.mult),
                        waits=[l6], incs=[(s_ve, 1)])
            l8 = sch.op("vector",
                        lambda v, pi=pi: v.tensor_copy(x2bf[pi][:, :], cen[pi][:, :]),
                        waits=[l7, x2bf_free.get(tt - 2), x2bf_free.get((tt - 2, "q"))],
                        incs=[(s_ve, 1)])
            ln_free[pi] = l7

            w1 = sch.op("scalar",
                        lambda sc, tt=tt, pi=pi: sc.dma_start(
                            x2_scr[tt * 128:(tt + 1) * 128, :], cen[pi][:, :]),
                        waits=[l7], incs=[(s_dq, 16)])
            x2w_ev[tt] = w1
            cen_free[tt] = w1
            if tt >= 8:
                w2 = sch.op("scalar",
                            lambda sc, tt=tt, pi=pi: sc.dma_start(
                                x2bf_scr[tt * 128:(tt + 1) * 128, :], x2bf[pi][:, :]),
                            waits=[l8], incs=[(s_dq, 16)])
                x2bf_free[tt] = w2
            else:
                tp = None
                for cb in range(0, CC, 4):
                    tp = tpose_group4(x2bf[pi], cb, X2T0, tt, [l8], grpB)
                tposed_ev[tt] = tp
                x2bf_free[tt] = tp

            if dbg_x2:
                sch.op("gpsimd",
                       lambda g, tt=tt, pi=pi: g.dma_start(
                           dbgX[tt * 128:(tt + 1) * 128, :], cen[pi][:, :]),
                       waits=[l7], incs=[(s_out, 16)])

        phB = [(s_pe, s_pe.n), (s_ve, s_ve.n), (s_sc, s_sc.n), (s_dq, s_dq.n), (s_sy, s_sy.n)]

        st_b.close()
        st_ab.close()
        st_c = ExitStack()

        def sbc(name, shape, dt):
            return st_c.enter_context(nc.sbuf_tensor(name, shape, dt))

        acc = sbc("acc", [128, 8 * D], F32)
        X2T1 = sbc("X2T1", [128, CC * 1024], BF16)
        We_b = [sbc(f"Web{i}", [128, CC * 512], BF16) for i in range(2)]
        tmp = [sbc(f"tmp{i}", [128, 512], F32) for i in range(2)]
        yb = [sbc("yb0", [128, D], F32)]
        gwb = st_c.enter_context(nc.sbuf_tensor("gwb", [128, CC * NE], BF16, side="right"))
        gbb = st_c.enter_context(nc.sbuf_tensor("gbb", [1, NE], BF16, side="right"))
        ebb = [st_c.enter_context(nc.sbuf_tensor(f"ebb{i}", [1, D], BF16, side="right")) for i in range(2)]
        zg = sbc("zg", [128, NE], F32)
        zz = sbc("zz", [128, NE], F32)
        m1 = sbc("m1", [128, 1], F32)
        m2 = sbc("m2", [128, 1], F32)
        mask = [sbc(f"mask{i}", [128, 8 * NE], F32) for i in range(2)]
        nmean2 = sbc("nmean2", [128, 1], F32)
        sumsq2 = sbc("sumsq2", [128, 1], F32)
        rstd2 = sbc("rstd2", [128, 1], F32)
        sqd2 = sbc("sqd2", [128, D], BF16)

        phBf = [phB[0], phB[1], phB[2], phB[3]]
        gw_ev = sch.op("gpsimd", lambda g: g.dma_start(gwb[:, :], gwt), waits=phBf, incs=[(s_w, 16)])
        gb_ev = sch.op("gpsimd", lambda g: g.dma_start(gbb[:, :], gb), incs=[(s_w, 16)])

        grp_c = [0]
        acc_w = {}
        tmp_free = {}
        We_free = {}
        ebb_free = {}
        yb_free = {}
        acc_init_ev = {}
        ln2_chain = [None]
        mask_ev = {}
        halfC_pe_done = [None]
        tpose_c = {}
        x2bf_ld = {}
        last_y = []

        tp_slot = [0]
        tp_lds = {}

        def ensure_ld(tt):
            if tt in tp_lds or tt >= 16:
                return
            pi = tt % 2
            tp_lds[tt] = sch.op("scalar",
                                lambda sc, tt=tt, pi=pi: sc.dma_start(
                                    x2bf[pi][:, :], x2bf_scr[tt * 128:(tt + 1) * 128, :]),
                                waits=[x2bf_free.get(tt - 2), x2w_ev.get(tt)]
                                + (phBf if tt == 8 else []),
                                incs=[(s_dq, 16)])

        def flush_tp(n, grp_ref):
            while n > 0 and tp_slot[0] < 32:
                i = tp_slot[0]
                tp_slot[0] += 1
                tt = 8 + i // 4
                ti = tt - 8
                pi = tt % 2
                cb = (i % 4) * 4
                if i % 4 == 0:
                    ensure_ld(tt)
                    ensure_ld(tt + 1)
                w0 = [tp_lds[tt]] + (phBf if (ti == 0 and cb == 0) else [])
                ev2 = tpose_group4(x2bf[pi], cb, X2T1, ti, w0, grp_ref)
                tpose_c[tt] = ev2
                if cb == 12:
                    x2bf_free[tt] = ev2
                n -= 1

        for H in range(2):
            X2Tc = X2T0 if H == 0 else X2T1
            if H == 1:
                flush_tp(32, grp_c)
            for ti in range(8):
                tt = H * 8 + ti
                wv_ = [x2w_ev[tt], acc_w.get(("ln2", ti))]
                if H == 0 and ti == 0:
                    wv_ += phB
                acc_init_ev[ti] = sch.op("scalar",
                                         lambda sc, tt=tt, ti=ti: sc.dma_start(
                                             acc[:, ti * D:(ti + 1) * D],
                                             x2_scr[tt * 128:(tt + 1) * 128, :]),
                                         waits=wv_, incs=[(s_dq, 16)])

            for ti in range(8):
                tt = H * 8 + ti
                key, bank, boff = slots8[grp_c[0] % 8]
                grp_c[0] += 1
                wv_ = [gw_ev, gb_ev, wfree(key)]
                if H == 0:
                    wv_ += [tposed_ev.get(tt), tposed_ev.get((tt, "q"))]
                else:
                    wv_.append(tpose_c.get(tt))

                def fng(t, ti=ti, bank=bank, boff=boff, X2Tc=X2Tc):
                    t.matmul(bank[:, boff:boff + NE], onesbf_sb[:, :], gbb[:, :], start=True, stop=False)
                    inst = None
                    for c in range(CC):
                        inst = t.matmul(bank[:, boff:boff + NE],
                                        X2Tc[:, c * 1024 + ti * 128:c * 1024 + (ti + 1) * 128],
                                        gwb[:, c * NE:(c + 1) * NE],
                                        start=False, stop=(c == CC - 1))
                    return inst

                gm = sch.op("tensor", fng, waits=wv_, incs=[(s_pe, 1)])
                zev = sch.op("scalar", lambda sc, bank=bank, boff=boff: sc.copy(
                                 zg[:, :], bank[:, boff:boff + NE]),
                             waits=[gm, mask_ev.get(H * 8 + ti - 1)], incs=[(s_sc, 1)])
                bank_free[key] = zev
                c1 = sch.op("vector", lambda v: v.reduce_max(m1[:, :], zg[:, :], axis=AX.X),
                            waits=[zev], incs=[(s_ve, 1)])
                c2 = sch.op("vector",
                            lambda v: v.tensor_scalar(zz[:, :], zg[:, :], m1[:, :], None, ALU.is_equal),
                            waits=[c1], incs=[(s_ve, 1)])
                c3 = sch.op("vector",
                            lambda v: v.scalar_tensor_tensor(zz[:, :], zz[:, :], -1e30, zg[:, :],
                                                             ALU.mult, ALU.add),
                            waits=[c2], incs=[(s_ve, 1)])
                c4 = sch.op("vector", lambda v: v.reduce_max(m2[:, :], zz[:, :], axis=AX.X),
                            waits=[c3], incs=[(s_ve, 1)])
                c5 = sch.op("vector",
                            lambda v, ti=ti, H=H: v.tensor_scalar(mask[H][:, ti * NE:(ti + 1) * NE],
                                                                  zg[:, :], m2[:, :], 0.5,
                                                                  ALU.is_ge, ALU.mult),
                            waits=[c4], incs=[(s_ve, 1)])
                mask_ev[H * 8 + ti] = c5

            for e in range(NE):
                wpi = e % 2
                eb_ev = sch.op("gpsimd",
                               lambda g, e=e, wpi=wpi: g.dma_start(
                                   ebb[wpi][:, :], eb[0:1, e * D:(e + 1) * D]),
                               waits=[ebb_free.get(wpi)], incs=[(s_w, 16)])
                for oq in range(4):
                    if H == 0:
                        flush_tp(1, grp_c)
                    bpi = (e * 4 + oq) % 2
                    src = ew[e].rearrange("(c p) o -> p c o", p=128)[:, :, oq * 512:(oq + 1) * 512]
                    d3 = We_b[bpi][:, :].rearrange("p (c n) -> p c n", c=CC)
                    we_ev = None
                    for half in range(2):
                        cs = slice(half * 8, (half + 1) * 8)
                        we_ev = sch.op("gpsimd",
                                       lambda g, d=d3[:, cs, :], s2=src[:, cs, :]: g.dma_start(d, s2),
                                       waits=[We_free.get(bpi)],
                                       incs=[(s_w, 16)])
                    for ti in range(8):
                        key, bank, boff = slots8[grp_c[0] % 8]
                        grp_c[0] += 1
                        wv_ = [we_ev, eb_ev, wfree(key)]

                        def fne(t, ti=ti, bank=bank, boff=boff, e=e, oq=oq, wpi=wpi, bpi=bpi, X2Tc=X2Tc):
                            t.matmul(bank[:, boff:boff + 512], onesbf_sb[:, :],
                                     ebb[wpi][:, oq * 512:(oq + 1) * 512], start=True, stop=False)
                            inst = None
                            for c in range(CC):
                                inst = t.matmul(bank[:, boff:boff + 512],
                                                X2Tc[:, c * 1024 + ti * 128:c * 1024 + (ti + 1) * 128],
                                                We_b[bpi][:, c * 512:(c + 1) * 512],
                                                start=False, stop=(c == CC - 1))
                            return inst

                        hm = sch.op("tensor", fne, waits=wv_, incs=[(s_pe, 1)])
                        if ti == 7:
                            We_free[bpi] = hm
                            if oq == 3:
                                ebb_free[wpi] = hm
                        tpi = grp_c[0] % 2
                        rl = sch.op("scalar",
                                    lambda sc, bank=bank, boff=boff, ti=ti, e=e, tpi=tpi, H=H: sc.activation(
                                        tmp[tpi][:, :], bank[:, boff:boff + 512], AF.Relu,
                                        scale=mask[H][:, ti * NE + e:ti * NE + e + 1]),
                                    waits=[hm, mask_ev[H * 8 + ti], tmp_free.get(tpi)],
                                    incs=[(s_sc, 1)])
                        bank_free[key] = rl
                        ac = sch.op("vector",
                                    lambda v, ti=ti, oq=oq, tpi=tpi: v.tensor_tensor(
                                        acc[:, ti * D + oq * 512:ti * D + (oq + 1) * 512],
                                        acc[:, ti * D + oq * 512:ti * D + (oq + 1) * 512],
                                        tmp[tpi][:, :], ALU.add),
                                    waits=[rl, acc_w.get((ti, oq), acc_init_ev[ti])],
                                    incs=[(s_ve, 1)])
                        acc_w[(ti, oq)] = ac
                        tmp_free[tpi] = ac
            halfC_pe_done[0] = s_pe.n

            for ti in range(8):
                tt = H * 8 + ti
                pi = 0
                aw = [acc_w[(ti, oq)] for oq in range(4)]
                l1 = sch.op("vector",
                            lambda v, ti=ti: v.reduce_sum(nmean2[:, :], acc[:, ti * D:(ti + 1) * D],
                                                          axis=AX.X, negate=True),
                            waits=aw + [ln2_chain[0]], incs=[(s_ve, 1)])
                l2 = sch.op("vector",
                            lambda v: v.tensor_scalar(nmean2[:, :], nmean2[:, :], 1.0 / D, None, ALU.mult),
                            waits=[l1], incs=[(s_ve, 1)])
                l3 = sch.op("vector",
                            lambda v, ti=ti: v.tensor_scalar(acc[:, ti * D:(ti + 1) * D],
                                                             acc[:, ti * D:(ti + 1) * D],
                                                             nmean2[:, :], None, ALU.add),
                            waits=[l2], incs=[(s_ve, 1)])
                l4 = sch.op("scalar",
                            lambda sc, ti=ti: sc.activation(sqd2[:, :], acc[:, ti * D:(ti + 1) * D],
                                                            AF.Square, accum_out=sumsq2[:, :]),
                            waits=[l3], incs=[(s_sc, 1)])
                l5 = sch.op("scalar",
                            lambda sc: sc.activation(sumsq2[:, :], sumsq2[:, :], AF.Ln,
                                                     bias=eps_sb[:, 0:1], scale=1.0 / D),
                            waits=[l4], incs=[(s_sc, 1)])
                l6 = sch.op("scalar",
                            lambda sc: sc.activation(rstd2[:, :], sumsq2[:, :], AF.Exp, scale=-0.5),
                            waits=[l5], incs=[(s_sc, 1)])
                l7 = sch.op("vector",
                            lambda v, ti=ti, pi=pi: v.tensor_scalar(
                                yb[pi][:, :], acc[:, ti * D:(ti + 1) * D],
                                rstd2[:, :], None, ALU.mult),
                            waits=[l6, yb_free.get(pi)], incs=[(s_ve, 1)])
                ln2_chain[0] = l7
                acc_w[("ln2", ti)] = l7
                yw = sch.op("scalar",
                            lambda sc, tt=tt, pi=pi: sc.dma_start(
                                y[tt * 128:(tt + 1) * 128, :], yb[pi][:, :]),
                            waits=[l7], incs=[(s_out, 16)])
                yb_free[pi] = yw
                last_y.append(yw)

        sch.op("sync", lambda s2: s2.nop(), waits=[(s_out, s_out.n)])

        sch.emit(blk)
        st_c.close()
    return nc


_NC_CACHE = {}


def _get_nc():
    if "nc" not in _NC_CACHE:
        _NC_CACHE["nc"] = build_program()
    return _NC_CACHE["nc"]


def kernel(L0_attn_hidden_states, L0_attn_Wq, L0_attn_Wk, L0_attn_Wv, L0_attn_Wo,
           L0_ffn_gate_w, L0_ffn_gate_b, L0_ffn_expert_w, L0_ffn_expert_b):
    x = np.ascontiguousarray(np.asarray(L0_attn_hidden_states, dtype=np.float32))
    Wq = np.asarray(L0_attn_Wq, dtype=np.float32)
    Wk = np.asarray(L0_attn_Wk, dtype=np.float32)
    Wv = np.asarray(L0_attn_Wv, dtype=np.float32)
    Wo = np.asarray(L0_attn_Wo, dtype=np.float32)
    gate_w = np.asarray(L0_ffn_gate_w, dtype=np.float32)
    gate_b = np.asarray(L0_ffn_gate_b, dtype=np.float32)
    expert_w = np.ascontiguousarray(np.asarray(L0_ffn_expert_w, dtype=np.float32))
    expert_b = np.asarray(L0_ffn_expert_b, dtype=np.float32)

    wqT = np.ascontiguousarray(Wq.T)
    wkT = np.ascontiguousarray(Wk.T)
    wvT = np.ascontiguousarray(Wv.T)
    woT = np.ascontiguousarray(Wo.T)
    gwt = np.ascontiguousarray(gate_w.reshape(CC, 128, NE).transpose(1, 0, 2).reshape(128, CC * NE))
    gb1 = np.ascontiguousarray(gate_b.reshape(1, NE))
    eb1 = np.ascontiguousarray(expert_b.reshape(1, NE * D))
    ones_bf = np.ones((1, 128), dtype=ml_dtypes.bfloat16)
    onescol_bf = np.ones((128, 1), dtype=ml_dtypes.bfloat16)
    onesmat_bf1 = np.ones((128, 128), dtype=ml_dtypes.bfloat16)
    ident_bf1 = np.eye(128, dtype=ml_dtypes.bfloat16)
    ones_f = np.ones((1, 128), dtype=np.float32)
    epscol = np.full((128, 1), EPS, dtype=np.float32)

    nc = _get_nc()
    in_maps = []
    for core in range(8):
        b, hf = core // 2, core % 2
        own = x[b, hf * NQ:(hf + 1) * NQ]
        oth = x[b, (1 - hf) * NQ:(2 - hf) * NQ]
        xroll = np.concatenate([own, oth], axis=0)
        in_maps.append({
            "xT": np.ascontiguousarray(xroll.T),
            "xq": np.ascontiguousarray(own),
            "wqT": wqT, "wkT": wkT, "wvT": wvT, "woT": woT,
            "gwt": gwt, "gb": gb1, "ew": expert_w, "eb": eb1,
            "ones_bf": ones_bf, "onescol_bf": onescol_bf, "onesmat_bf": onesmat_bf1,
            "ident_bf": ident_bf1,
            "ones_f": ones_f, "epscol": epscol,
        })

    res = run_bass_kernel_spmd(nc, in_maps, core_ids=list(range(8)))
    out = np.empty((4, S, D), dtype=np.float32)
    for core in range(8):
        b, hf = core // 2, core % 2
        out[b, hf * NQ:(hf + 1) * NQ] = res.results[core]["y"]
    return out


if __name__ == "__main__":
    build_program()
    print("build ok")
